# version 04
# baseline (speedup 1.0000x reference)
import numpy as np
import ml_dtypes
from contextlib import ExitStack

import concourse.bass as bass
import concourse.mybir as mybir
from concourse.bass_utils import run_bass_kernel_spmd

F32 = mybir.dt.float32
BF16 = mybir.dt.bfloat16
AF = mybir.ActivationFunctionType
ALU = mybir.AluOpType
AX = mybir.AxisListType
NPBF = ml_dtypes.bfloat16

NCORES = 8
TOK = 1024
D = 2048
NT = 8
KT = 16
NE = 32
NIC = 11
NSC = 22
SEQ = 4096
CH = 128
NCH = 32
NIT = 8
DST = 64
DEBUG_TAPS = False


class Chan:
    def __init__(self, st, nc, name):
        self.sem = st.enter_context(nc.semaphore(name))
        self.n = 0

    def inc(self, instr, k=1):
        instr.then_inc(self.sem, k)
        self.n += k
        return instr

    def wait(self, eng, n=None):
        eng.wait_ge(self.sem, self.n if n is None else n)


def build_nc():
    nc = bass.Bass("TRN2", target_bir_lowering=False)

    dt = {}
    def din(name, shape, dtype=F32):
        dt[name] = nc.dram_tensor(name, shape, dtype, kind="ExternalInput")
        return dt[name]

    din("x_slice", [TOK, D])
    din("wproj", [128, KT * 288])
    for n in ("rg_w", "ru_w", "srg_w", "sru_w"):
        din(n, [128, 2048])
    din("rd_w", [128, 1408])
    din("srd_w", [128, 2816])
    din("scg_w", [128, 2816])
    din("scu_w", [128, 2816])
    din("scd_w", [128, 2048])
    for n in ("sug_w", "suu_w", "sud_w"):
        din(n, [128, 128])
    din("ug_all", [NE, 128, 128]); din("uu_all", [NE, 128, 128]); din("ud_all", [NE, 128, 128])
    din("cg_all", [NE, 128, 1408]); din("cu_all", [NE, 128, 1408])
    din("cd_all", [NE, 128, 2048])
    din("wo_w", [DST, 2048])
    din("normw", [DST, 1])
    din("lam_r", [1, DST])
    din("wab_r", [1, DST])
    din("ident32", [128, 128])
    din("identbf", [128, 128], BF16)
    din("ltinc", [128, 128])
    din("ltstrict", [128, 128])
    din("ones_r", [1, 128])

    out_t = nc.dram_tensor("out", [TOK, D], F32, kind="ExternalOutput")

    dbg = {}
    if DEBUG_TAPS:
        for name, shape, dtp in (
            ("dbg_xn", [128, KT * TOK], BF16),
            ("dbg_we", [128, NT * 32], F32),
            ("dbg_rg", [128, TOK], BF16),
            ("dbg_ru", [128, TOK], BF16),
            ("dbg_t2", [128, 33 * TOK], BF16),
            ("dbg_proj", [SEQ, 256], F32),
            ("dbg_yn", [SEQ, DST], F32),
            ("dbg_ynown", [TOK, DST], F32),
            ("dbg_h0", [64, DST], F32),
            ("dbg_gu0", [128, TOK], BF16),
            ("dbg_g32", [128, 128], F32),
            ("dbg_top8", [128, 8], F32),
            ("dbg_sGT", [128, 384], BF16),
            ("dbg_e0w0", [128, 128], F32),
            ("dbg_U", [128, DST], BF16),
            ("dbg_prep", [128, 448], F32),
            ("dbg_t1g0", [128, TOK], BF16),
            ("dbg_t1u0", [128, TOK], BF16),
            ("dbg_rd0", [128, TOK], BF16),
        ):
            dbg[name] = nc.dram_tensor(name, shape, dtp, kind="ExternalOutput")

    cc_in = nc.dram_tensor("cc_in", [TOK, 256], F32)
    cc_out = nc.dram_tensor("cc_out", [SEQ, 256], F32)
    cc2_in = nc.dram_tensor("cc2_in", [SEQ, DST], F32)
    cc2_out = nc.dram_tensor("cc2_out", [TOK, DST], F32)
    cdbf = nc.dram_tensor("cdbf", [NE, 128, 2048], BF16)

    ST = ExitStack()
    nc.ctx.enter_context(ST)

    def sb(name, shape, dtype):
        return ST.enter_context(nc.sbuf_tensor(name, shape, dtype))

    wproj_sb = sb("wproj_sb", [128, KT * 288], BF16)
    rg_w_sb = sb("rg_w_sb", [128, 2048], BF16)
    ru_w_sb = sb("ru_w_sb", [128, 2048], BF16)
    srg_w_sb = sb("srg_w_sb", [128, 2048], BF16)
    sru_w_sb = sb("sru_w_sb", [128, 2048], BF16)
    rd_w_sb = sb("rd_w_sb", [128, 1408], BF16)
    srd_w_sb = sb("srd_w_sb", [128, 2816], BF16)
    scg_w_sb = sb("scg_w_sb", [128, 2816], BF16)
    scu_w_sb = sb("scu_w_sb", [128, 2816], BF16)
    scd_w_sb = sb("scd_w_sb", [128, 2048], BF16)
    sug_w_sb = sb("sug_w_sb", [128, 128], BF16)
    suu_w_sb = sb("suu_w_sb", [128, 128], BF16)
    sud_w_sb = sb("sud_w_sb", [128, 128], BF16)
    ew_sb = [sb(f"ew{i}", [128, 3200], BF16) for i in range(2)]
    wo_sb = sb("wo_sb", [64, 2048], BF16)
    ident32_sb = sb("ident32_sb", [128, 128], F32)
    identbf_sb = sb("identbf_sb", [128, 128], BF16)
    ltinc_sb = sb("ltinc_sb", [128, 128], F32)
    ltstrict_sb = sb("ltstrict_sb", [128, 128], F32)
    ones_r_sb = sb("ones_r_sb", [1, 128], F32)
    normw_sb = sb("normw_sb", [DST, 1], F32)
    la8b_sb = sb("la8b_sb", [128, DST], F32)
    wab_b_sb = sb("wab_b_sb", [128, DST], F32)
    srow_sb = sb("srow_sb", [1, 1024], F32)
    union_sb = sb("union_sb", [128, 33 * TOK], BF16)
    rg_sb = sb("rg_sb", [128, TOK], BF16)
    ru_sb = sb("ru_sb", [128, TOK], BF16)
    t1g_sb = sb("t1g_sb", [128, TOK], BF16)
    t1u_sb = sb("t1u_sb", [128, TOK], BF16)
    gch_sb = sb("gch_sb", [128, TOK], BF16)
    uch_sb = sb("uch_sb", [128, TOK], BF16)
    rdx_sb = sb("rdx_sb", [128, TOK], BF16)
    webc_sb = sb("webc_sb", [128, TOK], BF16)
    gu_sb = sb("gu_sb", [128, TOK], BF16)
    we_sb = sb("we_sb", [128, NT * 32], F32)
    we_fm_sb = sb("we_fm_sb", [32, TOK], F32)
    yn_tm_sb = sb("yn_tm_sb", [128, NCH * DST], F32)
    ynb_sb = sb("ynb_sb", [128, NT * DST], BF16)
    ynfm_sb = sb("ynfm_sb", [64, TOK], BF16)
    pj_sb = [sb(f"pj{i}", [128, 256], F32) for i in range(2)]
    xb = [sb(f"xb{i}", [128, D], F32) for i in range(3)]
    s_c = sb("s_c", [128, DST], F32)
    s_A = sb("s_A", [128, DST], F32)
    s_Em1 = sb("s_Em1", [128, DST], F32)
    s_Eneg = sb("s_Eneg", [128, DST], F32)
    s_tmp = sb("s_tmp", [128, DST], F32)
    s_kn = sb("s_kn", [128, DST], F32)
    s_qn = sb("s_qn", [128, DST], F32)
    s_qt = sb("s_qt", [128, DST], F32)
    s_kt = sb("s_kt", [128, DST], F32)
    s_kh = sb("s_kh", [128, DST], F32)
    s_qA = sb("s_qA", [128, DST], F32)
    s_y0 = sb("s_y0", [128, DST], F32)
    s_scr = sb("s_scr", [128, DST], F32)
    s_e0w0 = sb("s_e0w0", [128, 128], F32)
    s_ew = sb("s_ew", [128, 128], F32)
    s_qtb = sb("s_qtb", [128, DST], BF16)
    s_ktb = sb("s_ktb", [128, DST], BF16)
    s_khb = sb("s_khb", [128, DST], BF16)
    s_qAb = sb("s_qAb", [128, DST], BF16)
    s_vb = sb("s_vb", [128, DST], BF16)
    s_U = sb("s_U", [128, DST], BF16)
    s_GT = sb("s_GT", [128, 128], BF16)
    s_MT = sb("s_MT", [128, 128], BF16)
    s_GaT = sb("s_GaT", [128, 128], BF16)
    s_qtf = sb("s_qtf", [64, 128], BF16)
    s_ktf = sb("s_ktf", [64, 128], BF16)
    s_khf = sb("s_khf", [64, 128], BF16)
    s_qAf = sb("s_qAf", [64, 128], BF16)
    s_cfm = sb("s_cfm", [64, 128], F32)
    s_H0 = sb("s_H0", [64, DST], F32)
    s_H0b = sb("s_H0b", [64, DST], BF16)
    s_alast = sb("s_alast", [64, 1], F32)
    cba = sb("cba", [128, 8], F32)
    cbb = sb("cbb", [128, 8], F32)
    g32 = sb("g32", [128, 4 * 32], F32)
    top8 = sb("top8", [128, 8], F32)


    CT = Chan(ST, nc, "ct")
    CV = Chan(ST, nc, "cv")
    CS = Chan(ST, nc, "cs")
    CDW = Chan(ST, nc, "cdw")
    CDX = Chan(ST, nc, "cdx")
    CCC = Chan(ST, nc, "ccc")
    PX = [Chan(ST, nc, f"px{i}") for i in range(2)]
    PW = [Chan(ST, nc, f"pw{i}") for i in range(2)]
    PJ = [Chan(ST, nc, f"pjc{i}") for i in range(2)]
    PCD = [Chan(ST, nc, f"pcd{i}") for i in range(2)]

    with nc.Block() as blk:
        with (
            nc.psum_tensor("pA0", [128, 512], F32) as pA0,
            nc.psum_tensor("pA1", [128, 512], F32) as pA1,
            nc.psum_tensor("pB0", [128, 512], F32) as pB0,
            nc.psum_tensor("pB1", [128, 512], F32) as pB1,
            nc.psum_tensor("pC0", [128, 512], F32) as pC0,
            nc.psum_tensor("pC1", [128, 512], F32) as pC1,
            nc.psum_tensor("pT0", [128, 1024], BF16) as pT0,
            nc.psum_tensor("pT1", [128, 1024], BF16) as pT1,
        ):
            pA = [pA0, pA1]; pB = [pB0, pB1]; pC = [pC0, pC1]; pT = [pT0, pT1]

            @blk.gpsimd
            def _(g):
                for name, dst in (("ident32", ident32_sb), ("identbf", identbf_sb),
                                  ("ltinc", ltinc_sb), ("ltstrict", ltstrict_sb),
                                  ("ones_r", ones_r_sb), ("normw", normw_sb)):
                    CDW.inc(g.dma_start(out=dst[:], in_=dt[name].ap()), 16)
                CDW.inc(g.dma_start(out=srow_sb[:, 0:DST], in_=dt["lam_r"].ap()), 16)
                CDW.inc(g.dma_start(out=srow_sb[:, 64:64 + DST], in_=dt["wab_r"].ap()), 16)
                for name, dst in (("wproj", wproj_sb), ("rg_w", rg_w_sb), ("ru_w", ru_w_sb),
                                  ("srg_w", srg_w_sb), ("sru_w", sru_w_sb), ("rd_w", rd_w_sb),
                                  ("srd_w", srd_w_sb), ("scg_w", scg_w_sb), ("scu_w", scu_w_sb),
                                  ("scd_w", scd_w_sb), ("sug_w", sug_w_sb), ("suu_w", suu_w_sb),
                                  ("sud_w", sud_w_sb)):
                    CDW.inc(g.dma_start(out=dst[:], in_=dt[name].ap()), 16)
            n_const_w = CDW.n

            @blk.sync
            def _(sy):
                CDX.inc(sy.dma_start(out=xb[2][0:64, :], in_=dt["wo_w"].ap()), 16)
            n_wo_dma = CDX.n

            @blk.scalar
            def _(s):
                s.wait_ge(CDW.sem, n_const_w)
                CS.inc(s.activation(srow_sb[:, 128:128 + DST], srow_sb[:, 0:DST], AF.Sigmoid))

            @blk.vector
            def _(v):
                CS.wait(v)
                CV.inc(v.tensor_scalar_add(srow_sb[:, 128:128 + DST],
                                           srow_sb[:, 128:128 + DST], 1e-8))

            @blk.scalar
            def _(s):
                CV.wait(s)
                CS.inc(s.activation(srow_sb[:, 128:128 + DST], srow_sb[:, 128:128 + DST],
                                    AF.Ln))

            @blk.vector
            def _(v):
                CS.wait(v)
                CV.inc(v.tensor_scalar_mul(srow_sb[:, 128:128 + DST],
                                           srow_sb[:, 128:128 + DST], 8.0))

            @blk.tensor
            def _(t):
                CV.wait(t)
                t.matmul(pB0[:, 0:DST], ones_r_sb[:], srow_sb[:, 128:128 + DST],
                         start=True, stop=True)
                CT.inc(t.matmul(pB0[:, 64:64 + DST], ones_r_sb[:],
                                srow_sb[:, 64:64 + DST], start=True, stop=True))

            @blk.vector
            def _(v):
                CT.wait(v)
                v.tensor_copy(la8b_sb[:], pB0[:, 0:DST])
                CV.inc(v.tensor_copy(wab_b_sb[:], pB0[:, 64:64 + DST]))
                v.wait_ge(CDX.sem, n_wo_dma)
                CV.inc(v.tensor_scalar_mul(wo_sb[:], xb[2][0:64, :], normw_sb[:]))
            n_wo_done = CV.n

            xfree = [0, 0]
            pj_free = [0, 0]
            for tt in range(NT):
                pxc = PX[tt % 2]

                @blk.sync
                def _(sy, tt=tt, pxc=pxc):
                    if tt >= 2:
                        sy.wait_ge(CS.sem, xfree[tt % 2])
                    if tt == 0:
                        sy.wait_ge(CV.sem, n_wo_done)
                    pxc.inc(sy.dma_start(out=xb[tt % 2][:],
                                         in_=dt["x_slice"].ap()[tt * 128:(tt + 1) * 128, :]), 16)

                @blk.vector
                def _(v, tt=tt, pxc=pxc):
                    pxc.wait(v)
                    v.tensor_reduce(cbb[:, 0:1], xb[tt % 2][:], AX.X, ALU.add)
                    CV.inc(v.tensor_scalar_mul(cba[:, 0:1], cbb[:, 0:1], -1.0 / D))

                @blk.scalar
                def _(s, tt=tt):
                    CV.wait(s)
                    CS.inc(s.activation(xb[2][:], xb[tt % 2][:], AF.Square,
                                        bias=cba[:, 0:1], scale=1.0))

                @blk.vector
                def _(v, tt=tt):
                    CS.wait(v)
                    CV.inc(v.tensor_reduce(cbb[:, 1:2], xb[2][:], AX.X, ALU.add))
                    CV.wait(v)
                    CV.inc(v.tensor_scalar(cbb[:, 1:2], cbb[:, 1:2],
                                           1.0 / D, 1e-5, ALU.mult, ALU.add))

                @blk.scalar
                def _(s, tt=tt):
                    CV.wait(s)
                    CS.inc(s.activation(cbb[:, 1:2], cbb[:, 1:2],
                                        AF.Sqrt))

                @blk.vector
                def _(v, tt=tt):
                    CS.wait(v)
                    CV.inc(v.reciprocal(cba[:, 1:2], cbb[:, 1:2]))
                    CV.wait(v)
                    CV.inc(v.tensor_scalar(cba[:, 0:1], cbb[:, 0:1],
                                           cba[:, 1:2], -1.0 / D,
                                           ALU.mult, ALU.mult))

                @blk.scalar
                def _(s, tt=tt):
                    CV.wait(s)
                    CS.inc(s.activation(gch_sb[:], xb[tt % 2][:, 0:D // 2], AF.Identity,
                                        bias=cba[:, 0:1],
                                        scale=cba[:, 1:2]))
                    CS.inc(s.activation(uch_sb[:], xb[tt % 2][:, D // 2:D], AF.Identity,
                                        bias=cba[:, 0:1],
                                        scale=cba[:, 1:2]))
                xfree[tt % 2] = CS.n

                for half in range(2):
                    src_half = [gch_sb, uch_sb][half]

                    @blk.tensor
                    def _(t, tt=tt, half=half, src_half=src_half):
                        CS.wait(t)
                        if tt > 0 or half > 0:
                            CV.wait(t)
                        for b in range(8):
                            ins = t.transpose(pT[half][:, b * 128:(b + 1) * 128],
                                              src_half[:, b * 128:(b + 1) * 128],
                                              identbf_sb[:])
                            if b == 7:
                                CT.inc(ins)

                    @blk.vector
                    def _(v, tt=tt, half=half):
                        CT.wait(v)
                        for b in range(8):
                            fb = half * 8 + b
                            ins = v.tensor_copy(
                                union_sb[:, fb * TOK + tt * 128:fb * TOK + (tt + 1) * 128],
                                pT[half][:, b * 128:(b + 1) * 128])
                            if b == 7:
                                CV.inc(ins)

                @blk.tensor
                def _(t, tt=tt):
                    CV.wait(t)
                    ps = pA[tt % 2]
                    for k in range(KT):
                        ins = t.matmul(ps[:, 0:288],
                                       union_sb[:, k * TOK + tt * 128:k * TOK + (tt + 1) * 128],
                                       wproj_sb[:, k * 288:(k + 1) * 288],
                                       start=(k == 0), stop=(k == KT - 1))
                    CT.inc(ins)

                @blk.vector
                def _(v, tt=tt):
                    CT.wait(v)
                    if tt >= 2:
                        v.wait_ge(CDX.sem, pj_free[tt % 2])
                    ps = pA[tt % 2]
                    CV.inc(v.tensor_copy(pj_sb[tt % 2][:], ps[:, 0:256]))
                    CV.inc(v.tensor_copy(g32[:, 0:32], ps[:, 256:288]))

                @blk.sync
                def _(sy, tt=tt):
                    CV.wait(sy)
                    CDX.inc(sy.dma_start(out=cc_in.ap()[tt * 128:(tt + 1) * 128, :],
                                         in_=pj_sb[tt % 2][:]), 16)
                pj_free[tt % 2] = CDX.n

                @blk.vector
                def _(v, tt=tt):
                    CV.inc(v.reduce_max(cba[:, 2:3], g32[:, 0:32], AX.X,
                                        negate=True))

                @blk.scalar
                def _(s, tt=tt):
                    CV.wait(s)
                    CS.inc(s.activation(g32[:, 32:64], g32[:, 0:32], AF.Exp,
                                        bias=cba[:, 2:3]))

                @blk.vector
                def _(v, tt=tt):
                    CS.wait(v)
                    CV.inc(v.tensor_reduce(cba[:, 7:8], g32[:, 32:64], AX.X, ALU.add))
                    CV.wait(v)
                    CV.inc(v.reciprocal(cbb[:, 3:4], cba[:, 7:8]))
                    CV.wait(v)
                    CV.inc(v.tensor_scalar_mul(g32[:, 64:96], g32[:, 32:64], cbb[:, 3:4]))
                    CV.wait(v)
                    CV.inc(v.max(top8[:], g32[:, 64:96]))
                    CV.wait(v)
                    CV.inc(v.tensor_reduce(cbb[:, 6:7], top8[:, 0:4], AX.X, ALU.add))
                    CV.wait(v)
                    CV.inc(v.tensor_single_scalar(g32[:, 96:128], g32[:, 64:96],
                                                  top8[:, 3:4], ALU.is_ge))
                    CV.wait(v)
                    CV.inc(v.tensor_scalar_add(cba[:, 4:5], cbb[:, 6:7], 1e-20))
                    CV.wait(v)
                    CV.inc(v.reciprocal(cba[:, 4:5], cba[:, 4:5]))
                    CV.wait(v)
                    CV.inc(v.scalar_tensor_tensor(we_sb[:, tt * 32:(tt + 1) * 32],
                                                  g32[:, 64:96], cba[:, 4:5],
                                                  g32[:, 96:128], ALU.mult, ALU.mult))

            for batch in range(2):
                @blk.tensor
                def _(t, batch=batch):
                    CV.wait(t)
                    for j in range(4):
                        tt = batch * 4 + j
                        ins = t.transpose(pC[batch][0:32, j * 128:(j + 1) * 128],
                                          we_sb[:, tt * 32:(tt + 1) * 32], ident32_sb[:])
                    CT.inc(ins)

                @blk.vector
                def _(v, batch=batch):
                    CT.wait(v)
                    for j in range(4):
                        tt = batch * 4 + j
                        ins = v.tensor_copy(we_fm_sb[:, tt * 128:(tt + 1) * 128],
                                            pC[batch][0:32, j * 128:(j + 1) * 128])
                    CV.inc(ins)

            if DEBUG_TAPS:
                @blk.sync
                def _(sy):
                    CV.wait(sy)
                    CS.wait(sy)
                    CDX.inc(sy.dma_start(out=dbg["dbg_xn"].ap(),
                                         in_=union_sb[:, 0:KT * TOK]), 16)
                    CDX.inc(sy.dma_start(out=dbg["dbg_we"].ap(), in_=we_sb[:]), 16)
                    CDX.inc(sy.dma_start(out=dbg["dbg_g32"].ap(), in_=g32[:]), 16)
                    CDX.inc(sy.dma_start(out=dbg["dbg_top8"].ap(), in_=top8[:]), 16)

            @blk.gpsimd
            def _(g):
                for e in range(2):
                    pw = PW[e % 2]
                    pw.inc(g.dma_start(out=ew_sb[e % 2][:, 0:128],
                                       in_=dt["ug_all"].ap()[e]), 16)
                    pw.inc(g.dma_start(out=ew_sb[e % 2][:, 128:256],
                                       in_=dt["uu_all"].ap()[e]), 16)
                    pw.inc(g.dma_start(out=ew_sb[e % 2][:, 256:384],
                                       in_=dt["ud_all"].ap()[e]), 16)
                    pw.inc(g.dma_start(out=ew_sb[e % 2][:, 384:1792],
                                       in_=dt["cg_all"].ap()[e]), 16)
                    pw.inc(g.dma_start(out=ew_sb[e % 2][:, 1792:3200],
                                       in_=dt["cu_all"].ap()[e]), 16)
                g.wait_ge(CDX.sem, pj_free[1])
                CCC.inc(g.collective_compute(
                    "AllGather", ALU.bypass,
                    replica_groups=[[0, 1, 2, 3], [4, 5, 6, 7]],
                    ins=[cc_in.ap().opt()], outs=[cc_out.ap().opt()]))

            def rproj(dst_sb, w_sb):
                for h in range(2):
                    @blk.tensor
                    def _(t, h=h):
                        CV.wait(t)
                        for k in range(KT):
                            ins = t.matmul(pA[h][:],
                                           w_sb[:, k * 128:(k + 1) * 128],
                                           union_sb[:, k * TOK + h * 512:k * TOK + h * 512 + 512],
                                           start=(k == 0), stop=(k == KT - 1))
                        CT.inc(ins)

                    @blk.vector
                    def _(v, h=h):
                        CT.wait(v)
                        CV.inc(v.tensor_copy(dst_sb[:, h * 512:h * 512 + 512], pA[h][:]))

            rproj(rg_sb, rg_w_sb)
            rproj(ru_sb, ru_w_sb)
            rproj(rdx_sb, srg_w_sb)
            rproj(webc_sb, sru_w_sb)

            def stage2(dst_sb, w_sb, src_sb):
                @blk.tensor
                def _(t):
                    CV.wait(t)
                    for h in range(2):
                        ins = t.matmul(pA[h][:], w_sb[:], src_sb[:, h * 512:h * 512 + 512],
                                       start=True, stop=True)
                    CT.inc(ins)

                @blk.vector
                def _(v):
                    CT.wait(v)
                    for h in range(2):
                        ins = v.tensor_copy(dst_sb[:, h * 512:h * 512 + 512], pA[h][:])
                    CV.inc(ins)

            stage2(t1g_sb, sug_w_sb, rdx_sb)
            stage2(t1u_sb, suu_w_sb, webc_sb)

            if DEBUG_TAPS:
                @blk.sync
                def _(sy):
                    CV.wait(sy)
                    CDX.inc(sy.dma_start(out=dbg["dbg_rg"].ap(), in_=rg_sb[:]), 16)
                    CDX.inc(sy.dma_start(out=dbg["dbg_ru"].ap(), in_=ru_sb[:]), 16)

            def chunk_loop(n_chunks, cgw, cuw, gsrc, usrc, rdw, rd_psum):
                for ci in range(n_chunks):
                    @blk.tensor
                    def _(t, ci=ci):
                        CS.wait(t)
                        if ci == 0:
                            CV.wait(t)
                        for h in range(2):
                            t.matmul(pA[h][:], cgw[:, ci * 128:(ci + 1) * 128],
                                     gsrc[:, h * 512:h * 512 + 512], start=True, stop=True)
                        for h in range(2):
                            ins = t.matmul(pB[h][:], cuw[:, ci * 128:(ci + 1) * 128],
                                           usrc[:, h * 512:h * 512 + 512], start=True, stop=True)
                        CT.inc(ins)

                    @blk.scalar
                    def _(s, ci=ci):
                        CT.wait(s)
                        if ci > 0:
                            CV.wait(s)
                        for h in range(2):
                            s.activation(gch_sb[:, h * 512:h * 512 + 512], pA[h][:], AF.Silu)
                        for h in range(2):
                            ins = s.activation(uch_sb[:, h * 512:h * 512 + 512], pB[h][:],
                                               AF.Copy)
                        CS.inc(ins)

                    @blk.vector
                    def _(v, ci=ci):
                        CS.wait(v)
                        CV.inc(v.tensor_mul(gu_sb[:], gch_sb[:], uch_sb[:]))

                    @blk.tensor
                    def _(t, ci=ci, n_chunks=n_chunks):
                        CV.wait(t)
                        for h in range(2):
                            ins = t.matmul(rd_psum[h][:], rdw[:, ci * 128:(ci + 1) * 128],
                                           gu_sb[:, h * 512:h * 512 + 512],
                                           start=(ci == 0), stop=(ci == n_chunks - 1))
                        if ci == n_chunks - 1:
                            CT.inc(ins)

            chunk_loop(NSC, scg_w_sb, scu_w_sb, t1g_sb, t1u_sb, srd_w_sb, pC)

            @blk.vector
            def _(v):
                CT.wait(v)
                for h in range(2):
                    ins = v.tensor_copy(rdx_sb[:, h * 512:h * 512 + 512], pC[h][:])
                CV.inc(ins)

            @blk.tensor
            def _(t):
                CV.wait(t)
                for h in range(2):
                    ins = t.matmul(pA[h][:], sud_w_sb[:], rdx_sb[:, h * 512:h * 512 + 512],
                                   start=True, stop=True)
                CT.inc(ins)

            @blk.vector
            def _(v):
                CT.wait(v)
                for h in range(2):
                    ins = v.tensor_copy(union_sb[:, 32 * TOK + h * 512:32 * TOK + h * 512 + 512],
                                        pA[h][:])
                CV.inc(ins)

            ew_free = [0, 0]
            for e in range(NE):
                buf = ew_sb[e % 2]
                pw = PW[e % 2]
                if e >= 2:
                    @blk.gpsimd
                    def _(g, e=e, pw=pw, buf=buf):
                        g.wait_ge(CT.sem, ew_free[e % 2])
                        pw.inc(g.dma_start(out=buf[:, 0:128], in_=dt["ug_all"].ap()[e]), 16)
                        pw.inc(g.dma_start(out=buf[:, 128:256], in_=dt["uu_all"].ap()[e]), 16)
                        pw.inc(g.dma_start(out=buf[:, 256:384], in_=dt["ud_all"].ap()[e]), 16)
                        pw.inc(g.dma_start(out=buf[:, 384:1792], in_=dt["cg_all"].ap()[e]), 16)
                        pw.inc(g.dma_start(out=buf[:, 1792:3200], in_=dt["cu_all"].ap()[e]), 16)

                @blk.tensor
                def _(t, e=e, buf=buf, pw=pw):
                    pw.wait(t)
                    CS.wait(t)
                    CV.wait(t)
                    for h in range(2):
                        t.matmul(pA[h][:], buf[:, 0:128], rg_sb[:, h * 512:h * 512 + 512],
                                 start=True, stop=True)
                    for h in range(2):
                        t.matmul(pB[h][:], buf[:, 128:256], ru_sb[:, h * 512:h * 512 + 512],
                                 start=True, stop=True)
                    for h in range(2):
                        ins = t.matmul(pC[h][0:1, :], ident32_sb[0:32, e:e + 1],
                                       we_fm_sb[:, h * 512:h * 512 + 512],
                                       start=True, stop=True)
                    CT.inc(ins)

                @blk.scalar
                def _(s, e=e):
                    CT.wait(s)
                    for h in range(2):
                        s.activation(srow_sb[0:1, h * 512:h * 512 + 512], pC[h][0:1, :],
                                     AF.Copy)
                    for h in range(2):
                        ins = s.activation(uch_sb[:, h * 512:h * 512 + 512], pB[h][:], AF.Copy)
                    CS.inc(ins)

                @blk.tensor
                def _(t, e=e):
                    CS.wait(t)
                    for h in range(2):
                        ins = t.matmul(pC[h][:], ones_r_sb[:],
                                       srow_sb[0:1, h * 512:h * 512 + 512],
                                       start=True, stop=True)
                    CT.inc(ins)

                @blk.scalar
                def _(s, e=e):
                    CT.wait(s)
                    for h in range(2):
                        ins = s.activation(webc_sb[:, h * 512:h * 512 + 512], pC[h][:], AF.Copy)
                    CS.inc(ins)

                @blk.vector
                def _(v, e=e):
                    CS.wait(v)
                    for h in range(2):
                        v.tensor_copy(t1g_sb[:, h * 512:h * 512 + 512], pA[h][:])
                    CV.inc(v.tensor_mul(t1u_sb[:], uch_sb[:], webc_sb[:]))

                if DEBUG_TAPS and e == 0:
                    @blk.sync
                    def _(sy):
                        CV.wait(sy)
                        CDX.inc(sy.dma_start(out=dbg["dbg_t1g0"].ap(), in_=t1g_sb[:]), 16)
                        CDX.inc(sy.dma_start(out=dbg["dbg_t1u0"].ap(), in_=t1u_sb[:]), 16)

                chunk_loop(NIC, buf[:, 384:1792], buf[:, 1792:3200],
                           t1g_sb, t1u_sb, rd_w_sb, pC)
                ew_free[e % 2] = CT.n

                if DEBUG_TAPS and e == 0:
                    @blk.sync
                    def _(sy):
                        CV.wait(sy)
                        CDX.inc(sy.dma_start(out=dbg["dbg_gu0"].ap(), in_=gu_sb[:]), 16)

                @blk.vector
                def _(v, e=e):
                    CT.wait(v)
                    for h in range(2):
                        ins = v.tensor_copy(rdx_sb[:, h * 512:h * 512 + 512], pC[h][:])
                    CV.inc(ins)

                @blk.tensor
                def _(t, e=e, buf=buf):
                    CV.wait(t)
                    for h in range(2):
                        ins = t.matmul(pA[h][:], buf[:, 256:384],
                                       rdx_sb[:, h * 512:h * 512 + 512], start=True, stop=True)
                    CT.inc(ins)

                @blk.vector
                def _(v, e=e):
                    CT.wait(v)
                    for h in range(2):
                        ins = v.tensor_copy(
                            union_sb[:, e * TOK + h * 512:e * TOK + h * 512 + 512], pA[h][:])
                    CV.inc(ins)

                if DEBUG_TAPS and e == 0:
                    @blk.sync
                    def _(sy):
                        CV.wait(sy)
                        CDX.inc(sy.dma_start(out=dbg["dbg_rd0"].ap(), in_=rdx_sb[:]), 16)

            @blk.gpsimd
            def _(g):
                for e in range(NE):
                    CDW.inc(g.dma_start(out=cdbf.ap()[e], in_=dt["cd_all"].ap()[e]), 16)
            n_cdbf = CDW.n

            @blk.sync
            def _(sy):
                CCC.wait(sy)
                for c in range(2):
                    PJ[c % 2].inc(sy.dma_start(out=pj_sb[c % 2][:],
                                               in_=cc_out.ap()[c * 128:(c + 1) * 128, :]), 16)

            @blk.vector
            def _(v):
                v.memset(s_H0[:], 0.0)
                CV.inc(v.memset(s_H0b[:], 0.0))

            for c in range(NCH):
                pjc = PJ[c % 2]
                pjt = pj_sb[c % 2]
                pj_need = pjc.n

                @blk.scalar
                def _(s, c=c, pjc=pjc, pjt=pjt, pj_need=pj_need):
                    pjc.wait(s, pj_need)
                    CV.wait(s)
                    s.activation(s_scr[:], pjt[:, 0:64], AF.Square,
                                 accum_out=cbb[:, 4:5])
                    s.activation(cbb[:, 4:5], cbb[:, 4:5], AF.Sqrt)
                    s.activation(s_scr[:], pjt[:, 128:192], AF.Square,
                                 accum_out=cbb[:, 5:6])
                    CS.inc(s.activation(cbb[:, 5:6], cbb[:, 5:6], AF.Sqrt))

                @blk.vector
                def _(v, c=c, pjt=pjt):
                    CS.wait(v)
                    v.tensor_scalar_max(cbb[:, 4:5], cbb[:, 4:5], 1e-12)
                    v.tensor_scalar_max(cbb[:, 5:6], cbb[:, 5:6], 1e-12)
                    v.reciprocal(cbb[:, 4:5], cbb[:, 4:5])
                    CV.inc(v.reciprocal(cbb[:, 5:6], cbb[:, 5:6]))
                    CV.wait(v)
                    CV.inc(v.tensor_add(s_tmp[:], pjt[:, 192:256], wab_b_sb[:]))

                @blk.scalar
                def _(s, c=c):
                    CV.wait(s)
                    CS.inc(s.activation(s_scr[:], s_tmp[:], AF.Sigmoid))

                @blk.vector
                def _(v, c=c):
                    CS.wait(v)
                    CV.inc(v.tensor_mul(s_tmp[:], s_scr[:], la8b_sb[:]))

                @blk.tensor
                def _(t, c=c):
                    CV.wait(t)
                    CT.inc(t.matmul(pC0[:, 0:64], ltinc_sb[:], s_tmp[:],
                                    start=True, stop=True))

                @blk.vector
                def _(v, c=c):
                    CT.wait(v)
                    v.tensor_copy(s_c[:], pC0[:, 0:64])
                    CV.inc(v.tensor_sub(s_Eneg[:], s_c[:], s_tmp[:]))

                @blk.scalar
                def _(s, c=c):
                    CV.wait(s)
                    s.activation(s_Em1[:], s_Eneg[:], AF.Exp)
                    s.activation(s_A[:], s_c[:], AF.Exp)
                    s.activation(s_scr[:], s_tmp[:], AF.Exp)
                    CS.inc(s.activation(s_Eneg[:], s_c[:], AF.Exp, scale=-1.0))

                @blk.vector
                def _(v, c=c, pjt=pjt):
                    CS.wait(v)
                    v.tensor_scalar(s_scr[:], s_scr[:], -1.0, 1.0, ALU.mult, ALU.add)
                    v.tensor_mul(s_tmp[:], s_scr[:], s_Eneg[:])
                    v.tensor_scalar_mul(s_kn[:], pjt[:, 0:64], cbb[:, 4:5])
                    v.tensor_scalar_mul(s_qn[:], pjt[:, 128:192], cbb[:, 5:6])
                    v.tensor_mul(s_qt[:], s_qn[:], s_Em1[:])
                    v.tensor_mul(s_kt[:], s_kn[:], s_Em1[:])
                    v.tensor_mul(s_kh[:], s_kn[:], s_tmp[:])
                    v.tensor_mul(s_qA[:], s_qn[:], s_A[:])
                    v.tensor_copy(s_qtb[:], s_qt[:])
                    v.tensor_copy(s_ktb[:], s_kt[:])
                    v.tensor_copy(s_khb[:], s_kh[:])
                    v.tensor_copy(s_qAb[:], s_qA[:])
                    CV.inc(v.tensor_copy(s_vb[:], pjt[:, 64:128]))
                pj_consumed = CV.n


                @blk.tensor
                def _(t, c=c):
                    CV.wait(t)
                    t.transpose(pT0[0:64, 0:128], s_qtb[:], identbf_sb[:])
                    t.transpose(pT0[0:64, 128:256], s_ktb[:], identbf_sb[:])
                    t.transpose(pT0[0:64, 256:384], s_khb[:], identbf_sb[:])
                    t.transpose(pT0[0:64, 384:512], s_qAb[:], identbf_sb[:])
                    CT.inc(t.transpose(pC1[0:64, 0:128], s_c[:], ident32_sb[:]))

                @blk.vector
                def _(v, c=c):
                    CT.wait(v)
                    v.tensor_copy(s_qtf[:], pT0[0:64, 0:128])
                    v.tensor_copy(s_ktf[:], pT0[0:64, 128:256])
                    v.tensor_copy(s_khf[:], pT0[0:64, 256:384])
                    v.tensor_copy(s_qAf[:], pT0[0:64, 384:512])
                    CV.inc(v.tensor_copy(s_cfm[:], pC1[0:64, 0:128]))

                @blk.scalar
                def _(s, c=c):
                    CV.wait(s)
                    CS.inc(s.activation(s_alast[:], s_cfm[:, 127:128], AF.Exp))

                @blk.tensor
                def _(t, c=c):
                    CV.wait(t)
                    t.matmul(pA0[:, 0:128], s_khf[:], s_qtf[:], start=True, stop=True)
                    t.matmul(pA0[:, 128:256], s_khf[:], s_ktf[:], start=True, stop=True)
                    t.matmul(pA0[:, 256:384], s_khf[:], s_qAf[:], start=True, stop=True)
                    t.matmul(pB0[:, 0:64], s_qtf[:], s_H0b[:], start=True, stop=True)
                    t.matmul(pB0[:, 64:128], s_ktf[:], s_H0b[:], start=True, stop=True)
                    CT.inc(t.matmul(pB0[:, 128:192], s_qAf[:], s_H0b[:],
                                    start=True, stop=True))

                @blk.vector
                def _(v, c=c, pjt=pjt):
                    CT.wait(v)
                    v.tensor_mul(s_GT[:], pA0[:, 0:128], ltstrict_sb[:])
                    v.tensor_mul(s_MT[:], pA0[:, 128:256], ltstrict_sb[:])
                    v.tensor_mul(s_GaT[:], pA0[:, 256:384], ltinc_sb[:])
                    v.tensor_sub(s_e0w0[:, 0:64], pjt[:, 64:128], pB0[:, 0:64])
                    v.tensor_sub(s_e0w0[:, 64:128], pjt[:, 64:128], pB0[:, 64:128])
                    v.tensor_copy(s_y0[:], pB0[:, 128:192])
                    CV.inc(v.memset(s_U[:], 0.0))
                pj_done = CV.n

                if c + 2 < NCH:
                    @blk.sync
                    def _(sy, c=c, pj_done=pj_done):
                        sy.wait_ge(CV.sem, pj_done)
                        PJ[c % 2].inc(sy.dma_start(
                            out=pj_sb[c % 2][:],
                            in_=cc_out.ap()[(c + 2) * 128:(c + 3) * 128, :]), 16)

                for it in range(NIT):
                    @blk.tensor
                    def _(t, c=c, it=it):
                        CV.wait(t)
                        t.matmul(pA1[:, 0:64], s_GT[:], s_U[:], start=True, stop=True)
                        CT.inc(t.matmul(pA1[:, 64:128], s_MT[:], s_U[:],
                                        start=True, stop=True))

                    @blk.vector
                    def _(v, c=c, it=it):
                        CT.wait(v)
                        CV.inc(v.tensor_sub(s_ew[:], s_e0w0[:], pA1[:, 0:128]))

                    @blk.scalar
                    def _(s, c=c, it=it):
                        CV.wait(s)
                        s.activation(s_scr[:], s_ew[:, 0:64], AF.Square,
                                     accum_out=cbb[:, 2:3])
                        CS.inc(s.activation(cba[:, 3:4], cbb[:, 2:3],
                                            AF.Sigmoid, scale=1.0 / (1.0 + 1e-6)))

                    @blk.vector
                    def _(v, c=c, it=it):
                        CS.wait(v)
                        CV.inc(v.tensor_scalar_mul(s_U[:], s_ew[:, 64:128],
                                                   cba[:, 3:4]))

                @blk.tensor
                def _(t, c=c):
                    CV.wait(t)
                    t.matmul(pB1[:, 0:64], s_GaT[:], s_U[:], start=True, stop=True)
                    CT.inc(t.matmul(pB1[0:64, 64:128], s_khb[:], s_U[:],
                                    start=True, stop=True))

                @blk.vector
                def _(v, c=c):
                    CT.wait(v)
                    CV.inc(v.tensor_add(s_kn[:], s_y0[:], pB1[:, 0:64]))

                @blk.scalar
                def _(s, c=c):
                    CV.wait(s)
                    CS.inc(s.activation(s_scr[:], s_kn[:], AF.Square))

                @blk.vector
                def _(v, c=c):
                    CS.wait(v)
                    CV.inc(v.tensor_reduce(cbb[:, 7:8], s_scr[:], AX.X, ALU.add))
                    CV.wait(v)
                    CV.inc(v.tensor_scalar(cba[:, 5:6], cbb[:, 7:8],
                                           1.0 / DST, 1e-6, ALU.mult, ALU.add))

                @blk.scalar
                def _(s, c=c):
                    CV.wait(s)
                    CS.inc(s.activation(cba[:, 5:6], cba[:, 5:6], AF.Sqrt))

                @blk.vector
                def _(v, c=c):
                    CS.wait(v)
                    CV.inc(v.reciprocal(cba[:, 5:6], cba[:, 5:6]))
                    CV.wait(v)
                    CV.inc(v.tensor_scalar_mul(yn_tm_sb[:, c * DST:(c + 1) * DST],
                                               s_kn[:], cba[:, 5:6]))
                    v.tensor_add(s_scr[0:64, :], pB1[0:64, 64:128], s_H0[:])
                    v.tensor_scalar_mul(s_H0[:], s_scr[0:64, :], s_alast[:])
                    CV.inc(v.tensor_copy(s_H0b[:], s_H0[:]))

                @blk.sync
                def _(sy, c=c):
                    CV.wait(sy)
                    CDX.inc(sy.dma_start(out=cc2_in.ap()[c * 128:(c + 1) * 128, :],
                                         in_=yn_tm_sb[:, c * DST:(c + 1) * DST]), 16)
            n_yn_dma = CDX.n

            if DEBUG_TAPS:
                @blk.sync
                def _(sy):
                    sy.wait_ge(CDX.sem, n_yn_dma)
                    CV.wait(sy)
                    CDX.inc(sy.dma_start(out=dbg["dbg_sGT"].ap()[:, 0:128], in_=s_GT[:]), 16)
                    CDX.inc(sy.dma_start(out=dbg["dbg_sGT"].ap()[:, 128:256], in_=s_MT[:]), 16)
                    CDX.inc(sy.dma_start(out=dbg["dbg_sGT"].ap()[:, 256:384], in_=s_GaT[:]), 16)
                    CDX.inc(sy.dma_start(out=dbg["dbg_e0w0"].ap(), in_=s_e0w0[:]), 16)
                    CDX.inc(sy.dma_start(out=dbg["dbg_U"].ap(), in_=s_U[:]), 16)
                    for i, tile in enumerate((s_c, s_A, s_Em1, s_Eneg, s_qt, s_kt, s_kh)):
                        CDX.inc(sy.dma_start(
                            out=dbg["dbg_prep"].ap()[:, i * 64:(i + 1) * 64], in_=tile[:]), 16)
                    CDX.inc(sy.dma_start(out=dbg["dbg_proj"].ap(), in_=cc_out.ap()), 16)
                    CDX.inc(sy.dma_start(out=dbg["dbg_yn"].ap(), in_=cc2_in.ap()), 16)
                    CDX.inc(sy.dma_start(out=dbg["dbg_h0"].ap(), in_=s_H0[:]), 16)

            @blk.gpsimd
            def _(g):
                g.wait_ge(CDX.sem, n_yn_dma)
                CCC.inc(g.collective_compute(
                    "ReduceScatter", ALU.max,
                    replica_groups=[[0, 1, 2, 3], [4, 5, 6, 7]],
                    ins=[cc2_in.ap().opt()], outs=[cc2_out.ap().opt()]))

            @blk.sync
            def _(sy):
                CCC.wait(sy)
                for tt in range(NT):
                    CDX.inc(sy.dma_start(out=yn_tm_sb[:, tt * 64:(tt + 1) * 64],
                                         in_=cc2_out.ap()[tt * 128:(tt + 1) * 128, :]), 16)

            @blk.vector
            def _(v):
                CDX.wait(v)
                CV.inc(v.tensor_copy(ynb_sb[:], yn_tm_sb[:, 0:NT * 64]))

            @blk.tensor
            def _(t):
                CV.wait(t)
                for tt in range(NT):
                    ins = t.transpose(pT0[0:64, tt * 128:(tt + 1) * 128],
                                      ynb_sb[:, tt * 64:(tt + 1) * 64], identbf_sb[:])
                CT.inc(ins)

            @blk.vector
            def _(v):
                CT.wait(v)
                CV.inc(v.tensor_copy(ynfm_sb[:], pT0[0:64, 0:1024]))
            n_ynfm = CV.n

            if DEBUG_TAPS:
                @blk.sync
                def _(sy):
                    CV.wait(sy)
                    CDX.inc(sy.dma_start(out=dbg["dbg_ynown"].ap(), in_=cc2_out.ap()), 16)
                    CDX.inc(sy.dma_start(out=dbg["dbg_t2"].ap(), in_=union_sb[:]), 16)

        with (
            nc.psum_tensor("pyA", [128, 2048], F32) as pyA,
            nc.psum_tensor("pyB", [128, 2048], F32) as pyB,
        ):
            py = [pyA, pyB]
            cdbuf = [wproj_sb[:, 0:2048], wproj_sb[:, 2048:4096]]
            cd_free = [CT.n, CT.n]
            xq_free = [CV.n, CV.n]
            out_dma = [CDX.n, CDX.n]
            for st in range(4):
                for q in range(2):
                    tt = st * 2 + q

                    @blk.sync
                    def _(sy, st=st, q=q, tt=tt):
                        sy.wait_ge(CV.sem, xq_free[q])
                        sy.wait_ge(CDX.sem, out_dma[q])
                        PX[q].inc(sy.dma_start(out=xb[q][:],
                                               in_=dt["x_slice"].ap()[tt * 128:(tt + 1) * 128, :]),
                                  16)

                for e in range(NE + 1):
                    if e < NE:
                        @blk.gpsimd
                        def _(g, st=st, e=e):
                            g.wait_ge(CT.sem, cd_free[e % 2])
                            if st == 0 and e < 2:
                                g.wait_ge(CDW.sem, n_cdbf)
                            PCD[e % 2].inc(g.dma_start(out=cdbuf[e % 2], in_=cdbf.ap()[e]), 16)

                    @blk.tensor
                    def _(t, st=st, e=e):
                        if e < NE:
                            PCD[e % 2].wait(t)
                        if e == 0:
                            CV.wait(t)
                        mov = cdbuf[e % 2] if e < NE else scd_w_sb[:]
                        for q in range(2):
                            tt = st * 2 + q
                            stat = union_sb[:, e * TOK + tt * 128:e * TOK + (tt + 1) * 128]
                            for nch in range(4):
                                ins = t.matmul(py[q][:, nch * 512:(nch + 1) * 512],
                                               stat, mov[:, nch * 512:(nch + 1) * 512],
                                               start=(e == 0), stop=False)
                        if e < NE:
                            CT.inc(ins)
                            cd_free[e % 2] = CT.n
                        else:
                            CT.inc(ins)

                @blk.tensor
                def _(t, st=st):
                    t.wait_ge(CV.sem, n_ynfm)
                    for q in range(2):
                        tt = st * 2 + q
                        stat = ynfm_sb[:, tt * 128:(tt + 1) * 128]
                        for nch in range(4):
                            ins = t.matmul(py[q][:, nch * 512:(nch + 1) * 512],
                                           stat, wo_sb[:, nch * 512:(nch + 1) * 512],
                                           start=False, stop=True)
                    CT.inc(ins)

                for q in range(2):
                    tt = st * 2 + q

                    @blk.vector
                    def _(v, st=st, q=q, tt=tt):
                        CT.wait(v)
                        PX[q].wait(v)
                        CV.inc(v.tensor_add(xb[q][:], xb[q][:], py[q][:]))
                    xq_free[q] = CV.n

                    @blk.sync
                    def _(sy, st=st, q=q, tt=tt):
                        CV.wait(sy)
                        CDX.inc(sy.dma_start(out=out_t.ap()[tt * 128:(tt + 1) * 128, :],
                                             in_=xb[q][:]), 16)
                    out_dma[q] = CDX.n

            @blk.sync
            def _(sy):
                CDX.wait(sy)

    return nc


_NC_CACHE = {}


def kernel(**inputs):
    f32 = np.float32

    def ca(x):
        return np.ascontiguousarray(np.asarray(x, dtype=f32))

    x = ca(inputs["L0_attn_x"])
    Wk = ca(inputs["L0_attn_Wk"]); Wv = ca(inputs["L0_attn_Wv"])
    Wq = ca(inputs["L0_attn_Wq"]); Wa = ca(inputs["L0_attn_Wa_w"])
    Wa_b = ca(inputs["L0_attn_Wa_b"]); lam = ca(inputs["L0_attn_lam"])
    norm_w = ca(inputs["L0_attn_norm_w"]); Wo = ca(inputs["L0_attn_Wo"])
    gate_w = ca(inputs["L0_ffn_gate_w"])
    Rg = ca(inputs["L0_ffn_Rg"]); Ru = ca(inputs["L0_ffn_Ru"]); Rd = ca(inputs["L0_ffn_Rd"])
    Ug = ca(inputs["L0_ffn_Ug"]); Cg = ca(inputs["L0_ffn_Cg"])
    Uu = ca(inputs["L0_ffn_Uu"]); Cu = ca(inputs["L0_ffn_Cu"])
    Ud = ca(inputs["L0_ffn_Ud"]); Cd = ca(inputs["L0_ffn_Cd"])
    s_Rg = ca(inputs["L0_ffn_s_Rg"]); s_Ug = ca(inputs["L0_ffn_s_Ug"])
    s_Cg = ca(inputs["L0_ffn_s_Cg"]); s_Ru = ca(inputs["L0_ffn_s_Ru"])
    s_Uu = ca(inputs["L0_ffn_s_Uu"]); s_Cu = ca(inputs["L0_ffn_s_Cu"])
    s_Rd = ca(inputs["L0_ffn_s_Rd"]); s_Ud = ca(inputs["L0_ffn_s_Ud"])
    s_Cd = ca(inputs["L0_ffn_s_Cd"])

    def kimg(wT, kt):
        Dc, M = wT.shape
        assert Dc == kt * 128
        return np.ascontiguousarray(wT.reshape(kt, 128, M).transpose(1, 0, 2).reshape(128, kt * M))

    wproj = np.concatenate([Wk.T, Wv.T, Wq.T, Wa.T, gate_w.T], axis=1)
    shared = {
        "wproj": kimg(wproj, 16),
        "rg_w": kimg(Rg.T, 16), "ru_w": kimg(Ru.T, 16),
        "srg_w": kimg(s_Rg.T, 16), "sru_w": kimg(s_Ru.T, 16),
        "rd_w": kimg(Rd.T, 11), "srd_w": kimg(s_Rd.T, 22),
        "scg_w": np.ascontiguousarray(s_Cg.T), "scu_w": np.ascontiguousarray(s_Cu.T),
        "scd_w": np.ascontiguousarray(s_Cd.T),
        "sug_w": np.ascontiguousarray(s_Ug.T), "suu_w": np.ascontiguousarray(s_Uu.T),
        "sud_w": np.ascontiguousarray(s_Ud.T),
        "ug_all": np.ascontiguousarray(Ug.transpose(0, 2, 1)),
        "uu_all": np.ascontiguousarray(Uu.transpose(0, 2, 1)),
        "ud_all": np.ascontiguousarray(Ud.transpose(0, 2, 1)),
        "cg_all": np.ascontiguousarray(Cg.transpose(0, 2, 1)),
        "cu_all": np.ascontiguousarray(Cu.transpose(0, 2, 1)),
        "cd_all": np.ascontiguousarray(Cd.transpose(0, 2, 1)),
        "wo_w": np.ascontiguousarray(Wo.T),
        "normw": np.ascontiguousarray(norm_w.reshape(DST, 1)),
        "lam_r": np.ascontiguousarray(lam.reshape(1, DST)),
        "wab_r": np.ascontiguousarray(Wa_b.reshape(1, DST)),
        "ident32": np.eye(128, dtype=f32),
        "identbf": np.eye(128, dtype=NPBF),
        "ltinc": np.triu(np.ones((128, 128), f32)),
        "ltstrict": np.triu(np.ones((128, 128), f32), 1),
        "ones_r": np.ones((1, 128), f32),
    }

    xf = x.reshape(NCORES, TOK, D)
    in_maps = []
    for c in range(NCORES):
        m = dict(shared)
        m["x_slice"] = np.ascontiguousarray(xf[c])
        in_maps.append(m)

    if "nc" not in _NC_CACHE:
        _NC_CACHE["nc"] = build_nc()
    nc = _NC_CACHE["nc"]

    res = run_bass_kernel_spmd(nc, in_maps, core_ids=list(range(NCORES)))
    out = np.concatenate([res.results[c]["out"] for c in range(NCORES)], axis=0)
    return out.reshape(2, SEQ, D)


# version 05
# speedup vs baseline: 1.0131x; 1.0131x over previous
import numpy as np
import ml_dtypes
from contextlib import ExitStack

import concourse.bass as bass
import concourse.mybir as mybir
from concourse.bass_utils import run_bass_kernel_spmd

F32 = mybir.dt.float32
BF16 = mybir.dt.bfloat16
AF = mybir.ActivationFunctionType
ALU = mybir.AluOpType
AX = mybir.AxisListType
NPBF = ml_dtypes.bfloat16

NCORES = 8
TOK = 1024
D = 2048
NT = 8
KT = 16
NE = 32
NIC = 11
NSC = 22
SEQ = 4096
CH = 128
NCH = 32
NIT = 8
DST = 64
DEBUG_TAPS = False


class Chan:
    def __init__(self, st, nc, name):
        self.sem = st.enter_context(nc.semaphore(name))
        self.n = 0

    def inc(self, instr, k=1):
        instr.then_inc(self.sem, k)
        self.n += k
        return instr

    def wait(self, eng, n=None):
        eng.wait_ge(self.sem, self.n if n is None else n)


def build_nc():
    nc = bass.Bass("TRN2", target_bir_lowering=False)

    dt = {}
    def din(name, shape, dtype=F32):
        dt[name] = nc.dram_tensor(name, shape, dtype, kind="ExternalInput")
        return dt[name]

    din("x_slice", [TOK, D])
    din("wproj", [128, KT * 288])
    for n in ("rg_w", "ru_w", "srg_w", "sru_w"):
        din(n, [128, 2048])
    din("rd_w", [128, 1408])
    din("srd_w", [128, 2816])
    din("scg_w", [128, 2816])
    din("scu_w", [128, 2816])
    din("scd_w", [128, 2048])
    for n in ("sug_w", "suu_w", "sud_w"):
        din(n, [128, 128])
    din("ug_all", [NE, 128, 128]); din("uu_all", [NE, 128, 128]); din("ud_all", [NE, 128, 128])
    din("cg_all", [NE, 128, 1408]); din("cu_all", [NE, 128, 1408])
    din("cd_all", [NE, 128, 2048])
    din("wo_w", [DST, 2048])
    din("normw", [DST, 1])
    din("lam_r", [1, DST])
    din("wab_r", [1, DST])
    din("ident32", [128, 128])
    din("identbf", [128, 128], BF16)
    din("ltinc", [128, 128])
    din("ltstrict", [128, 128])
    din("ones_r", [1, 128])

    out_t = nc.dram_tensor("out", [TOK, D], F32, kind="ExternalOutput")

    dbg = {}
    if DEBUG_TAPS:
        for name, shape, dtp in (
            ("dbg_xn", [128, KT * TOK], BF16),
            ("dbg_we", [128, NT * 32], F32),
            ("dbg_rg", [128, TOK], BF16),
            ("dbg_ru", [128, TOK], BF16),
            ("dbg_t2", [128, 33 * TOK], BF16),
            ("dbg_proj", [SEQ, 256], F32),
            ("dbg_yn", [SEQ, DST], F32),
            ("dbg_ynown", [TOK, DST], F32),
            ("dbg_h0", [64, DST], F32),
            ("dbg_gu0", [128, TOK], BF16),
            ("dbg_g32", [128, 128], F32),
            ("dbg_top8", [128, 8], F32),
            ("dbg_sGT", [128, 384], BF16),
            ("dbg_e0w0", [128, 128], F32),
            ("dbg_U", [128, DST], BF16),
            ("dbg_prep", [128, 448], F32),
            ("dbg_t1g0", [128, TOK], BF16),
            ("dbg_t1u0", [128, TOK], BF16),
            ("dbg_rd0", [128, TOK], BF16),
        ):
            dbg[name] = nc.dram_tensor(name, shape, dtp, kind="ExternalOutput")

    cc_in = nc.dram_tensor("cc_in", [TOK, 256], F32)
    cc_out = nc.dram_tensor("cc_out", [SEQ, 256], F32)
    cc2_in = nc.dram_tensor("cc2_in", [SEQ, DST], F32)
    cc2_out = nc.dram_tensor("cc2_out", [TOK, DST], F32)
    cdbf = nc.dram_tensor("cdbf", [NE, 128, 2048], BF16)

    ST = ExitStack()
    nc.ctx.enter_context(ST)

    def sb(name, shape, dtype):
        return ST.enter_context(nc.sbuf_tensor(name, shape, dtype))

    wproj_sb = sb("wproj_sb", [128, KT * 288], BF16)
    rg_w_sb = sb("rg_w_sb", [128, 2048], BF16)
    ru_w_sb = sb("ru_w_sb", [128, 2048], BF16)
    srg_w_sb = sb("srg_w_sb", [128, 2048], BF16)
    sru_w_sb = sb("sru_w_sb", [128, 2048], BF16)
    rd_w_sb = sb("rd_w_sb", [128, 1408], BF16)
    srd_w_sb = sb("srd_w_sb", [128, 2816], BF16)
    scg_w_sb = sb("scg_w_sb", [128, 2816], BF16)
    scu_w_sb = sb("scu_w_sb", [128, 2816], BF16)
    scd_w_sb = sb("scd_w_sb", [128, 2048], BF16)
    sug_w_sb = sb("sug_w_sb", [128, 128], BF16)
    suu_w_sb = sb("suu_w_sb", [128, 128], BF16)
    sud_w_sb = sb("sud_w_sb", [128, 128], BF16)
    ew_sb = [sb(f"ew{i}", [128, 3200], BF16) for i in range(2)]
    wo_sb = sb("wo_sb", [64, 2048], BF16)
    ident32_sb = sb("ident32_sb", [128, 128], F32)
    identbf_sb = sb("identbf_sb", [128, 128], BF16)
    ltinc_sb = sb("ltinc_sb", [128, 128], F32)
    ltstrict_sb = sb("ltstrict_sb", [128, 128], F32)
    ones_r_sb = sb("ones_r_sb", [1, 128], F32)
    normw_sb = sb("normw_sb", [DST, 1], F32)
    la8b_sb = sb("la8b_sb", [128, DST], F32)
    wab_b_sb = sb("wab_b_sb", [128, DST], F32)
    srow_sb = sb("srow_sb", [1, 1024], F32)
    union_sb = sb("union_sb", [128, 33 * TOK], BF16)
    rg_sb = sb("rg_sb", [128, TOK], BF16)
    ru_sb = sb("ru_sb", [128, TOK], BF16)
    t1g_sb = sb("t1g_sb", [128, TOK], BF16)
    t1u_sb = sb("t1u_sb", [128, TOK], BF16)
    gch_sb = sb("gch_sb", [128, TOK], BF16)
    uch_sb = sb("uch_sb", [128, TOK], BF16)
    rdx_sb = sb("rdx_sb", [128, TOK], BF16)
    webc_sb = sb("webc_sb", [128, TOK], BF16)
    gu_sb = sb("gu_sb", [128, TOK], BF16)
    we_sb = sb("we_sb", [128, NT * 32], F32)
    we_fm_sb = sb("we_fm_sb", [32, TOK], F32)
    yn_tm_sb = sb("yn_tm_sb", [128, NCH * DST], F32)
    ynb_sb = sb("ynb_sb", [128, NT * DST], BF16)
    ynfm_sb = sb("ynfm_sb", [64, TOK], BF16)
    pj_sb = [sb(f"pj{i}", [128, 256], F32) for i in range(2)]
    xb = [sb(f"xb{i}", [128, D], F32) for i in range(3)]
    s_c = sb("s_c", [128, DST], F32)
    s_A = sb("s_A", [128, DST], F32)
    s_Em1 = sb("s_Em1", [128, DST], F32)
    s_Eneg = sb("s_Eneg", [128, DST], F32)
    s_tmp = sb("s_tmp", [128, DST], F32)
    s_kn = sb("s_kn", [128, DST], F32)
    s_qn = sb("s_qn", [128, DST], F32)
    s_qt = sb("s_qt", [128, DST], F32)
    s_kt = sb("s_kt", [128, DST], F32)
    s_kh = sb("s_kh", [128, DST], F32)
    s_qA = sb("s_qA", [128, DST], F32)
    s_y0 = sb("s_y0", [128, DST], F32)
    s_scr = sb("s_scr", [128, DST], F32)
    s_e0w0 = sb("s_e0w0", [128, 128], F32)
    s_ew = sb("s_ew", [128, 128], F32)
    s_qtb = sb("s_qtb", [128, DST], BF16)
    s_ktb = sb("s_ktb", [128, DST], BF16)
    s_khb = sb("s_khb", [128, DST], BF16)
    s_qAb = sb("s_qAb", [128, DST], BF16)
    s_vb = sb("s_vb", [128, DST], BF16)
    s_U = sb("s_U", [128, DST], BF16)
    s_GT = sb("s_GT", [128, 128], BF16)
    s_MT = sb("s_MT", [128, 128], BF16)
    s_GaT = sb("s_GaT", [128, 128], BF16)
    s_qtf = sb("s_qtf", [64, 128], BF16)
    s_ktf = sb("s_ktf", [64, 128], BF16)
    s_khf = sb("s_khf", [64, 128], BF16)
    s_qAf = sb("s_qAf", [64, 128], BF16)
    s_cfm = sb("s_cfm", [64, 128], F32)
    s_H0 = sb("s_H0", [64, DST], F32)
    s_H0b = sb("s_H0b", [64, DST], BF16)
    s_alast = sb("s_alast", [64, 1], F32)
    cba = sb("cba", [128, 8], F32)
    cbb = sb("cbb", [128, 8], F32)
    g32 = sb("g32", [128, 4 * 32], F32)
    top8 = sb("top8", [128, 8], F32)


    CT = Chan(ST, nc, "ct")
    CV = Chan(ST, nc, "cv")
    CS = Chan(ST, nc, "cs")
    CDW = Chan(ST, nc, "cdw")
    CDX = Chan(ST, nc, "cdx")
    CCC = Chan(ST, nc, "ccc")
    PX = [Chan(ST, nc, f"px{i}") for i in range(2)]
    PW = [Chan(ST, nc, f"pw{i}") for i in range(2)]
    PJ = [Chan(ST, nc, f"pjc{i}") for i in range(2)]
    PCD = [Chan(ST, nc, f"pcd{i}") for i in range(2)]

    with nc.Block() as blk:
        with (
            nc.psum_tensor("pA0", [128, 512], F32) as pA0,
            nc.psum_tensor("pA1", [128, 512], F32) as pA1,
            nc.psum_tensor("pB0", [128, 512], F32) as pB0,
            nc.psum_tensor("pB1", [128, 512], F32) as pB1,
            nc.psum_tensor("pC0", [128, 512], F32) as pC0,
            nc.psum_tensor("pC1", [128, 512], F32) as pC1,
            nc.psum_tensor("pT0", [128, 1024], BF16) as pT0,
            nc.psum_tensor("pT1", [128, 1024], BF16) as pT1,
        ):
            pA = [pA0, pA1]; pB = [pB0, pB1]; pC = [pC0, pC1]; pT = [pT0, pT1]

            @blk.gpsimd
            def _(g):
                for name, dst in (("ident32", ident32_sb), ("identbf", identbf_sb),
                                  ("ltinc", ltinc_sb), ("ltstrict", ltstrict_sb),
                                  ("ones_r", ones_r_sb), ("normw", normw_sb)):
                    CDW.inc(g.dma_start(out=dst[:], in_=dt[name].ap()), 16)
                CDW.inc(g.dma_start(out=srow_sb[:, 0:DST], in_=dt["lam_r"].ap()), 16)
                CDW.inc(g.dma_start(out=srow_sb[:, 64:64 + DST], in_=dt["wab_r"].ap()), 16)
                for name, dst in (("wproj", wproj_sb), ("rg_w", rg_w_sb), ("ru_w", ru_w_sb),
                                  ("srg_w", srg_w_sb), ("sru_w", sru_w_sb), ("rd_w", rd_w_sb),
                                  ("srd_w", srd_w_sb), ("scg_w", scg_w_sb), ("scu_w", scu_w_sb),
                                  ("scd_w", scd_w_sb), ("sug_w", sug_w_sb), ("suu_w", suu_w_sb),
                                  ("sud_w", sud_w_sb)):
                    CDW.inc(g.dma_start(out=dst[:], in_=dt[name].ap()), 16)
            n_const_w = CDW.n

            @blk.sync
            def _(sy):
                CDX.inc(sy.dma_start(out=xb[2][0:64, :], in_=dt["wo_w"].ap()), 16)
            n_wo_dma = CDX.n

            @blk.scalar
            def _(s):
                s.wait_ge(CDW.sem, n_const_w)
                CS.inc(s.activation(srow_sb[:, 128:128 + DST], srow_sb[:, 0:DST], AF.Sigmoid))

            @blk.vector
            def _(v):
                CS.wait(v)
                CV.inc(v.tensor_scalar_add(srow_sb[:, 128:128 + DST],
                                           srow_sb[:, 128:128 + DST], 1e-8))

            @blk.scalar
            def _(s):
                CV.wait(s)
                CS.inc(s.activation(srow_sb[:, 128:128 + DST], srow_sb[:, 128:128 + DST],
                                    AF.Ln))

            @blk.vector
            def _(v):
                CS.wait(v)
                CV.inc(v.tensor_scalar_mul(srow_sb[:, 128:128 + DST],
                                           srow_sb[:, 128:128 + DST], 8.0))

            @blk.tensor
            def _(t):
                CV.wait(t)
                t.matmul(pB0[:, 0:DST], ones_r_sb[:], srow_sb[:, 128:128 + DST],
                         start=True, stop=True)
                CT.inc(t.matmul(pB0[:, 64:64 + DST], ones_r_sb[:],
                                srow_sb[:, 64:64 + DST], start=True, stop=True))

            @blk.vector
            def _(v):
                CT.wait(v)
                v.tensor_copy(la8b_sb[:], pB0[:, 0:DST])
                CV.inc(v.tensor_copy(wab_b_sb[:], pB0[:, 64:64 + DST]))
                v.wait_ge(CDX.sem, n_wo_dma)
                CV.inc(v.tensor_scalar_mul(wo_sb[:], xb[2][0:64, :], normw_sb[:]))
            n_wo_done = CV.n

            xfree = [0, 0]
            pj_free = [0, 0]
            for tt in range(NT):
                pxc = PX[tt % 2]

                @blk.sync
                def _(sy, tt=tt, pxc=pxc):
                    if tt >= 2:
                        sy.wait_ge(CS.sem, xfree[tt % 2])
                    if tt == 0:
                        sy.wait_ge(CV.sem, n_wo_done)
                    pxc.inc(sy.dma_start(out=xb[tt % 2][:],
                                         in_=dt["x_slice"].ap()[tt * 128:(tt + 1) * 128, :]), 16)

                @blk.vector
                def _(v, tt=tt, pxc=pxc):
                    pxc.wait(v)
                    v.tensor_reduce(cbb[:, 0:1], xb[tt % 2][:], AX.X, ALU.add)
                    CV.inc(v.tensor_scalar_mul(cba[:, 0:1], cbb[:, 0:1], -1.0 / D))

                @blk.scalar
                def _(s, tt=tt):
                    CV.wait(s)
                    CS.inc(s.activation(xb[2][:], xb[tt % 2][:], AF.Square,
                                        bias=cba[:, 0:1], scale=1.0))

                @blk.vector
                def _(v, tt=tt):
                    CS.wait(v)
                    CV.inc(v.tensor_reduce(cbb[:, 1:2], xb[2][:], AX.X, ALU.add))
                    CV.wait(v)
                    CV.inc(v.tensor_scalar(cbb[:, 1:2], cbb[:, 1:2],
                                           1.0 / D, 1e-5, ALU.mult, ALU.add))

                @blk.scalar
                def _(s, tt=tt):
                    CV.wait(s)
                    CS.inc(s.activation(cbb[:, 1:2], cbb[:, 1:2],
                                        AF.Sqrt))

                @blk.vector
                def _(v, tt=tt):
                    CS.wait(v)
                    CV.inc(v.reciprocal(cba[:, 1:2], cbb[:, 1:2]))
                    CV.wait(v)
                    CV.inc(v.tensor_scalar(cba[:, 0:1], cbb[:, 0:1],
                                           cba[:, 1:2], -1.0 / D,
                                           ALU.mult, ALU.mult))

                @blk.scalar
                def _(s, tt=tt):
                    CV.wait(s)
                    CS.inc(s.activation(gch_sb[:], xb[tt % 2][:, 0:D // 2], AF.Identity,
                                        bias=cba[:, 0:1],
                                        scale=cba[:, 1:2]))
                    CS.inc(s.activation(uch_sb[:], xb[tt % 2][:, D // 2:D], AF.Identity,
                                        bias=cba[:, 0:1],
                                        scale=cba[:, 1:2]))
                xfree[tt % 2] = CS.n

                for half in range(2):
                    src_half = [gch_sb, uch_sb][half]

                    @blk.tensor
                    def _(t, tt=tt, half=half, src_half=src_half):
                        CS.wait(t)
                        if tt > 0 or half > 0:
                            CV.wait(t)
                        for b in range(8):
                            ins = t.transpose(pT[half][:, b * 128:(b + 1) * 128],
                                              src_half[:, b * 128:(b + 1) * 128],
                                              identbf_sb[:])
                            if b == 7:
                                CT.inc(ins)

                    @blk.vector
                    def _(v, tt=tt, half=half):
                        CT.wait(v)
                        for b in range(8):
                            fb = half * 8 + b
                            ins = v.tensor_copy(
                                union_sb[:, fb * TOK + tt * 128:fb * TOK + (tt + 1) * 128],
                                pT[half][:, b * 128:(b + 1) * 128])
                            if b == 7:
                                CV.inc(ins)

                @blk.tensor
                def _(t, tt=tt):
                    CV.wait(t)
                    ps = pA[tt % 2]
                    for k in range(KT):
                        ins = t.matmul(ps[:, 0:288],
                                       union_sb[:, k * TOK + tt * 128:k * TOK + (tt + 1) * 128],
                                       wproj_sb[:, k * 288:(k + 1) * 288],
                                       start=(k == 0), stop=(k == KT - 1))
                    CT.inc(ins)

                @blk.vector
                def _(v, tt=tt):
                    CT.wait(v)
                    if tt >= 2:
                        v.wait_ge(CDX.sem, pj_free[tt % 2])
                    ps = pA[tt % 2]
                    CV.inc(v.tensor_copy(pj_sb[tt % 2][:], ps[:, 0:256]))
                    CV.inc(v.tensor_copy(g32[:, 0:32], ps[:, 256:288]))

                @blk.sync
                def _(sy, tt=tt):
                    CV.wait(sy)
                    CDX.inc(sy.dma_start(out=cc_in.ap()[tt * 128:(tt + 1) * 128, :],
                                         in_=pj_sb[tt % 2][:]), 16)
                pj_free[tt % 2] = CDX.n

                @blk.vector
                def _(v, tt=tt):
                    CV.inc(v.reduce_max(cba[:, 2:3], g32[:, 0:32], AX.X,
                                        negate=True))

                @blk.scalar
                def _(s, tt=tt):
                    CV.wait(s)
                    CS.inc(s.activation(g32[:, 32:64], g32[:, 0:32], AF.Exp,
                                        bias=cba[:, 2:3]))

                @blk.vector
                def _(v, tt=tt):
                    CS.wait(v)
                    CV.inc(v.tensor_reduce(cba[:, 7:8], g32[:, 32:64], AX.X, ALU.add))
                    CV.wait(v)
                    CV.inc(v.reciprocal(cbb[:, 3:4], cba[:, 7:8]))
                    CV.wait(v)
                    CV.inc(v.tensor_scalar_mul(g32[:, 64:96], g32[:, 32:64], cbb[:, 3:4]))
                    CV.wait(v)
                    CV.inc(v.max(top8[:], g32[:, 64:96]))
                    CV.wait(v)
                    CV.inc(v.tensor_reduce(cbb[:, 6:7], top8[:, 0:4], AX.X, ALU.add))
                    CV.wait(v)
                    CV.inc(v.tensor_single_scalar(g32[:, 96:128], g32[:, 64:96],
                                                  top8[:, 3:4], ALU.is_ge))
                    CV.wait(v)
                    CV.inc(v.tensor_scalar_add(cba[:, 4:5], cbb[:, 6:7], 1e-20))
                    CV.wait(v)
                    CV.inc(v.reciprocal(cba[:, 4:5], cba[:, 4:5]))
                    CV.wait(v)
                    CV.inc(v.scalar_tensor_tensor(we_sb[:, tt * 32:(tt + 1) * 32],
                                                  g32[:, 64:96], cba[:, 4:5],
                                                  g32[:, 96:128], ALU.mult, ALU.mult))

            for batch in range(2):
                @blk.tensor
                def _(t, batch=batch):
                    CV.wait(t)
                    for j in range(4):
                        tt = batch * 4 + j
                        ins = t.transpose(pC[batch][0:32, j * 128:(j + 1) * 128],
                                          we_sb[:, tt * 32:(tt + 1) * 32], ident32_sb[:])
                    CT.inc(ins)

                @blk.vector
                def _(v, batch=batch):
                    CT.wait(v)
                    for j in range(4):
                        tt = batch * 4 + j
                        ins = v.tensor_copy(we_fm_sb[:, tt * 128:(tt + 1) * 128],
                                            pC[batch][0:32, j * 128:(j + 1) * 128])
                    CV.inc(ins)

            if DEBUG_TAPS:
                @blk.sync
                def _(sy):
                    CV.wait(sy)
                    CS.wait(sy)
                    CDX.inc(sy.dma_start(out=dbg["dbg_xn"].ap(),
                                         in_=union_sb[:, 0:KT * TOK]), 16)
                    CDX.inc(sy.dma_start(out=dbg["dbg_we"].ap(), in_=we_sb[:]), 16)
                    CDX.inc(sy.dma_start(out=dbg["dbg_g32"].ap(), in_=g32[:]), 16)
                    CDX.inc(sy.dma_start(out=dbg["dbg_top8"].ap(), in_=top8[:]), 16)

            @blk.gpsimd
            def _(g):
                for e in range(2):
                    pw = PW[e % 2]
                    pw.inc(g.dma_start(out=ew_sb[e % 2][:, 0:128],
                                       in_=dt["ug_all"].ap()[e]), 16)
                    pw.inc(g.dma_start(out=ew_sb[e % 2][:, 128:256],
                                       in_=dt["uu_all"].ap()[e]), 16)
                    pw.inc(g.dma_start(out=ew_sb[e % 2][:, 256:384],
                                       in_=dt["ud_all"].ap()[e]), 16)
                    pw.inc(g.dma_start(out=ew_sb[e % 2][:, 384:1792],
                                       in_=dt["cg_all"].ap()[e]), 16)
                    pw.inc(g.dma_start(out=ew_sb[e % 2][:, 1792:3200],
                                       in_=dt["cu_all"].ap()[e]), 16)
                g.wait_ge(CDX.sem, pj_free[1])
                CCC.inc(g.collective_compute(
                    "AllGather", ALU.bypass,
                    replica_groups=[[0, 1, 2, 3], [4, 5, 6, 7]],
                    ins=[cc_in.ap().opt()], outs=[cc_out.ap().opt()]))

            def rproj(dst_sb, w_sb):
                for h in range(2):
                    @blk.tensor
                    def _(t, h=h):
                        CV.wait(t)
                        for k in range(KT):
                            ins = t.matmul(pA[h][:],
                                           w_sb[:, k * 128:(k + 1) * 128],
                                           union_sb[:, k * TOK + h * 512:k * TOK + h * 512 + 512],
                                           start=(k == 0), stop=(k == KT - 1))
                        CT.inc(ins)

                    @blk.vector
                    def _(v, h=h):
                        CT.wait(v)
                        CV.inc(v.tensor_copy(dst_sb[:, h * 512:h * 512 + 512], pA[h][:]))

            rproj(rg_sb, rg_w_sb)
            rproj(ru_sb, ru_w_sb)
            rproj(rdx_sb, srg_w_sb)
            rproj(webc_sb, sru_w_sb)

            def stage2(dst_sb, w_sb, src_sb):
                @blk.tensor
                def _(t):
                    CV.wait(t)
                    for h in range(2):
                        ins = t.matmul(pA[h][:], w_sb[:], src_sb[:, h * 512:h * 512 + 512],
                                       start=True, stop=True)
                    CT.inc(ins)

                @blk.vector
                def _(v):
                    CT.wait(v)
                    for h in range(2):
                        ins = v.tensor_copy(dst_sb[:, h * 512:h * 512 + 512], pA[h][:])
                    CV.inc(ins)

            stage2(t1g_sb, sug_w_sb, rdx_sb)
            stage2(t1u_sb, suu_w_sb, webc_sb)

            if DEBUG_TAPS:
                @blk.sync
                def _(sy):
                    CV.wait(sy)
                    CDX.inc(sy.dma_start(out=dbg["dbg_rg"].ap(), in_=rg_sb[:]), 16)
                    CDX.inc(sy.dma_start(out=dbg["dbg_ru"].ap(), in_=ru_sb[:]), 16)

            def chunk_loop(n_chunks, cgw, cuw, gsrc, usrc, rdw, rd_psum):
                for ci in range(n_chunks):
                    @blk.tensor
                    def _(t, ci=ci):
                        CS.wait(t)
                        if ci == 0:
                            CV.wait(t)
                        for h in range(2):
                            t.matmul(pA[h][:], cgw[:, ci * 128:(ci + 1) * 128],
                                     gsrc[:, h * 512:h * 512 + 512], start=True, stop=True)
                        for h in range(2):
                            ins = t.matmul(pB[h][:], cuw[:, ci * 128:(ci + 1) * 128],
                                           usrc[:, h * 512:h * 512 + 512], start=True, stop=True)
                        CT.inc(ins)

                    @blk.scalar
                    def _(s, ci=ci):
                        CT.wait(s)
                        if ci > 0:
                            CV.wait(s)
                        for h in range(2):
                            s.activation(gch_sb[:, h * 512:h * 512 + 512], pA[h][:], AF.Silu)
                        for h in range(2):
                            ins = s.activation(uch_sb[:, h * 512:h * 512 + 512], pB[h][:],
                                               AF.Copy)
                        CS.inc(ins)

                    @blk.vector
                    def _(v, ci=ci):
                        CS.wait(v)
                        CV.inc(v.tensor_mul(gu_sb[:], gch_sb[:], uch_sb[:]))

                    @blk.tensor
                    def _(t, ci=ci, n_chunks=n_chunks):
                        CV.wait(t)
                        for h in range(2):
                            ins = t.matmul(rd_psum[h][:], rdw[:, ci * 128:(ci + 1) * 128],
                                           gu_sb[:, h * 512:h * 512 + 512],
                                           start=(ci == 0), stop=(ci == n_chunks - 1))
                        if ci == n_chunks - 1:
                            CT.inc(ins)

            chunk_loop(NSC, scg_w_sb, scu_w_sb, t1g_sb, t1u_sb, srd_w_sb, pC)

            @blk.vector
            def _(v):
                CT.wait(v)
                for h in range(2):
                    ins = v.tensor_copy(rdx_sb[:, h * 512:h * 512 + 512], pC[h][:])
                CV.inc(ins)

            @blk.tensor
            def _(t):
                CV.wait(t)
                for h in range(2):
                    ins = t.matmul(pA[h][:], sud_w_sb[:], rdx_sb[:, h * 512:h * 512 + 512],
                                   start=True, stop=True)
                CT.inc(ins)

            @blk.vector
            def _(v):
                CT.wait(v)
                for h in range(2):
                    ins = v.tensor_copy(union_sb[:, 32 * TOK + h * 512:32 * TOK + h * 512 + 512],
                                        pA[h][:])
                CV.inc(ins)

            ew_free = [0, 0]
            for e in range(NE):
                buf = ew_sb[e % 2]
                pw = PW[e % 2]
                if e >= 2:
                    @blk.gpsimd
                    def _(g, e=e, pw=pw, buf=buf):
                        g.wait_ge(CT.sem, ew_free[e % 2])
                        pw.inc(g.dma_start(out=buf[:, 0:128], in_=dt["ug_all"].ap()[e]), 16)
                        pw.inc(g.dma_start(out=buf[:, 128:256], in_=dt["uu_all"].ap()[e]), 16)
                        pw.inc(g.dma_start(out=buf[:, 256:384], in_=dt["ud_all"].ap()[e]), 16)
                        pw.inc(g.dma_start(out=buf[:, 384:1792], in_=dt["cg_all"].ap()[e]), 16)
                        pw.inc(g.dma_start(out=buf[:, 1792:3200], in_=dt["cu_all"].ap()[e]), 16)

                @blk.tensor
                def _(t, e=e, buf=buf, pw=pw):
                    pw.wait(t)
                    CS.wait(t)
                    CV.wait(t)
                    for h in range(2):
                        t.matmul(pA[h][:], buf[:, 0:128], rg_sb[:, h * 512:h * 512 + 512],
                                 start=True, stop=True)
                    for h in range(2):
                        t.matmul(pB[h][:], buf[:, 128:256], ru_sb[:, h * 512:h * 512 + 512],
                                 start=True, stop=True)
                    for h in range(2):
                        ins = t.matmul(pC[h][0:1, :], ident32_sb[0:32, e:e + 1],
                                       we_fm_sb[:, h * 512:h * 512 + 512],
                                       start=True, stop=True)
                    CT.inc(ins)

                @blk.scalar
                def _(s, e=e):
                    CT.wait(s)
                    for h in range(2):
                        s.activation(srow_sb[0:1, h * 512:h * 512 + 512], pC[h][0:1, :],
                                     AF.Copy)
                    for h in range(2):
                        ins = s.activation(uch_sb[:, h * 512:h * 512 + 512], pB[h][:], AF.Copy)
                    CS.inc(ins)

                @blk.tensor
                def _(t, e=e):
                    CS.wait(t)
                    for h in range(2):
                        ins = t.matmul(pC[h][:], ones_r_sb[:],
                                       srow_sb[0:1, h * 512:h * 512 + 512],
                                       start=True, stop=True)
                    CT.inc(ins)

                @blk.scalar
                def _(s, e=e):
                    CT.wait(s)
                    for h in range(2):
                        ins = s.activation(webc_sb[:, h * 512:h * 512 + 512], pC[h][:], AF.Copy)
                    CS.inc(ins)

                @blk.vector
                def _(v, e=e):
                    CS.wait(v)
                    for h in range(2):
                        v.tensor_copy(t1g_sb[:, h * 512:h * 512 + 512], pA[h][:])
                    CV.inc(v.tensor_mul(t1u_sb[:], uch_sb[:], webc_sb[:]))

                if DEBUG_TAPS and e == 0:
                    @blk.sync
                    def _(sy):
                        CV.wait(sy)
                        CDX.inc(sy.dma_start(out=dbg["dbg_t1g0"].ap(), in_=t1g_sb[:]), 16)
                        CDX.inc(sy.dma_start(out=dbg["dbg_t1u0"].ap(), in_=t1u_sb[:]), 16)

                chunk_loop(NIC, buf[:, 384:1792], buf[:, 1792:3200],
                           t1g_sb, t1u_sb, rd_w_sb, pC)
                ew_free[e % 2] = CT.n

                if DEBUG_TAPS and e == 0:
                    @blk.sync
                    def _(sy):
                        CV.wait(sy)
                        CDX.inc(sy.dma_start(out=dbg["dbg_gu0"].ap(), in_=gu_sb[:]), 16)

                @blk.vector
                def _(v, e=e):
                    CT.wait(v)
                    for h in range(2):
                        ins = v.tensor_copy(rdx_sb[:, h * 512:h * 512 + 512], pC[h][:])
                    CV.inc(ins)

                @blk.tensor
                def _(t, e=e, buf=buf):
                    CV.wait(t)
                    for h in range(2):
                        ins = t.matmul(pA[h][:], buf[:, 256:384],
                                       rdx_sb[:, h * 512:h * 512 + 512], start=True, stop=True)
                    CT.inc(ins)

                @blk.vector
                def _(v, e=e):
                    CT.wait(v)
                    for h in range(2):
                        ins = v.tensor_copy(
                            union_sb[:, e * TOK + h * 512:e * TOK + h * 512 + 512], pA[h][:])
                    CV.inc(ins)

                if DEBUG_TAPS and e == 0:
                    @blk.sync
                    def _(sy):
                        CV.wait(sy)
                        CDX.inc(sy.dma_start(out=dbg["dbg_rd0"].ap(), in_=rdx_sb[:]), 16)

            @blk.gpsimd
            def _(g):
                for e in range(NE):
                    CDW.inc(g.dma_start(out=cdbf.ap()[e], in_=dt["cd_all"].ap()[e]), 16)
            n_cdbf = CDW.n

            @blk.sync
            def _(sy):
                CCC.wait(sy)
                for c in range(2):
                    PJ[c % 2].inc(sy.dma_start(out=pj_sb[c % 2][:],
                                               in_=cc_out.ap()[c * 128:(c + 1) * 128, :]), 16)

            @blk.vector
            def _(v):
                v.memset(s_H0[:], 0.0)
                CV.inc(v.memset(s_H0b[:], 0.0))

            for c in range(NCH):
                pjc = PJ[c % 2]
                pjt = pj_sb[c % 2]
                pj_need = pjc.n

                @blk.vector
                def _(v, c=c, pjc=pjc, pjt=pjt, pj_need=pj_need):
                    pjc.wait(v, pj_need)
                    CV.wait(v)
                    v.tensor_mul(s_scr[:], pjt[:, 0:64], pjt[:, 0:64])
                    v.tensor_reduce(cbb[:, 4:5], s_scr[:], AX.X, ALU.add)
                    v.tensor_mul(s_scr[:], pjt[:, 128:192], pjt[:, 128:192])
                    CV.inc(v.tensor_reduce(cbb[:, 5:6], s_scr[:], AX.X, ALU.add))

                @blk.scalar
                def _(s, c=c):
                    CV.wait(s)
                    s.activation(cbb[:, 4:5], cbb[:, 4:5], AF.Sqrt)
                    CS.inc(s.activation(cbb[:, 5:6], cbb[:, 5:6], AF.Sqrt))

                @blk.vector
                def _(v, c=c, pjt=pjt):
                    CS.wait(v)
                    v.tensor_scalar_max(cbb[:, 4:5], cbb[:, 4:5], 1e-12)
                    v.tensor_scalar_max(cbb[:, 5:6], cbb[:, 5:6], 1e-12)
                    v.reciprocal(cbb[:, 4:5], cbb[:, 4:5])
                    CV.inc(v.reciprocal(cbb[:, 5:6], cbb[:, 5:6]))
                    CV.wait(v)
                    CV.inc(v.tensor_add(s_tmp[:], pjt[:, 192:256], wab_b_sb[:]))

                @blk.scalar
                def _(s, c=c):
                    CV.wait(s)
                    CS.inc(s.activation(s_scr[:], s_tmp[:], AF.Sigmoid))

                @blk.vector
                def _(v, c=c):
                    CS.wait(v)
                    CV.inc(v.tensor_mul(s_tmp[:], s_scr[:], la8b_sb[:]))

                @blk.tensor
                def _(t, c=c):
                    CV.wait(t)
                    CT.inc(t.matmul(pC0[:, 0:64], ltinc_sb[:], s_tmp[:],
                                    start=True, stop=True))

                @blk.vector
                def _(v, c=c):
                    CT.wait(v)
                    v.tensor_copy(s_c[:], pC0[:, 0:64])
                    CV.inc(v.tensor_sub(s_Eneg[:], s_c[:], s_tmp[:]))

                @blk.scalar
                def _(s, c=c):
                    CV.wait(s)
                    s.activation(s_Em1[:], s_Eneg[:], AF.Exp)
                    s.activation(s_A[:], s_c[:], AF.Exp)
                    s.activation(s_scr[:], s_tmp[:], AF.Exp)
                    CS.inc(s.activation(s_Eneg[:], s_c[:], AF.Exp, scale=-1.0))

                @blk.vector
                def _(v, c=c, pjt=pjt):
                    CS.wait(v)
                    v.tensor_scalar(s_scr[:], s_scr[:], -1.0, 1.0, ALU.mult, ALU.add)
                    v.tensor_mul(s_tmp[:], s_scr[:], s_Eneg[:])
                    v.tensor_scalar_mul(s_kn[:], pjt[:, 0:64], cbb[:, 4:5])
                    v.tensor_scalar_mul(s_qn[:], pjt[:, 128:192], cbb[:, 5:6])
                    v.tensor_mul(s_qt[:], s_qn[:], s_Em1[:])
                    v.tensor_mul(s_kt[:], s_kn[:], s_Em1[:])
                    v.tensor_mul(s_kh[:], s_kn[:], s_tmp[:])
                    v.tensor_mul(s_qA[:], s_qn[:], s_A[:])
                    v.tensor_copy(s_qtb[:], s_qt[:])
                    v.tensor_copy(s_ktb[:], s_kt[:])
                    v.tensor_copy(s_khb[:], s_kh[:])
                    v.tensor_copy(s_qAb[:], s_qA[:])
                    CV.inc(v.tensor_copy(s_vb[:], pjt[:, 64:128]))
                pj_consumed = CV.n


                @blk.tensor
                def _(t, c=c):
                    CV.wait(t)
                    t.transpose(pT0[0:64, 0:128], s_qtb[:], identbf_sb[:])
                    t.transpose(pT0[0:64, 128:256], s_ktb[:], identbf_sb[:])
                    t.transpose(pT0[0:64, 256:384], s_khb[:], identbf_sb[:])
                    t.transpose(pT0[0:64, 384:512], s_qAb[:], identbf_sb[:])
                    CT.inc(t.transpose(pC1[0:64, 0:128], s_c[:], ident32_sb[:]))

                @blk.vector
                def _(v, c=c):
                    CT.wait(v)
                    v.tensor_copy(s_qtf[:], pT0[0:64, 0:128])
                    v.tensor_copy(s_ktf[:], pT0[0:64, 128:256])
                    v.tensor_copy(s_khf[:], pT0[0:64, 256:384])
                    v.tensor_copy(s_qAf[:], pT0[0:64, 384:512])
                    CV.inc(v.tensor_copy(s_cfm[:], pC1[0:64, 0:128]))

                @blk.scalar
                def _(s, c=c):
                    CV.wait(s)
                    CS.inc(s.activation(s_alast[:], s_cfm[:, 127:128], AF.Exp))

                @blk.tensor
                def _(t, c=c):
                    CV.wait(t)
                    t.matmul(pA0[:, 0:128], s_khf[:], s_qtf[:], start=True, stop=True)
                    t.matmul(pA0[:, 128:256], s_khf[:], s_ktf[:], start=True, stop=True)
                    t.matmul(pA0[:, 256:384], s_khf[:], s_qAf[:], start=True, stop=True)
                    t.matmul(pB0[:, 0:64], s_qtf[:], s_H0b[:], start=True, stop=True)
                    t.matmul(pB0[:, 64:128], s_ktf[:], s_H0b[:], start=True, stop=True)
                    CT.inc(t.matmul(pB0[:, 128:192], s_qAf[:], s_H0b[:],
                                    start=True, stop=True))

                @blk.vector
                def _(v, c=c, pjt=pjt):
                    CT.wait(v)
                    v.tensor_mul(s_GT[:], pA0[:, 0:128], ltstrict_sb[:])
                    v.tensor_mul(s_MT[:], pA0[:, 128:256], ltstrict_sb[:])
                    v.tensor_mul(s_GaT[:], pA0[:, 256:384], ltinc_sb[:])
                    v.tensor_sub(s_e0w0[:, 0:64], pjt[:, 64:128], pB0[:, 0:64])
                    v.tensor_sub(s_e0w0[:, 64:128], pjt[:, 64:128], pB0[:, 64:128])
                    v.tensor_copy(s_y0[:], pB0[:, 128:192])
                    CV.inc(v.memset(s_U[:], 0.0))
                pj_done = CV.n

                if c + 2 < NCH:
                    @blk.sync
                    def _(sy, c=c, pj_done=pj_done):
                        sy.wait_ge(CV.sem, pj_done)
                        PJ[c % 2].inc(sy.dma_start(
                            out=pj_sb[c % 2][:],
                            in_=cc_out.ap()[(c + 2) * 128:(c + 3) * 128, :]), 16)

                for it in range(NIT):
                    @blk.tensor
                    def _(t, c=c, it=it):
                        CV.wait(t)
                        t.matmul(pA1[:, 0:64], s_GT[:], s_U[:], start=True, stop=True)
                        CT.inc(t.matmul(pA1[:, 64:128], s_MT[:], s_U[:],
                                        start=True, stop=True))

                    @blk.vector
                    def _(v, c=c, it=it):
                        CT.wait(v)
                        v.tensor_sub(s_ew[:], s_e0w0[:], pA1[:, 0:128])
                        v.tensor_mul(s_scr[:], s_ew[:, 0:64], s_ew[:, 0:64])
                        CV.inc(v.tensor_reduce(cbb[:, 2:3], s_scr[:], AX.X, ALU.add))

                    @blk.scalar
                    def _(s, c=c, it=it):
                        CV.wait(s)
                        CS.inc(s.activation(cba[:, 3:4], cbb[:, 2:3],
                                            AF.Sigmoid, scale=1.0 / (1.0 + 1e-6)))

                    @blk.vector
                    def _(v, c=c, it=it):
                        CS.wait(v)
                        CV.inc(v.tensor_scalar_mul(s_U[:], s_ew[:, 64:128],
                                                   cba[:, 3:4]))

                @blk.tensor
                def _(t, c=c):
                    CV.wait(t)
                    t.matmul(pB1[:, 0:64], s_GaT[:], s_U[:], start=True, stop=True)
                    CT.inc(t.matmul(pB1[0:64, 64:128], s_khb[:], s_U[:],
                                    start=True, stop=True))

                @blk.vector
                def _(v, c=c):
                    CT.wait(v)
                    CV.inc(v.tensor_add(s_kn[:], s_y0[:], pB1[:, 0:64]))

                @blk.vector
                def _(v, c=c):
                    CV.wait(v)
                    v.tensor_mul(s_scr[:], s_kn[:], s_kn[:])
                    CV.inc(v.tensor_reduce(cbb[:, 7:8], s_scr[:], AX.X, ALU.add))
                    CV.wait(v)
                    CV.inc(v.tensor_scalar(cba[:, 5:6], cbb[:, 7:8],
                                           1.0 / DST, 1e-6, ALU.mult, ALU.add))

                @blk.scalar
                def _(s, c=c):
                    CV.wait(s)
                    CS.inc(s.activation(cba[:, 5:6], cba[:, 5:6], AF.Sqrt))

                @blk.vector
                def _(v, c=c):
                    CS.wait(v)
                    CV.inc(v.reciprocal(cba[:, 5:6], cba[:, 5:6]))
                    CV.wait(v)
                    CV.inc(v.tensor_scalar_mul(yn_tm_sb[:, c * DST:(c + 1) * DST],
                                               s_kn[:], cba[:, 5:6]))
                    v.tensor_add(s_scr[0:64, :], pB1[0:64, 64:128], s_H0[:])
                    v.tensor_scalar_mul(s_H0[:], s_scr[0:64, :], s_alast[:])
                    CV.inc(v.tensor_copy(s_H0b[:], s_H0[:]))

                @blk.sync
                def _(sy, c=c):
                    CV.wait(sy)
                    CDX.inc(sy.dma_start(out=cc2_in.ap()[c * 128:(c + 1) * 128, :],
                                         in_=yn_tm_sb[:, c * DST:(c + 1) * DST]), 16)
            n_yn_dma = CDX.n

            if DEBUG_TAPS:
                @blk.sync
                def _(sy):
                    sy.wait_ge(CDX.sem, n_yn_dma)
                    CV.wait(sy)
                    CDX.inc(sy.dma_start(out=dbg["dbg_sGT"].ap()[:, 0:128], in_=s_GT[:]), 16)
                    CDX.inc(sy.dma_start(out=dbg["dbg_sGT"].ap()[:, 128:256], in_=s_MT[:]), 16)
                    CDX.inc(sy.dma_start(out=dbg["dbg_sGT"].ap()[:, 256:384], in_=s_GaT[:]), 16)
                    CDX.inc(sy.dma_start(out=dbg["dbg_e0w0"].ap(), in_=s_e0w0[:]), 16)
                    CDX.inc(sy.dma_start(out=dbg["dbg_U"].ap(), in_=s_U[:]), 16)
                    for i, tile in enumerate((s_c, s_A, s_Em1, s_Eneg, s_qt, s_kt, s_kh)):
                        CDX.inc(sy.dma_start(
                            out=dbg["dbg_prep"].ap()[:, i * 64:(i + 1) * 64], in_=tile[:]), 16)
                    CDX.inc(sy.dma_start(out=dbg["dbg_proj"].ap(), in_=cc_out.ap()), 16)
                    CDX.inc(sy.dma_start(out=dbg["dbg_yn"].ap(), in_=cc2_in.ap()), 16)
                    CDX.inc(sy.dma_start(out=dbg["dbg_h0"].ap(), in_=s_H0[:]), 16)

            @blk.gpsimd
            def _(g):
                g.wait_ge(CDX.sem, n_yn_dma)
                CCC.inc(g.collective_compute(
                    "ReduceScatter", ALU.max,
                    replica_groups=[[0, 1, 2, 3], [4, 5, 6, 7]],
                    ins=[cc2_in.ap().opt()], outs=[cc2_out.ap().opt()]))

            @blk.sync
            def _(sy):
                CCC.wait(sy)
                for tt in range(NT):
                    CDX.inc(sy.dma_start(out=yn_tm_sb[:, tt * 64:(tt + 1) * 64],
                                         in_=cc2_out.ap()[tt * 128:(tt + 1) * 128, :]), 16)

            @blk.vector
            def _(v):
                CDX.wait(v)
                CV.inc(v.tensor_copy(ynb_sb[:], yn_tm_sb[:, 0:NT * 64]))

            @blk.tensor
            def _(t):
                CV.wait(t)
                for tt in range(NT):
                    ins = t.transpose(pT0[0:64, tt * 128:(tt + 1) * 128],
                                      ynb_sb[:, tt * 64:(tt + 1) * 64], identbf_sb[:])
                CT.inc(ins)

            @blk.vector
            def _(v):
                CT.wait(v)
                CV.inc(v.tensor_copy(ynfm_sb[:], pT0[0:64, 0:1024]))
            n_ynfm = CV.n

            if DEBUG_TAPS:
                @blk.sync
                def _(sy):
                    CV.wait(sy)
                    CDX.inc(sy.dma_start(out=dbg["dbg_ynown"].ap(), in_=cc2_out.ap()), 16)
                    CDX.inc(sy.dma_start(out=dbg["dbg_t2"].ap(), in_=union_sb[:]), 16)

        with (
            nc.psum_tensor("pyA", [128, 2048], F32) as pyA,
            nc.psum_tensor("pyB", [128, 2048], F32) as pyB,
        ):
            py = [pyA, pyB]
            cdbuf = [wproj_sb[:, 0:2048], wproj_sb[:, 2048:4096]]
            cd_free = [CT.n, CT.n]
            xq_free = [CV.n, CV.n]
            out_dma = [CDX.n, CDX.n]
            for st in range(4):
                for q in range(2):
                    tt = st * 2 + q

                    @blk.sync
                    def _(sy, st=st, q=q, tt=tt):
                        sy.wait_ge(CV.sem, xq_free[q])
                        sy.wait_ge(CDX.sem, out_dma[q])
                        PX[q].inc(sy.dma_start(out=xb[q][:],
                                               in_=dt["x_slice"].ap()[tt * 128:(tt + 1) * 128, :]),
                                  16)

                for e in range(NE + 1):
                    if e < NE:
                        @blk.gpsimd
                        def _(g, st=st, e=e):
                            g.wait_ge(CT.sem, cd_free[e % 2])
                            if st == 0 and e < 2:
                                g.wait_ge(CDW.sem, n_cdbf)
                            PCD[e % 2].inc(g.dma_start(out=cdbuf[e % 2], in_=cdbf.ap()[e]), 16)

                    @blk.tensor
                    def _(t, st=st, e=e):
                        if e < NE:
                            PCD[e % 2].wait(t)
                        if e == 0:
                            CV.wait(t)
                        mov = cdbuf[e % 2] if e < NE else scd_w_sb[:]
                        for q in range(2):
                            tt = st * 2 + q
                            stat = union_sb[:, e * TOK + tt * 128:e * TOK + (tt + 1) * 128]
                            for nch in range(4):
                                ins = t.matmul(py[q][:, nch * 512:(nch + 1) * 512],
                                               stat, mov[:, nch * 512:(nch + 1) * 512],
                                               start=(e == 0), stop=False)
                        if e < NE:
                            CT.inc(ins)
                            cd_free[e % 2] = CT.n
                        else:
                            CT.inc(ins)

                @blk.tensor
                def _(t, st=st):
                    t.wait_ge(CV.sem, n_ynfm)
                    for q in range(2):
                        tt = st * 2 + q
                        stat = ynfm_sb[:, tt * 128:(tt + 1) * 128]
                        for nch in range(4):
                            ins = t.matmul(py[q][:, nch * 512:(nch + 1) * 512],
                                           stat, wo_sb[:, nch * 512:(nch + 1) * 512],
                                           start=False, stop=True)
                    CT.inc(ins)

                for q in range(2):
                    tt = st * 2 + q

                    @blk.vector
                    def _(v, st=st, q=q, tt=tt):
                        CT.wait(v)
                        PX[q].wait(v)
                        CV.inc(v.tensor_add(xb[q][:], xb[q][:], py[q][:]))
                    xq_free[q] = CV.n

                    @blk.sync
                    def _(sy, st=st, q=q, tt=tt):
                        CV.wait(sy)
                        CDX.inc(sy.dma_start(out=out_t.ap()[tt * 128:(tt + 1) * 128, :],
                                             in_=xb[q][:]), 16)
                    out_dma[q] = CDX.n

            @blk.sync
            def _(sy):
                CDX.wait(sy)

    return nc


_NC_CACHE = {}


def kernel(**inputs):
    f32 = np.float32

    def ca(x):
        return np.ascontiguousarray(np.asarray(x, dtype=f32))

    x = ca(inputs["L0_attn_x"])
    Wk = ca(inputs["L0_attn_Wk"]); Wv = ca(inputs["L0_attn_Wv"])
    Wq = ca(inputs["L0_attn_Wq"]); Wa = ca(inputs["L0_attn_Wa_w"])
    Wa_b = ca(inputs["L0_attn_Wa_b"]); lam = ca(inputs["L0_attn_lam"])
    norm_w = ca(inputs["L0_attn_norm_w"]); Wo = ca(inputs["L0_attn_Wo"])
    gate_w = ca(inputs["L0_ffn_gate_w"])
    Rg = ca(inputs["L0_ffn_Rg"]); Ru = ca(inputs["L0_ffn_Ru"]); Rd = ca(inputs["L0_ffn_Rd"])
    Ug = ca(inputs["L0_ffn_Ug"]); Cg = ca(inputs["L0_ffn_Cg"])
    Uu = ca(inputs["L0_ffn_Uu"]); Cu = ca(inputs["L0_ffn_Cu"])
    Ud = ca(inputs["L0_ffn_Ud"]); Cd = ca(inputs["L0_ffn_Cd"])
    s_Rg = ca(inputs["L0_ffn_s_Rg"]); s_Ug = ca(inputs["L0_ffn_s_Ug"])
    s_Cg = ca(inputs["L0_ffn_s_Cg"]); s_Ru = ca(inputs["L0_ffn_s_Ru"])
    s_Uu = ca(inputs["L0_ffn_s_Uu"]); s_Cu = ca(inputs["L0_ffn_s_Cu"])
    s_Rd = ca(inputs["L0_ffn_s_Rd"]); s_Ud = ca(inputs["L0_ffn_s_Ud"])
    s_Cd = ca(inputs["L0_ffn_s_Cd"])

    def kimg(wT, kt):
        Dc, M = wT.shape
        assert Dc == kt * 128
        return np.ascontiguousarray(wT.reshape(kt, 128, M).transpose(1, 0, 2).reshape(128, kt * M))

    wproj = np.concatenate([Wk.T, Wv.T, Wq.T, Wa.T, gate_w.T], axis=1)
    shared = {
        "wproj": kimg(wproj, 16),
        "rg_w": kimg(Rg.T, 16), "ru_w": kimg(Ru.T, 16),
        "srg_w": kimg(s_Rg.T, 16), "sru_w": kimg(s_Ru.T, 16),
        "rd_w": kimg(Rd.T, 11), "srd_w": kimg(s_Rd.T, 22),
        "scg_w": np.ascontiguousarray(s_Cg.T), "scu_w": np.ascontiguousarray(s_Cu.T),
        "scd_w": np.ascontiguousarray(s_Cd.T),
        "sug_w": np.ascontiguousarray(s_Ug.T), "suu_w": np.ascontiguousarray(s_Uu.T),
        "sud_w": np.ascontiguousarray(s_Ud.T),
        "ug_all": np.ascontiguousarray(Ug.transpose(0, 2, 1)),
        "uu_all": np.ascontiguousarray(Uu.transpose(0, 2, 1)),
        "ud_all": np.ascontiguousarray(Ud.transpose(0, 2, 1)),
        "cg_all": np.ascontiguousarray(Cg.transpose(0, 2, 1)),
        "cu_all": np.ascontiguousarray(Cu.transpose(0, 2, 1)),
        "cd_all": np.ascontiguousarray(Cd.transpose(0, 2, 1)),
        "wo_w": np.ascontiguousarray(Wo.T),
        "normw": np.ascontiguousarray(norm_w.reshape(DST, 1)),
        "lam_r": np.ascontiguousarray(lam.reshape(1, DST)),
        "wab_r": np.ascontiguousarray(Wa_b.reshape(1, DST)),
        "ident32": np.eye(128, dtype=f32),
        "identbf": np.eye(128, dtype=NPBF),
        "ltinc": np.triu(np.ones((128, 128), f32)),
        "ltstrict": np.triu(np.ones((128, 128), f32), 1),
        "ones_r": np.ones((1, 128), f32),
    }

    xf = x.reshape(NCORES, TOK, D)
    in_maps = []
    for c in range(NCORES):
        m = dict(shared)
        m["x_slice"] = np.ascontiguousarray(xf[c])
        in_maps.append(m)

    if "nc" not in _NC_CACHE:
        _NC_CACHE["nc"] = build_nc()
    nc = _NC_CACHE["nc"]

    res = run_bass_kernel_spmd(nc, in_maps, core_ids=list(range(NCORES)))
    out = np.concatenate([res.results[c]["out"] for c in range(NCORES)], axis=0)
    return out.reshape(2, SEQ, D)


# version 06
# speedup vs baseline: 1.0176x; 1.0045x over previous
import numpy as np
import ml_dtypes
from contextlib import ExitStack

import concourse.bass as bass
import concourse.mybir as mybir
from concourse.bass_utils import run_bass_kernel_spmd

F32 = mybir.dt.float32
BF16 = mybir.dt.bfloat16
AF = mybir.ActivationFunctionType
ALU = mybir.AluOpType
AX = mybir.AxisListType
NPBF = ml_dtypes.bfloat16

NCORES = 8
TOK = 1024
D = 2048
NT = 8
KT = 16
NE = 32
NIC = 11
NSC = 22
SEQ = 4096
CH = 128
NCH = 32
NIT = 8
DST = 64
DEBUG_TAPS = False


class Chan:
    def __init__(self, st, nc, name):
        self.sem = st.enter_context(nc.semaphore(name))
        self.n = 0

    def inc(self, instr, k=1):
        instr.then_inc(self.sem, k)
        self.n += k
        return instr

    def wait(self, eng, n=None):
        eng.wait_ge(self.sem, self.n if n is None else n)


def build_nc():
    nc = bass.Bass("TRN2", target_bir_lowering=False)

    dt = {}
    def din(name, shape, dtype=F32):
        dt[name] = nc.dram_tensor(name, shape, dtype, kind="ExternalInput")
        return dt[name]

    din("x_slice", [TOK, D])
    din("wproj", [128, KT * 288])
    for n in ("rg_w", "ru_w", "srg_w", "sru_w"):
        din(n, [128, 2048])
    din("rd_w", [128, 1408])
    din("srd_w", [128, 2816])
    din("scg_w", [128, 2816])
    din("scu_w", [128, 2816])
    din("scd_w", [128, 2048])
    for n in ("sug_w", "suu_w", "sud_w"):
        din(n, [128, 128])
    din("ug_all", [NE, 128, 128]); din("uu_all", [NE, 128, 128]); din("ud_all", [NE, 128, 128])
    din("cg_all", [NE, 128, 1408]); din("cu_all", [NE, 128, 1408])
    din("cd_all", [NE, 128, 2048])
    din("wo_w", [DST, 2048])
    din("normw", [DST, 1])
    din("lam_r", [1, DST])
    din("wab_r", [1, DST])
    din("ident32", [128, 128])
    din("identbf", [128, 128], BF16)
    din("ltinc", [128, 128])
    din("ltstrict", [128, 128])
    din("ones_r", [1, 128])

    out_t = nc.dram_tensor("out", [TOK, D], F32, kind="ExternalOutput")

    dbg = {}
    if DEBUG_TAPS:
        for name, shape, dtp in (
            ("dbg_xn", [128, KT * TOK], BF16),
            ("dbg_we", [128, NT * 32], F32),
            ("dbg_rg", [128, TOK], BF16),
            ("dbg_ru", [128, TOK], BF16),
            ("dbg_t2", [128, 33 * TOK], BF16),
            ("dbg_proj", [SEQ, 256], F32),
            ("dbg_yn", [SEQ, DST], F32),
            ("dbg_ynown", [TOK, DST], F32),
            ("dbg_h0", [64, DST], F32),
            ("dbg_gu0", [128, TOK], BF16),
            ("dbg_g32", [128, 128], F32),
            ("dbg_top8", [128, 8], F32),
            ("dbg_sGT", [128, 384], BF16),
            ("dbg_e0w0", [128, 128], F32),
            ("dbg_U", [128, DST], BF16),
            ("dbg_prep", [128, 448], F32),
            ("dbg_t1g0", [128, TOK], BF16),
            ("dbg_t1u0", [128, TOK], BF16),
            ("dbg_rd0", [128, TOK], BF16),
        ):
            dbg[name] = nc.dram_tensor(name, shape, dtp, kind="ExternalOutput")

    cc_in = nc.dram_tensor("cc_in", [TOK, 256], F32)
    cc_out = nc.dram_tensor("cc_out", [SEQ, 256], F32)
    cc2_in = nc.dram_tensor("cc2_in", [SEQ, DST], F32)
    cc2_out = nc.dram_tensor("cc2_out", [TOK, DST], F32)
    cdbf = nc.dram_tensor("cdbf", [NE, 128, 2048], BF16)

    ST = ExitStack()
    nc.ctx.enter_context(ST)

    def sb(name, shape, dtype):
        return ST.enter_context(nc.sbuf_tensor(name, shape, dtype))

    wproj_sb = sb("wproj_sb", [128, KT * 288], BF16)
    rg_w_sb = sb("rg_w_sb", [128, 2048], BF16)
    ru_w_sb = sb("ru_w_sb", [128, 2048], BF16)
    srg_w_sb = sb("srg_w_sb", [128, 2048], BF16)
    sru_w_sb = sb("sru_w_sb", [128, 2048], BF16)
    rd_w_sb = sb("rd_w_sb", [128, 1408], BF16)
    srd_w_sb = sb("srd_w_sb", [128, 2816], BF16)
    scg_w_sb = sb("scg_w_sb", [128, 2816], BF16)
    scu_w_sb = sb("scu_w_sb", [128, 2816], BF16)
    scd_w_sb = sb("scd_w_sb", [128, 2048], BF16)
    sug_w_sb = sb("sug_w_sb", [128, 128], BF16)
    suu_w_sb = sb("suu_w_sb", [128, 128], BF16)
    sud_w_sb = sb("sud_w_sb", [128, 128], BF16)
    ew_sb = [sb(f"ew{i}", [128, 3200], BF16) for i in range(2)]
    wo_sb = sb("wo_sb", [64, 2048], BF16)
    ident32_sb = sb("ident32_sb", [128, 128], F32)
    identbf_sb = sb("identbf_sb", [128, 128], BF16)
    ltinc_sb = sb("ltinc_sb", [128, 128], F32)
    ltstrict_sb = sb("ltstrict_sb", [128, 128], F32)
    ones_r_sb = sb("ones_r_sb", [1, 128], F32)
    normw_sb = sb("normw_sb", [DST, 1], F32)
    la8b_sb = sb("la8b_sb", [128, DST], F32)
    wab_b_sb = sb("wab_b_sb", [128, DST], F32)
    srow_sb = sb("srow_sb", [1, 1024], F32)
    union_sb = sb("union_sb", [128, 33 * TOK], BF16)
    rg_sb = sb("rg_sb", [128, TOK], BF16)
    ru_sb = sb("ru_sb", [128, TOK], BF16)
    t1g_sb = sb("t1g_sb", [128, TOK], BF16)
    t1u_sb = sb("t1u_sb", [128, TOK], BF16)
    gch_sb = sb("gch_sb", [128, TOK], BF16)
    uch_sb = sb("uch_sb", [128, TOK], BF16)
    rdx_sb = sb("rdx_sb", [128, TOK], BF16)
    webc_sb = sb("webc_sb", [128, TOK], BF16)
    gu_sb = sb("gu_sb", [128, TOK], BF16)
    we_sb = sb("we_sb", [128, NT * 32], F32)
    we_fm_sb = sb("we_fm_sb", [32, TOK], F32)
    yn_tm_sb = sb("yn_tm_sb", [128, NCH * DST], F32)
    ynb_sb = sb("ynb_sb", [128, NT * DST], BF16)
    ynfm_sb = sb("ynfm_sb", [64, TOK], BF16)
    pj_sb = [sb(f"pj{i}", [128, 256], F32) for i in range(2)]
    xb = [sb(f"xb{i}", [128, D], F32) for i in range(3)]
    s_c = sb("s_c", [128, DST], F32)
    s_A = sb("s_A", [128, DST], F32)
    s_Em1 = sb("s_Em1", [128, DST], F32)
    s_Eneg = sb("s_Eneg", [128, DST], F32)
    s_tmp = sb("s_tmp", [128, DST], F32)
    s_kn = sb("s_kn", [128, DST], F32)
    s_qn = sb("s_qn", [128, DST], F32)
    s_qt = sb("s_qt", [128, DST], F32)
    s_kt = sb("s_kt", [128, DST], F32)
    s_kh = sb("s_kh", [128, DST], F32)
    s_qA = sb("s_qA", [128, DST], F32)
    s_y0 = sb("s_y0", [128, DST], F32)
    s_scr = sb("s_scr", [128, DST], F32)
    s_e0w0 = sb("s_e0w0", [128, 128], F32)
    s_ew = sb("s_ew", [128, 128], F32)
    s_qtb = sb("s_qtb", [128, DST], BF16)
    s_ktb = sb("s_ktb", [128, DST], BF16)
    s_khb = sb("s_khb", [128, DST], BF16)
    s_qAb = sb("s_qAb", [128, DST], BF16)
    s_vb = sb("s_vb", [128, DST], BF16)
    s_U = sb("s_U", [128, DST], BF16)
    s_GT = sb("s_GT", [128, 128], BF16)
    s_MT = sb("s_MT", [128, 128], BF16)
    s_GaT = sb("s_GaT", [128, 128], BF16)
    s_qtf = sb("s_qtf", [64, 128], BF16)
    s_ktf = sb("s_ktf", [64, 128], BF16)
    s_khf = sb("s_khf", [64, 128], BF16)
    s_qAf = sb("s_qAf", [64, 128], BF16)
    s_cfm = sb("s_cfm", [64, 128], F32)
    s_H0 = sb("s_H0", [64, DST], F32)
    s_H0b = sb("s_H0b", [64, DST], BF16)
    s_alast = sb("s_alast", [64, 1], F32)
    cba = sb("cba", [128, 8], F32)
    cbb = sb("cbb", [128, 8], F32)
    g32 = sb("g32", [128, 4 * 32], F32)
    top8 = sb("top8", [128, 8], F32)


    CT = Chan(ST, nc, "ct")
    CV = Chan(ST, nc, "cv")
    CS = Chan(ST, nc, "cs")
    CDW = Chan(ST, nc, "cdw")
    CDX = Chan(ST, nc, "cdx")
    CCC = Chan(ST, nc, "ccc")
    PX = [Chan(ST, nc, f"px{i}") for i in range(2)]
    PW = [Chan(ST, nc, f"pw{i}") for i in range(2)]
    PJ = [Chan(ST, nc, f"pjc{i}") for i in range(2)]
    PCD = [Chan(ST, nc, f"pcd{i}") for i in range(2)]

    with nc.Block() as blk:
        with (
            nc.psum_tensor("pA0", [128, 512], F32) as pA0,
            nc.psum_tensor("pA1", [128, 512], F32) as pA1,
            nc.psum_tensor("pB0", [128, 512], F32) as pB0,
            nc.psum_tensor("pB1", [128, 512], F32) as pB1,
            nc.psum_tensor("pC0", [128, 512], F32) as pC0,
            nc.psum_tensor("pC1", [128, 512], F32) as pC1,
            nc.psum_tensor("pT0", [128, 1024], BF16) as pT0,
            nc.psum_tensor("pT1", [128, 1024], BF16) as pT1,
        ):
            pA = [pA0, pA1]; pB = [pB0, pB1]; pC = [pC0, pC1]; pT = [pT0, pT1]

            @blk.gpsimd
            def _(g):
                for name, dst in (("ident32", ident32_sb), ("identbf", identbf_sb),
                                  ("ltinc", ltinc_sb), ("ltstrict", ltstrict_sb),
                                  ("ones_r", ones_r_sb), ("normw", normw_sb)):
                    CDW.inc(g.dma_start(out=dst[:], in_=dt[name].ap()), 16)
                CDW.inc(g.dma_start(out=srow_sb[:, 0:DST], in_=dt["lam_r"].ap()), 16)
                CDW.inc(g.dma_start(out=srow_sb[:, 64:64 + DST], in_=dt["wab_r"].ap()), 16)
                for name, dst in (("wproj", wproj_sb), ("rg_w", rg_w_sb), ("ru_w", ru_w_sb),
                                  ("srg_w", srg_w_sb), ("sru_w", sru_w_sb), ("rd_w", rd_w_sb),
                                  ("srd_w", srd_w_sb), ("scg_w", scg_w_sb), ("scu_w", scu_w_sb),
                                  ("scd_w", scd_w_sb), ("sug_w", sug_w_sb), ("suu_w", suu_w_sb),
                                  ("sud_w", sud_w_sb)):
                    CDW.inc(g.dma_start(out=dst[:], in_=dt[name].ap()), 16)
            n_const_w = CDW.n

            @blk.sync
            def _(sy):
                CDX.inc(sy.dma_start(out=xb[2][0:64, :], in_=dt["wo_w"].ap()), 16)
            n_wo_dma = CDX.n

            @blk.scalar
            def _(s):
                s.wait_ge(CDW.sem, n_const_w)
                CS.inc(s.activation(srow_sb[:, 128:128 + DST], srow_sb[:, 0:DST], AF.Sigmoid))

            @blk.vector
            def _(v):
                v.memset(cba[:], 0.0)
                v.memset(cbb[:], 0.0)
                CS.wait(v)
                CV.inc(v.tensor_scalar_add(srow_sb[:, 128:128 + DST],
                                           srow_sb[:, 128:128 + DST], 1e-8))

            @blk.scalar
            def _(s):
                CV.wait(s)
                CS.inc(s.activation(srow_sb[:, 128:128 + DST], srow_sb[:, 128:128 + DST],
                                    AF.Ln))

            @blk.vector
            def _(v):
                CS.wait(v)
                CV.inc(v.tensor_scalar_mul(srow_sb[:, 128:128 + DST],
                                           srow_sb[:, 128:128 + DST], 8.0))

            @blk.tensor
            def _(t):
                CV.wait(t)
                t.matmul(pB0[:, 0:DST], ones_r_sb[:], srow_sb[:, 128:128 + DST],
                         start=True, stop=True)
                CT.inc(t.matmul(pB0[:, 64:64 + DST], ones_r_sb[:],
                                srow_sb[:, 64:64 + DST], start=True, stop=True))

            @blk.vector
            def _(v):
                CT.wait(v)
                v.tensor_copy(la8b_sb[:], pB0[:, 0:DST])
                CV.inc(v.tensor_copy(wab_b_sb[:], pB0[:, 64:64 + DST]))
                v.wait_ge(CDX.sem, n_wo_dma)
                CV.inc(v.tensor_scalar_mul(wo_sb[:], xb[2][0:64, :], normw_sb[:]))
            n_wo_done = CV.n

            xfree = [0, 0]
            pj_free = [0, 0]
            for tt in range(NT):
                pxc = PX[tt % 2]

                @blk.sync
                def _(sy, tt=tt, pxc=pxc):
                    if tt >= 2:
                        sy.wait_ge(CV.sem, xfree[tt % 2])
                    if tt == 0:
                        sy.wait_ge(CV.sem, n_wo_done)
                    pxc.inc(sy.dma_start(out=xb[tt % 2][:],
                                         in_=dt["x_slice"].ap()[tt * 128:(tt + 1) * 128, :]), 16)

                @blk.vector
                def _(v, tt=tt, pxc=pxc):
                    pxc.wait(v)
                    CV.inc(v.tensor_reduce(cbb[:, 0:1], xb[tt % 2][:], AX.X, ALU.add))
                    CV.wait(v)
                    CV.inc(v.tensor_scalar_mul(cba[:, 0:1], cbb[:, 0:1], -1.0 / D))
                    CV.wait(v)
                    v.tensor_scalar(xb[2][:], xb[tt % 2][:], cba[:, 0:1], None,
                                    ALU.add)
                    v.tensor_mul(xb[2][:], xb[2][:], xb[2][:])
                    CV.inc(v.tensor_reduce(cbb[:, 1:2], xb[2][:], AX.X, ALU.add))
                    CV.wait(v)
                    CV.inc(v.tensor_scalar(cbb[:, 1:2], cbb[:, 1:2],
                                           1.0 / D, 1e-5, ALU.mult, ALU.add))

                @blk.scalar
                def _(s, tt=tt):
                    CV.wait(s)
                    CS.inc(s.activation(cbb[:, 1:2], cbb[:, 1:2],
                                        AF.Sqrt))

                @blk.vector
                def _(v, tt=tt):
                    CS.wait(v)
                    CV.inc(v.reciprocal(cba[:, 1:2], cbb[:, 1:2]))
                    CV.wait(v)
                    CV.inc(v.tensor_scalar(cba[:, 0:1], cbb[:, 0:1],
                                           cba[:, 1:2], -1.0 / D,
                                           ALU.mult, ALU.mult))
                    CV.wait(v)
                    v.tensor_scalar(gch_sb[:], xb[tt % 2][:, 0:D // 2],
                                    cba[:, 1:2], cba[:, 0:1], ALU.mult, ALU.add)
                    CV.inc(v.tensor_scalar(uch_sb[:], xb[tt % 2][:, D // 2:D],
                                           cba[:, 1:2], cba[:, 0:1],
                                           ALU.mult, ALU.add))
                xfree[tt % 2] = CV.n

                for half in range(2):
                    src_half = [gch_sb, uch_sb][half]

                    @blk.tensor
                    def _(t, tt=tt, half=half, src_half=src_half):
                        CV.wait(t)
                        for b in range(8):
                            ins = t.transpose(pT[half][:, b * 128:(b + 1) * 128],
                                              src_half[:, b * 128:(b + 1) * 128],
                                              identbf_sb[:])
                            if b == 7:
                                CT.inc(ins)

                    @blk.vector
                    def _(v, tt=tt, half=half):
                        CT.wait(v)
                        for b in range(8):
                            fb = half * 8 + b
                            ins = v.tensor_copy(
                                union_sb[:, fb * TOK + tt * 128:fb * TOK + (tt + 1) * 128],
                                pT[half][:, b * 128:(b + 1) * 128])
                            if b == 7:
                                CV.inc(ins)

                @blk.tensor
                def _(t, tt=tt):
                    CV.wait(t)
                    ps = pA[tt % 2]
                    for k in range(KT):
                        ins = t.matmul(ps[:, 0:288],
                                       union_sb[:, k * TOK + tt * 128:k * TOK + (tt + 1) * 128],
                                       wproj_sb[:, k * 288:(k + 1) * 288],
                                       start=(k == 0), stop=(k == KT - 1))
                    CT.inc(ins)

                @blk.vector
                def _(v, tt=tt):
                    CT.wait(v)
                    if tt >= 2:
                        v.wait_ge(CDX.sem, pj_free[tt % 2])
                    ps = pA[tt % 2]
                    CV.inc(v.tensor_copy(pj_sb[tt % 2][:], ps[:, 0:256]))
                    CV.inc(v.tensor_copy(g32[:, 0:32], ps[:, 256:288]))

                @blk.sync
                def _(sy, tt=tt):
                    CV.wait(sy)
                    CDX.inc(sy.dma_start(out=cc_in.ap()[tt * 128:(tt + 1) * 128, :],
                                         in_=pj_sb[tt % 2][:]), 16)
                pj_free[tt % 2] = CDX.n

                @blk.vector
                def _(v, tt=tt):
                    CV.inc(v.reduce_max(cba[:, 2:3], g32[:, 0:32], AX.X,
                                        negate=True))
                    CV.wait(v)
                    CV.inc(v.tensor_scalar(g32[:, 32:64], g32[:, 0:32],
                                           cba[:, 2:3], None, ALU.add))

                @blk.scalar
                def _(s, tt=tt):
                    CV.wait(s)
                    CS.inc(s.activation(g32[:, 32:64], g32[:, 32:64], AF.Exp))

                @blk.vector
                def _(v, tt=tt):
                    CS.wait(v)
                    CV.inc(v.tensor_reduce(cba[:, 7:8], g32[:, 32:64], AX.X, ALU.add))
                    CV.wait(v)
                    CV.inc(v.reciprocal(cbb[:, 3:4], cba[:, 7:8]))
                    CV.wait(v)
                    CV.inc(v.tensor_scalar_mul(g32[:, 64:96], g32[:, 32:64], cbb[:, 3:4]))
                    CV.wait(v)
                    CV.inc(v.max(top8[:], g32[:, 64:96]))
                    CV.wait(v)
                    CV.inc(v.tensor_reduce(cbb[:, 6:7], top8[:, 0:4], AX.X, ALU.add))
                    CV.wait(v)
                    CV.inc(v.tensor_single_scalar(g32[:, 96:128], g32[:, 64:96],
                                                  top8[:, 3:4], ALU.is_ge))
                    CV.wait(v)
                    CV.inc(v.tensor_scalar_add(cba[:, 4:5], cbb[:, 6:7], 1e-20))
                    CV.wait(v)
                    CV.inc(v.reciprocal(cba[:, 4:5], cba[:, 4:5]))
                    CV.wait(v)
                    CV.inc(v.scalar_tensor_tensor(we_sb[:, tt * 32:(tt + 1) * 32],
                                                  g32[:, 64:96], cba[:, 4:5],
                                                  g32[:, 96:128], ALU.mult, ALU.mult))

            for batch in range(2):
                @blk.tensor
                def _(t, batch=batch):
                    CV.wait(t)
                    for j in range(4):
                        tt = batch * 4 + j
                        ins = t.transpose(pC[batch][0:32, j * 128:(j + 1) * 128],
                                          we_sb[:, tt * 32:(tt + 1) * 32], ident32_sb[:])
                    CT.inc(ins)

                @blk.vector
                def _(v, batch=batch):
                    CT.wait(v)
                    for j in range(4):
                        tt = batch * 4 + j
                        ins = v.tensor_copy(we_fm_sb[:, tt * 128:(tt + 1) * 128],
                                            pC[batch][0:32, j * 128:(j + 1) * 128])
                    CV.inc(ins)

            if DEBUG_TAPS:
                @blk.sync
                def _(sy):
                    CV.wait(sy)
                    CS.wait(sy)
                    CDX.inc(sy.dma_start(out=dbg["dbg_xn"].ap(),
                                         in_=union_sb[:, 0:KT * TOK]), 16)
                    CDX.inc(sy.dma_start(out=dbg["dbg_we"].ap(), in_=we_sb[:]), 16)
                    CDX.inc(sy.dma_start(out=dbg["dbg_g32"].ap(), in_=g32[:]), 16)
                    CDX.inc(sy.dma_start(out=dbg["dbg_top8"].ap(), in_=top8[:]), 16)

            @blk.gpsimd
            def _(g):
                for e in range(2):
                    pw = PW[e % 2]
                    pw.inc(g.dma_start(out=ew_sb[e % 2][:, 0:128],
                                       in_=dt["ug_all"].ap()[e]), 16)
                    pw.inc(g.dma_start(out=ew_sb[e % 2][:, 128:256],
                                       in_=dt["uu_all"].ap()[e]), 16)
                    pw.inc(g.dma_start(out=ew_sb[e % 2][:, 256:384],
                                       in_=dt["ud_all"].ap()[e]), 16)
                    pw.inc(g.dma_start(out=ew_sb[e % 2][:, 384:1792],
                                       in_=dt["cg_all"].ap()[e]), 16)
                    pw.inc(g.dma_start(out=ew_sb[e % 2][:, 1792:3200],
                                       in_=dt["cu_all"].ap()[e]), 16)
                g.wait_ge(CDX.sem, pj_free[1])
                CCC.inc(g.collective_compute(
                    "AllGather", ALU.bypass,
                    replica_groups=[[0, 1, 2, 3], [4, 5, 6, 7]],
                    ins=[cc_in.ap().opt()], outs=[cc_out.ap().opt()]))

            def rproj(dst_sb, w_sb):
                for h in range(2):
                    @blk.tensor
                    def _(t, h=h):
                        CV.wait(t)
                        for k in range(KT):
                            ins = t.matmul(pA[h][:],
                                           w_sb[:, k * 128:(k + 1) * 128],
                                           union_sb[:, k * TOK + h * 512:k * TOK + h * 512 + 512],
                                           start=(k == 0), stop=(k == KT - 1))
                        CT.inc(ins)

                    @blk.vector
                    def _(v, h=h):
                        CT.wait(v)
                        CV.inc(v.tensor_copy(dst_sb[:, h * 512:h * 512 + 512], pA[h][:]))

            rproj(rg_sb, rg_w_sb)
            rproj(ru_sb, ru_w_sb)
            rproj(rdx_sb, srg_w_sb)
            rproj(webc_sb, sru_w_sb)

            def stage2(dst_sb, w_sb, src_sb):
                @blk.tensor
                def _(t):
                    CV.wait(t)
                    for h in range(2):
                        ins = t.matmul(pA[h][:], w_sb[:], src_sb[:, h * 512:h * 512 + 512],
                                       start=True, stop=True)
                    CT.inc(ins)

                @blk.vector
                def _(v):
                    CT.wait(v)
                    for h in range(2):
                        ins = v.tensor_copy(dst_sb[:, h * 512:h * 512 + 512], pA[h][:])
                    CV.inc(ins)

            stage2(t1g_sb, sug_w_sb, rdx_sb)
            stage2(t1u_sb, suu_w_sb, webc_sb)

            if DEBUG_TAPS:
                @blk.sync
                def _(sy):
                    CV.wait(sy)
                    CDX.inc(sy.dma_start(out=dbg["dbg_rg"].ap(), in_=rg_sb[:]), 16)
                    CDX.inc(sy.dma_start(out=dbg["dbg_ru"].ap(), in_=ru_sb[:]), 16)

            def chunk_loop(n_chunks, cgw, cuw, gsrc, usrc, rdw, rd_psum):
                for ci in range(n_chunks):
                    @blk.tensor
                    def _(t, ci=ci):
                        CS.wait(t)
                        if ci == 0:
                            CV.wait(t)
                        for h in range(2):
                            t.matmul(pA[h][:], cgw[:, ci * 128:(ci + 1) * 128],
                                     gsrc[:, h * 512:h * 512 + 512], start=True, stop=True)
                        for h in range(2):
                            ins = t.matmul(pB[h][:], cuw[:, ci * 128:(ci + 1) * 128],
                                           usrc[:, h * 512:h * 512 + 512], start=True, stop=True)
                        CT.inc(ins)

                    @blk.scalar
                    def _(s, ci=ci):
                        CT.wait(s)
                        if ci > 0:
                            CV.wait(s)
                        for h in range(2):
                            s.activation(gch_sb[:, h * 512:h * 512 + 512], pA[h][:], AF.Silu)
                        for h in range(2):
                            ins = s.activation(uch_sb[:, h * 512:h * 512 + 512], pB[h][:],
                                               AF.Copy)
                        CS.inc(ins)

                    @blk.vector
                    def _(v, ci=ci):
                        CS.wait(v)
                        CV.inc(v.tensor_mul(gu_sb[:], gch_sb[:], uch_sb[:]))

                    @blk.tensor
                    def _(t, ci=ci, n_chunks=n_chunks):
                        CV.wait(t)
                        for h in range(2):
                            ins = t.matmul(rd_psum[h][:], rdw[:, ci * 128:(ci + 1) * 128],
                                           gu_sb[:, h * 512:h * 512 + 512],
                                           start=(ci == 0), stop=(ci == n_chunks - 1))
                        if ci == n_chunks - 1:
                            CT.inc(ins)

            chunk_loop(NSC, scg_w_sb, scu_w_sb, t1g_sb, t1u_sb, srd_w_sb, pC)

            @blk.vector
            def _(v):
                CT.wait(v)
                for h in range(2):
                    ins = v.tensor_copy(rdx_sb[:, h * 512:h * 512 + 512], pC[h][:])
                CV.inc(ins)

            @blk.tensor
            def _(t):
                CV.wait(t)
                for h in range(2):
                    ins = t.matmul(pA[h][:], sud_w_sb[:], rdx_sb[:, h * 512:h * 512 + 512],
                                   start=True, stop=True)
                CT.inc(ins)

            @blk.vector
            def _(v):
                CT.wait(v)
                for h in range(2):
                    ins = v.tensor_copy(union_sb[:, 32 * TOK + h * 512:32 * TOK + h * 512 + 512],
                                        pA[h][:])
                CV.inc(ins)

            ew_free = [0, 0]
            for e in range(NE):
                buf = ew_sb[e % 2]
                pw = PW[e % 2]
                if e >= 2:
                    @blk.gpsimd
                    def _(g, e=e, pw=pw, buf=buf):
                        g.wait_ge(CT.sem, ew_free[e % 2])
                        pw.inc(g.dma_start(out=buf[:, 0:128], in_=dt["ug_all"].ap()[e]), 16)
                        pw.inc(g.dma_start(out=buf[:, 128:256], in_=dt["uu_all"].ap()[e]), 16)
                        pw.inc(g.dma_start(out=buf[:, 256:384], in_=dt["ud_all"].ap()[e]), 16)
                        pw.inc(g.dma_start(out=buf[:, 384:1792], in_=dt["cg_all"].ap()[e]), 16)
                        pw.inc(g.dma_start(out=buf[:, 1792:3200], in_=dt["cu_all"].ap()[e]), 16)

                @blk.tensor
                def _(t, e=e, buf=buf, pw=pw):
                    pw.wait(t)
                    CS.wait(t)
                    CV.wait(t)
                    for h in range(2):
                        t.matmul(pA[h][:], buf[:, 0:128], rg_sb[:, h * 512:h * 512 + 512],
                                 start=True, stop=True)
                    for h in range(2):
                        t.matmul(pB[h][:], buf[:, 128:256], ru_sb[:, h * 512:h * 512 + 512],
                                 start=True, stop=True)
                    for h in range(2):
                        ins = t.matmul(pC[h][0:1, :], ident32_sb[0:32, e:e + 1],
                                       we_fm_sb[:, h * 512:h * 512 + 512],
                                       start=True, stop=True)
                    CT.inc(ins)

                @blk.scalar
                def _(s, e=e):
                    CT.wait(s)
                    for h in range(2):
                        s.activation(srow_sb[0:1, h * 512:h * 512 + 512], pC[h][0:1, :],
                                     AF.Copy)
                    for h in range(2):
                        ins = s.activation(uch_sb[:, h * 512:h * 512 + 512], pB[h][:], AF.Copy)
                    CS.inc(ins)

                @blk.tensor
                def _(t, e=e):
                    CS.wait(t)
                    for h in range(2):
                        ins = t.matmul(pC[h][:], ones_r_sb[:],
                                       srow_sb[0:1, h * 512:h * 512 + 512],
                                       start=True, stop=True)
                    CT.inc(ins)

                @blk.scalar
                def _(s, e=e):
                    CT.wait(s)
                    for h in range(2):
                        ins = s.activation(webc_sb[:, h * 512:h * 512 + 512], pC[h][:], AF.Copy)
                    CS.inc(ins)

                @blk.vector
                def _(v, e=e):
                    CS.wait(v)
                    for h in range(2):
                        v.tensor_copy(t1g_sb[:, h * 512:h * 512 + 512], pA[h][:])
                    CV.inc(v.tensor_mul(t1u_sb[:], uch_sb[:], webc_sb[:]))

                if DEBUG_TAPS and e == 0:
                    @blk.sync
                    def _(sy):
                        CV.wait(sy)
                        CDX.inc(sy.dma_start(out=dbg["dbg_t1g0"].ap(), in_=t1g_sb[:]), 16)
                        CDX.inc(sy.dma_start(out=dbg["dbg_t1u0"].ap(), in_=t1u_sb[:]), 16)

                chunk_loop(NIC, buf[:, 384:1792], buf[:, 1792:3200],
                           t1g_sb, t1u_sb, rd_w_sb, pC)
                ew_free[e % 2] = CT.n

                if DEBUG_TAPS and e == 0:
                    @blk.sync
                    def _(sy):
                        CV.wait(sy)
                        CDX.inc(sy.dma_start(out=dbg["dbg_gu0"].ap(), in_=gu_sb[:]), 16)

                @blk.vector
                def _(v, e=e):
                    CT.wait(v)
                    for h in range(2):
                        ins = v.tensor_copy(rdx_sb[:, h * 512:h * 512 + 512], pC[h][:])
                    CV.inc(ins)

                @blk.tensor
                def _(t, e=e, buf=buf):
                    CV.wait(t)
                    for h in range(2):
                        ins = t.matmul(pA[h][:], buf[:, 256:384],
                                       rdx_sb[:, h * 512:h * 512 + 512], start=True, stop=True)
                    CT.inc(ins)

                @blk.vector
                def _(v, e=e):
                    CT.wait(v)
                    for h in range(2):
                        ins = v.tensor_copy(
                            union_sb[:, e * TOK + h * 512:e * TOK + h * 512 + 512], pA[h][:])
                    CV.inc(ins)

                if DEBUG_TAPS and e == 0:
                    @blk.sync
                    def _(sy):
                        CV.wait(sy)
                        CDX.inc(sy.dma_start(out=dbg["dbg_rd0"].ap(), in_=rdx_sb[:]), 16)

            @blk.gpsimd
            def _(g):
                for e in range(NE):
                    CDW.inc(g.dma_start(out=cdbf.ap()[e], in_=dt["cd_all"].ap()[e]), 16)
            n_cdbf = CDW.n

            @blk.sync
            def _(sy):
                CCC.wait(sy)
                for c in range(2):
                    PJ[c % 2].inc(sy.dma_start(out=pj_sb[c % 2][:],
                                               in_=cc_out.ap()[c * 128:(c + 1) * 128, :]), 16)

            @blk.vector
            def _(v):
                v.memset(s_H0[:], 0.0)
                CV.inc(v.memset(s_H0b[:], 0.0))

            for c in range(NCH):
                pjc = PJ[c % 2]
                pjt = pj_sb[c % 2]
                pj_need = pjc.n

                @blk.vector
                def _(v, c=c, pjc=pjc, pjt=pjt, pj_need=pj_need):
                    pjc.wait(v, pj_need)
                    CV.wait(v)
                    v.tensor_mul(s_scr[:], pjt[:, 0:64], pjt[:, 0:64])
                    v.tensor_reduce(cbb[:, 4:5], s_scr[:], AX.X, ALU.add)
                    v.tensor_mul(s_scr[:], pjt[:, 128:192], pjt[:, 128:192])
                    CV.inc(v.tensor_reduce(cbb[:, 5:6], s_scr[:], AX.X, ALU.add))

                @blk.scalar
                def _(s, c=c):
                    CV.wait(s)
                    s.activation(cbb[:, 4:5], cbb[:, 4:5], AF.Sqrt)
                    CS.inc(s.activation(cbb[:, 5:6], cbb[:, 5:6], AF.Sqrt))

                @blk.vector
                def _(v, c=c, pjt=pjt):
                    CS.wait(v)
                    v.tensor_scalar_max(cbb[:, 4:5], cbb[:, 4:5], 1e-12)
                    v.tensor_scalar_max(cbb[:, 5:6], cbb[:, 5:6], 1e-12)
                    v.reciprocal(cbb[:, 4:5], cbb[:, 4:5])
                    CV.inc(v.reciprocal(cbb[:, 5:6], cbb[:, 5:6]))
                    CV.wait(v)
                    CV.inc(v.tensor_add(s_tmp[:], pjt[:, 192:256], wab_b_sb[:]))

                @blk.scalar
                def _(s, c=c):
                    CV.wait(s)
                    CS.inc(s.activation(s_scr[:], s_tmp[:], AF.Sigmoid))

                @blk.vector
                def _(v, c=c):
                    CS.wait(v)
                    CV.inc(v.tensor_mul(s_tmp[:], s_scr[:], la8b_sb[:]))

                @blk.tensor
                def _(t, c=c):
                    CV.wait(t)
                    CT.inc(t.matmul(pC0[:, 0:64], ltinc_sb[:], s_tmp[:],
                                    start=True, stop=True))

                @blk.vector
                def _(v, c=c):
                    CT.wait(v)
                    v.tensor_copy(s_c[:], pC0[:, 0:64])
                    CV.inc(v.tensor_sub(s_Eneg[:], s_c[:], s_tmp[:]))

                @blk.scalar
                def _(s, c=c):
                    CV.wait(s)
                    s.activation(s_Em1[:], s_Eneg[:], AF.Exp)
                    s.activation(s_A[:], s_c[:], AF.Exp)
                    s.activation(s_scr[:], s_tmp[:], AF.Exp)
                    CS.inc(s.activation(s_Eneg[:], s_c[:], AF.Exp, scale=-1.0))

                @blk.vector
                def _(v, c=c, pjt=pjt):
                    CS.wait(v)
                    v.tensor_scalar(s_scr[:], s_scr[:], -1.0, 1.0, ALU.mult, ALU.add)
                    v.tensor_mul(s_tmp[:], s_scr[:], s_Eneg[:])
                    v.tensor_scalar_mul(s_kn[:], pjt[:, 0:64], cbb[:, 4:5])
                    v.tensor_scalar_mul(s_qn[:], pjt[:, 128:192], cbb[:, 5:6])
                    v.tensor_mul(s_qt[:], s_qn[:], s_Em1[:])
                    v.tensor_mul(s_kt[:], s_kn[:], s_Em1[:])
                    v.tensor_mul(s_kh[:], s_kn[:], s_tmp[:])
                    v.tensor_mul(s_qA[:], s_qn[:], s_A[:])
                    v.tensor_copy(s_qtb[:], s_qt[:])
                    v.tensor_copy(s_ktb[:], s_kt[:])
                    v.tensor_copy(s_khb[:], s_kh[:])
                    v.tensor_copy(s_qAb[:], s_qA[:])
                    CV.inc(v.tensor_copy(s_vb[:], pjt[:, 64:128]))
                pj_consumed = CV.n


                @blk.tensor
                def _(t, c=c):
                    CV.wait(t)
                    t.transpose(pT0[0:64, 0:128], s_qtb[:], identbf_sb[:])
                    t.transpose(pT0[0:64, 128:256], s_ktb[:], identbf_sb[:])
                    t.transpose(pT0[0:64, 256:384], s_khb[:], identbf_sb[:])
                    t.transpose(pT0[0:64, 384:512], s_qAb[:], identbf_sb[:])
                    CT.inc(t.transpose(pC1[0:64, 0:128], s_c[:], ident32_sb[:]))

                @blk.vector
                def _(v, c=c):
                    CT.wait(v)
                    v.tensor_copy(s_qtf[:], pT0[0:64, 0:128])
                    v.tensor_copy(s_ktf[:], pT0[0:64, 128:256])
                    v.tensor_copy(s_khf[:], pT0[0:64, 256:384])
                    v.tensor_copy(s_qAf[:], pT0[0:64, 384:512])
                    CV.inc(v.tensor_copy(s_cfm[:], pC1[0:64, 0:128]))

                @blk.scalar
                def _(s, c=c):
                    CV.wait(s)
                    CS.inc(s.activation(s_alast[:], s_cfm[:, 127:128], AF.Exp))

                @blk.tensor
                def _(t, c=c):
                    CV.wait(t)
                    t.matmul(pA0[:, 0:128], s_khf[:], s_qtf[:], start=True, stop=True)
                    t.matmul(pA0[:, 128:256], s_khf[:], s_ktf[:], start=True, stop=True)
                    t.matmul(pA0[:, 256:384], s_khf[:], s_qAf[:], start=True, stop=True)
                    t.matmul(pB0[:, 0:64], s_qtf[:], s_H0b[:], start=True, stop=True)
                    t.matmul(pB0[:, 64:128], s_ktf[:], s_H0b[:], start=True, stop=True)
                    CT.inc(t.matmul(pB0[:, 128:192], s_qAf[:], s_H0b[:],
                                    start=True, stop=True))

                @blk.vector
                def _(v, c=c, pjt=pjt):
                    CT.wait(v)
                    v.tensor_mul(s_GT[:], pA0[:, 0:128], ltstrict_sb[:])
                    v.tensor_mul(s_MT[:], pA0[:, 128:256], ltstrict_sb[:])
                    v.tensor_mul(s_GaT[:], pA0[:, 256:384], ltinc_sb[:])
                    v.tensor_sub(s_e0w0[:, 0:64], pjt[:, 64:128], pB0[:, 0:64])
                    v.tensor_sub(s_e0w0[:, 64:128], pjt[:, 64:128], pB0[:, 64:128])
                    v.tensor_copy(s_y0[:], pB0[:, 128:192])
                    CV.inc(v.memset(s_U[:], 0.0))
                pj_done = CV.n

                if c + 2 < NCH:
                    @blk.sync
                    def _(sy, c=c, pj_done=pj_done):
                        sy.wait_ge(CV.sem, pj_done)
                        PJ[c % 2].inc(sy.dma_start(
                            out=pj_sb[c % 2][:],
                            in_=cc_out.ap()[(c + 2) * 128:(c + 3) * 128, :]), 16)

                for it in range(NIT):
                    @blk.tensor
                    def _(t, c=c, it=it):
                        CV.wait(t)
                        t.matmul(pA1[:, 0:64], s_GT[:], s_U[:], start=True, stop=True)
                        CT.inc(t.matmul(pA1[:, 64:128], s_MT[:], s_U[:],
                                        start=True, stop=True))

                    @blk.vector
                    def _(v, c=c, it=it):
                        CT.wait(v)
                        v.tensor_sub(s_ew[:], s_e0w0[:], pA1[:, 0:128])
                        v.tensor_mul(s_scr[:], s_ew[:, 0:64], s_ew[:, 0:64])
                        CV.inc(v.tensor_reduce(cbb[:, 2:3], s_scr[:], AX.X, ALU.add))

                    @blk.scalar
                    def _(s, c=c, it=it):
                        CV.wait(s)
                        CS.inc(s.activation(cba[:, 3:4], cbb[:, 2:3],
                                            AF.Sigmoid, scale=1.0 / (1.0 + 1e-6)))

                    @blk.vector
                    def _(v, c=c, it=it):
                        CS.wait(v)
                        CV.inc(v.tensor_scalar_mul(s_U[:], s_ew[:, 64:128],
                                                   cba[:, 3:4]))

                @blk.tensor
                def _(t, c=c):
                    CV.wait(t)
                    t.matmul(pB1[:, 0:64], s_GaT[:], s_U[:], start=True, stop=True)
                    CT.inc(t.matmul(pB1[0:64, 64:128], s_khb[:], s_U[:],
                                    start=True, stop=True))

                @blk.vector
                def _(v, c=c):
                    CT.wait(v)
                    CV.inc(v.tensor_add(s_kn[:], s_y0[:], pB1[:, 0:64]))

                @blk.vector
                def _(v, c=c):
                    CV.wait(v)
                    v.tensor_mul(s_scr[:], s_kn[:], s_kn[:])
                    CV.inc(v.tensor_reduce(cbb[:, 7:8], s_scr[:], AX.X, ALU.add))
                    CV.wait(v)
                    CV.inc(v.tensor_scalar(cba[:, 5:6], cbb[:, 7:8],
                                           1.0 / DST, 1e-6, ALU.mult, ALU.add))

                @blk.scalar
                def _(s, c=c):
                    CV.wait(s)
                    CS.inc(s.activation(cba[:, 5:6], cba[:, 5:6], AF.Sqrt))

                @blk.vector
                def _(v, c=c):
                    CS.wait(v)
                    CV.inc(v.reciprocal(cba[:, 5:6], cba[:, 5:6]))
                    CV.wait(v)
                    CV.inc(v.tensor_scalar_mul(yn_tm_sb[:, c * DST:(c + 1) * DST],
                                               s_kn[:], cba[:, 5:6]))
                    v.tensor_add(s_scr[0:64, :], pB1[0:64, 64:128], s_H0[:])
                    v.tensor_scalar_mul(s_H0[:], s_scr[0:64, :], s_alast[:])
                    CV.inc(v.tensor_copy(s_H0b[:], s_H0[:]))

                @blk.sync
                def _(sy, c=c):
                    CV.wait(sy)
                    CDX.inc(sy.dma_start(out=cc2_in.ap()[c * 128:(c + 1) * 128, :],
                                         in_=yn_tm_sb[:, c * DST:(c + 1) * DST]), 16)
            n_yn_dma = CDX.n

            if DEBUG_TAPS:
                @blk.sync
                def _(sy):
                    sy.wait_ge(CDX.sem, n_yn_dma)
                    CV.wait(sy)
                    CDX.inc(sy.dma_start(out=dbg["dbg_sGT"].ap()[:, 0:128], in_=s_GT[:]), 16)
                    CDX.inc(sy.dma_start(out=dbg["dbg_sGT"].ap()[:, 128:256], in_=s_MT[:]), 16)
                    CDX.inc(sy.dma_start(out=dbg["dbg_sGT"].ap()[:, 256:384], in_=s_GaT[:]), 16)
                    CDX.inc(sy.dma_start(out=dbg["dbg_e0w0"].ap(), in_=s_e0w0[:]), 16)
                    CDX.inc(sy.dma_start(out=dbg["dbg_U"].ap(), in_=s_U[:]), 16)
                    for i, tile in enumerate((s_c, s_A, s_Em1, s_Eneg, s_qt, s_kt, s_kh)):
                        CDX.inc(sy.dma_start(
                            out=dbg["dbg_prep"].ap()[:, i * 64:(i + 1) * 64], in_=tile[:]), 16)
                    CDX.inc(sy.dma_start(out=dbg["dbg_proj"].ap(), in_=cc_out.ap()), 16)
                    CDX.inc(sy.dma_start(out=dbg["dbg_yn"].ap(), in_=cc2_in.ap()), 16)
                    CDX.inc(sy.dma_start(out=dbg["dbg_h0"].ap(), in_=s_H0[:]), 16)

            @blk.gpsimd
            def _(g):
                g.wait_ge(CDX.sem, n_yn_dma)
                CCC.inc(g.collective_compute(
                    "ReduceScatter", ALU.max,
                    replica_groups=[[0, 1, 2, 3], [4, 5, 6, 7]],
                    ins=[cc2_in.ap().opt()], outs=[cc2_out.ap().opt()]))

            @blk.sync
            def _(sy):
                CCC.wait(sy)
                for tt in range(NT):
                    CDX.inc(sy.dma_start(out=yn_tm_sb[:, tt * 64:(tt + 1) * 64],
                                         in_=cc2_out.ap()[tt * 128:(tt + 1) * 128, :]), 16)

            @blk.vector
            def _(v):
                CDX.wait(v)
                CV.inc(v.tensor_copy(ynb_sb[:], yn_tm_sb[:, 0:NT * 64]))

            @blk.tensor
            def _(t):
                CV.wait(t)
                for tt in range(NT):
                    ins = t.transpose(pT0[0:64, tt * 128:(tt + 1) * 128],
                                      ynb_sb[:, tt * 64:(tt + 1) * 64], identbf_sb[:])
                CT.inc(ins)

            @blk.vector
            def _(v):
                CT.wait(v)
                CV.inc(v.tensor_copy(ynfm_sb[:], pT0[0:64, 0:1024]))
            n_ynfm = CV.n

            if DEBUG_TAPS:
                @blk.sync
                def _(sy):
                    CV.wait(sy)
                    CDX.inc(sy.dma_start(out=dbg["dbg_ynown"].ap(), in_=cc2_out.ap()), 16)
                    CDX.inc(sy.dma_start(out=dbg["dbg_t2"].ap(), in_=union_sb[:]), 16)

        with (
            nc.psum_tensor("pyA", [128, 2048], F32) as pyA,
            nc.psum_tensor("pyB", [128, 2048], F32) as pyB,
        ):
            py = [pyA, pyB]
            cdbuf = [wproj_sb[:, 0:2048], wproj_sb[:, 2048:4096]]
            cd_free = [CT.n, CT.n]
            xq_free = [CV.n, CV.n]
            out_dma = [CDX.n, CDX.n]
            for st in range(4):
                for q in range(2):
                    tt = st * 2 + q

                    @blk.sync
                    def _(sy, st=st, q=q, tt=tt):
                        sy.wait_ge(CV.sem, xq_free[q])
                        sy.wait_ge(CDX.sem, out_dma[q])
                        PX[q].inc(sy.dma_start(out=xb[q][:],
                                               in_=dt["x_slice"].ap()[tt * 128:(tt + 1) * 128, :]),
                                  16)

                for e in range(NE + 1):
                    if e < NE:
                        @blk.gpsimd
                        def _(g, st=st, e=e):
                            g.wait_ge(CT.sem, cd_free[e % 2])
                            if st == 0 and e < 2:
                                g.wait_ge(CDW.sem, n_cdbf)
                            PCD[e % 2].inc(g.dma_start(out=cdbuf[e % 2], in_=cdbf.ap()[e]), 16)

                    @blk.tensor
                    def _(t, st=st, e=e):
                        if e < NE:
                            PCD[e % 2].wait(t)
                        if e == 0:
                            CV.wait(t)
                        mov = cdbuf[e % 2] if e < NE else scd_w_sb[:]
                        for q in range(2):
                            tt = st * 2 + q
                            stat = union_sb[:, e * TOK + tt * 128:e * TOK + (tt + 1) * 128]
                            for nch in range(4):
                                ins = t.matmul(py[q][:, nch * 512:(nch + 1) * 512],
                                               stat, mov[:, nch * 512:(nch + 1) * 512],
                                               start=(e == 0), stop=False)
                        if e < NE:
                            CT.inc(ins)
                            cd_free[e % 2] = CT.n
                        else:
                            CT.inc(ins)

                @blk.tensor
                def _(t, st=st):
                    t.wait_ge(CV.sem, n_ynfm)
                    for q in range(2):
                        tt = st * 2 + q
                        stat = ynfm_sb[:, tt * 128:(tt + 1) * 128]
                        for nch in range(4):
                            ins = t.matmul(py[q][:, nch * 512:(nch + 1) * 512],
                                           stat, wo_sb[:, nch * 512:(nch + 1) * 512],
                                           start=False, stop=True)
                    CT.inc(ins)

                for q in range(2):
                    tt = st * 2 + q

                    @blk.vector
                    def _(v, st=st, q=q, tt=tt):
                        CT.wait(v)
                        PX[q].wait(v)
                        CV.inc(v.tensor_add(xb[q][:], xb[q][:], py[q][:]))
                    xq_free[q] = CV.n

                    @blk.sync
                    def _(sy, st=st, q=q, tt=tt):
                        CV.wait(sy)
                        CDX.inc(sy.dma_start(out=out_t.ap()[tt * 128:(tt + 1) * 128, :],
                                             in_=xb[q][:]), 16)
                    out_dma[q] = CDX.n

            @blk.sync
            def _(sy):
                CDX.wait(sy)

    return nc


_NC_CACHE = {}


def kernel(**inputs):
    f32 = np.float32

    def ca(x):
        return np.ascontiguousarray(np.asarray(x, dtype=f32))

    x = ca(inputs["L0_attn_x"])
    Wk = ca(inputs["L0_attn_Wk"]); Wv = ca(inputs["L0_attn_Wv"])
    Wq = ca(inputs["L0_attn_Wq"]); Wa = ca(inputs["L0_attn_Wa_w"])
    Wa_b = ca(inputs["L0_attn_Wa_b"]); lam = ca(inputs["L0_attn_lam"])
    norm_w = ca(inputs["L0_attn_norm_w"]); Wo = ca(inputs["L0_attn_Wo"])
    gate_w = ca(inputs["L0_ffn_gate_w"])
    Rg = ca(inputs["L0_ffn_Rg"]); Ru = ca(inputs["L0_ffn_Ru"]); Rd = ca(inputs["L0_ffn_Rd"])
    Ug = ca(inputs["L0_ffn_Ug"]); Cg = ca(inputs["L0_ffn_Cg"])
    Uu = ca(inputs["L0_ffn_Uu"]); Cu = ca(inputs["L0_ffn_Cu"])
    Ud = ca(inputs["L0_ffn_Ud"]); Cd = ca(inputs["L0_ffn_Cd"])
    s_Rg = ca(inputs["L0_ffn_s_Rg"]); s_Ug = ca(inputs["L0_ffn_s_Ug"])
    s_Cg = ca(inputs["L0_ffn_s_Cg"]); s_Ru = ca(inputs["L0_ffn_s_Ru"])
    s_Uu = ca(inputs["L0_ffn_s_Uu"]); s_Cu = ca(inputs["L0_ffn_s_Cu"])
    s_Rd = ca(inputs["L0_ffn_s_Rd"]); s_Ud = ca(inputs["L0_ffn_s_Ud"])
    s_Cd = ca(inputs["L0_ffn_s_Cd"])

    def kimg(wT, kt):
        Dc, M = wT.shape
        assert Dc == kt * 128
        return np.ascontiguousarray(wT.reshape(kt, 128, M).transpose(1, 0, 2).reshape(128, kt * M))

    wproj = np.concatenate([Wk.T, Wv.T, Wq.T, Wa.T, gate_w.T], axis=1)
    shared = {
        "wproj": kimg(wproj, 16),
        "rg_w": kimg(Rg.T, 16), "ru_w": kimg(Ru.T, 16),
        "srg_w": kimg(s_Rg.T, 16), "sru_w": kimg(s_Ru.T, 16),
        "rd_w": kimg(Rd.T, 11), "srd_w": kimg(s_Rd.T, 22),
        "scg_w": np.ascontiguousarray(s_Cg.T), "scu_w": np.ascontiguousarray(s_Cu.T),
        "scd_w": np.ascontiguousarray(s_Cd.T),
        "sug_w": np.ascontiguousarray(s_Ug.T), "suu_w": np.ascontiguousarray(s_Uu.T),
        "sud_w": np.ascontiguousarray(s_Ud.T),
        "ug_all": np.ascontiguousarray(Ug.transpose(0, 2, 1)),
        "uu_all": np.ascontiguousarray(Uu.transpose(0, 2, 1)),
        "ud_all": np.ascontiguousarray(Ud.transpose(0, 2, 1)),
        "cg_all": np.ascontiguousarray(Cg.transpose(0, 2, 1)),
        "cu_all": np.ascontiguousarray(Cu.transpose(0, 2, 1)),
        "cd_all": np.ascontiguousarray(Cd.transpose(0, 2, 1)),
        "wo_w": np.ascontiguousarray(Wo.T),
        "normw": np.ascontiguousarray(norm_w.reshape(DST, 1)),
        "lam_r": np.ascontiguousarray(lam.reshape(1, DST)),
        "wab_r": np.ascontiguousarray(Wa_b.reshape(1, DST)),
        "ident32": np.eye(128, dtype=f32),
        "identbf": np.eye(128, dtype=NPBF),
        "ltinc": np.triu(np.ones((128, 128), f32)),
        "ltstrict": np.triu(np.ones((128, 128), f32), 1),
        "ones_r": np.ones((1, 128), f32),
    }

    xf = x.reshape(NCORES, TOK, D)
    in_maps = []
    for c in range(NCORES):
        m = dict(shared)
        m["x_slice"] = np.ascontiguousarray(xf[c])
        in_maps.append(m)

    if "nc" not in _NC_CACHE:
        _NC_CACHE["nc"] = build_nc()
    nc = _NC_CACHE["nc"]

    res = run_bass_kernel_spmd(nc, in_maps, core_ids=list(range(NCORES)))
    out = np.concatenate([res.results[c]["out"] for c in range(NCORES)], axis=0)
    return out.reshape(2, SEQ, D)


# version 13
# speedup vs baseline: 1.3330x; 1.3099x over previous
import numpy as np
import ml_dtypes
from contextlib import ExitStack

import concourse.bass as bass
import concourse.mybir as mybir
from concourse.bass_utils import run_bass_kernel_spmd

F32 = mybir.dt.float32
BF16 = mybir.dt.bfloat16
AF = mybir.ActivationFunctionType
ALU = mybir.AluOpType
AX = mybir.AxisListType
NPBF = ml_dtypes.bfloat16

NCORES = 8
TOK = 1024
D = 2048
NT = 8
KT = 16
NE = 32
NIC = 11
NSC = 22
SEQ = 4096
CH = 128
NCH = 32
NIT = 6
DST = 64
DEBUG_TAPS = False


class Chan:
    def __init__(self, st, nc, name):
        self.sem = st.enter_context(nc.semaphore(name))
        self.n = 0

    def inc(self, instr, k=1):
        instr.then_inc(self.sem, k)
        self.n += k
        return instr

    def wait(self, eng, n=None):
        eng.wait_ge(self.sem, self.n if n is None else n)


def build_nc():
    nc = bass.Bass("TRN2", target_bir_lowering=False)

    dt = {}
    def din(name, shape, dtype=F32):
        dt[name] = nc.dram_tensor(name, shape, dtype, kind="ExternalInput")
        return dt[name]

    din("x_slice", [TOK, D])
    din("wproj", [128, KT * 288])
    for n in ("rg_w", "ru_w", "srg_w", "sru_w"):
        din(n, [128, 2048])
    din("rd_w", [128, 1408])
    din("srd_w", [128, 2816])
    din("scg_w", [128, 2816])
    din("scu_w", [128, 2816])
    din("scd_w", [128, 2048])
    for n in ("sug_w", "suu_w", "sud_w"):
        din(n, [128, 128])
    din("ug_all", [NE, 128, 128]); din("uu_all", [NE, 128, 128]); din("ud_all", [NE, 128, 128])
    din("cg_all", [NE, 128, 1408]); din("cu_all", [NE, 128, 1408])
    din("cd_all", [NE, 128, 2048])
    din("wo_w", [DST, 2048])
    din("normw", [DST, 1])
    din("lam_r", [1, DST])
    din("wab_r", [1, DST])
    din("ident32", [128, 128])
    din("identbf", [128, 128], BF16)
    din("ltinc", [128, 128])
    din("ltstrict", [128, 128])
    din("ones_r", [1, 128])
    din("ones_rb", [1, 128], BF16)

    out_t = nc.dram_tensor("out", [TOK, D], F32, kind="ExternalOutput")

    dbg = {}
    if DEBUG_TAPS:
        for name, shape, dtp in (
            ("dbg_xn", [128, KT * TOK], BF16),
            ("dbg_we", [128, NT * 32], F32),
            ("dbg_rg", [128, TOK], BF16),
            ("dbg_ru", [128, TOK], BF16),
            ("dbg_t2", [128, 33 * TOK], BF16),
            ("dbg_proj", [SEQ, 256], F32),
            ("dbg_yn", [SEQ, DST], F32),
            ("dbg_ynown", [TOK, DST], F32),
            ("dbg_h0", [64, DST], F32),
            ("dbg_gu0", [128, TOK], BF16),
            ("dbg_g32", [128, 128], F32),
            ("dbg_top8", [128, 8], F32),
            ("dbg_sGT", [128, 384], BF16),
            ("dbg_e0w0", [128, 128], F32),
            ("dbg_U", [128, DST], BF16),
            ("dbg_prep", [128, 448], F32),
            ("dbg_t1g0", [128, TOK], BF16),
            ("dbg_t1u0", [128, TOK], BF16),
            ("dbg_rd0", [128, TOK], BF16),
        ):
            dbg[name] = nc.dram_tensor(name, shape, dtp, kind="ExternalOutput")

    cc_in = nc.dram_tensor("cc_in", [TOK, 256], F32)
    cc_out = nc.dram_tensor("cc_out", [SEQ, 256], F32)
    cc2_in = nc.dram_tensor("cc2_in", [SEQ, DST], F32)
    cc2_out = nc.dram_tensor("cc2_out", [TOK, DST], F32)
    cdbf = nc.dram_tensor("cdbf", [NE, 128, 2048], BF16)

    ST = ExitStack()
    nc.ctx.enter_context(ST)

    def sb(name, shape, dtype):
        return ST.enter_context(nc.sbuf_tensor(name, shape, dtype))

    wproj_sb = sb("wproj_sb", [128, KT * 288], BF16)
    rg_w_sb = sb("rg_w_sb", [128, 2048], BF16)
    ru_w_sb = sb("ru_w_sb", [128, 2048], BF16)
    srg_w_sb = sb("srg_w_sb", [128, 2048], BF16)
    sru_w_sb = sb("sru_w_sb", [128, 2048], BF16)
    rd_w_sb = sb("rd_w_sb", [128, 1408], BF16)
    srd_w_sb = sb("srd_w_sb", [128, 2816], BF16)
    scg_w_sb = sb("scg_w_sb", [128, 2816], BF16)
    scu_w_sb = sb("scu_w_sb", [128, 2816], BF16)
    scd_w_sb = sb("scd_w_sb", [128, 2048], BF16)
    sug_w_sb = sb("sug_w_sb", [128, 128], BF16)
    suu_w_sb = sb("suu_w_sb", [128, 128], BF16)
    sud_w_sb = sb("sud_w_sb", [128, 128], BF16)
    ew_sb = [sb(f"ew{i}", [128, 3200], BF16) for i in range(2)]
    wo_sb = sb("wo_sb", [64, 2048], BF16)
    ident32_sb = sb("ident32_sb", [128, 128], F32)
    identbf_sb = sb("identbf_sb", [128, 128], BF16)
    ltinc_sb = sb("ltinc_sb", [128, 128], F32)
    ltstrict_sb = sb("ltstrict_sb", [128, 128], F32)
    ones_r_sb = sb("ones_r_sb", [1, 128], F32)
    ones_rb_sb = sb("ones_rb_sb", [1, 128], BF16)
    we_fmb_sb = sb("we_fmb_sb", [32, TOK], BF16)
    srow_b_sb = sb("srow_b_sb", [1, TOK], BF16)
    normw_sb = sb("normw_sb", [DST, 1], F32)
    la8b_sb = sb("la8b_sb", [128, DST], F32)
    wab_b_sb = sb("wab_b_sb", [128, DST], F32)
    srow_sb = sb("srow_sb", [1, 192], F32)
    union_sb = sb("union_sb", [128, 33 * TOK], BF16)
    rg_sb = sb("rg_sb", [128, TOK], BF16)
    ru_sb = sb("ru_sb", [128, TOK], BF16)
    t1g_sb = sb("t1g_sb", [128, TOK], BF16)
    t1u_sb = sb("t1u_sb", [128, TOK], BF16)
    gch_sb = sb("gch_sb", [128, TOK], BF16)
    uch_sb = sb("uch_sb", [128, TOK], BF16)
    rdx_sb = sb("rdx_sb", [128, TOK], BF16)
    webc_sb = sb("webc_sb", [128, TOK], BF16)
    gu_sb = sb("gu_sb", [128, TOK], BF16)
    gu2_sb = sb("gu2_sb", [128, TOK], BF16)
    we_sb = sb("we_sb", [128, NT * 32], F32)
    yn_tm_sb = sb("yn_tm_sb", [128, NCH * DST], F32)
    ynb_sb = sb("ynb_sb", [128, NT * DST], BF16)
    ynfm_sb = sb("ynfm_sb", [64, TOK], BF16)
    pj_sb = [sb(f"pj{i}", [128, 256], F32) for i in range(2)]
    xb = [sb(f"xb{i}", [128, D], F32) for i in range(3)]
    s_c = sb("s_c", [128, DST], F32)
    s_A = sb("s_A", [128, DST], F32)
    s_Em1 = sb("s_Em1", [128, DST], F32)
    s_Eneg = sb("s_Eneg", [128, DST], F32)
    s_tmp = sb("s_tmp", [128, DST], F32)
    s_kn = sb("s_kn", [128, DST], F32)
    s_qn = sb("s_qn", [128, DST], F32)
    s_qt = sb("s_qt", [128, DST], F32)
    s_kt = sb("s_kt", [128, DST], F32)
    s_kh = sb("s_kh", [128, DST], F32)
    s_qA = sb("s_qA", [128, DST], F32)
    s_y0 = sb("s_y0", [128, DST], F32)
    s_scr = sb("s_scr", [128, DST], F32)
    s_e0w0 = sb("s_e0w0", [128, 128], F32)
    s_ew = sb("s_ew", [128, 128], F32)
    s_qtb = sb("s_qtb", [128, DST], BF16)
    s_ktb = sb("s_ktb", [128, DST], BF16)
    s_khb = sb("s_khb", [128, DST], BF16)
    s_qAb = sb("s_qAb", [128, DST], BF16)
    s_vb = sb("s_vb", [128, DST], BF16)
    s_U = sb("s_U", [128, DST], BF16)
    s_GT = sb("s_GT", [128, 128], BF16)
    s_MT = sb("s_MT", [128, 128], BF16)
    s_GaT = sb("s_GaT", [128, 128], BF16)
    s_qtf = sb("s_qtf", [64, 128], BF16)
    s_ktf = sb("s_ktf", [64, 128], BF16)
    s_khf = sb("s_khf", [64, 128], BF16)
    s_qAf = sb("s_qAf", [64, 128], BF16)
    s_cfm = sb("s_cfm", [64, 128], F32)
    s_H0 = sb("s_H0", [64, DST], F32)
    s_H0b = sb("s_H0b", [64, DST], BF16)
    s_alast = sb("s_alast", [64, 1], F32)
    cba = sb("cba", [128, 8], F32)
    cbb = sb("cbb", [128, 8], F32)
    g32 = sb("g32", [128, 4 * 32], F32)
    top8 = sb("top8", [128, 8], F32)


    CT = Chan(ST, nc, "ct")
    CV = Chan(ST, nc, "cv")
    CS = Chan(ST, nc, "cs")
    CDW = Chan(ST, nc, "cdw")
    CDX = Chan(ST, nc, "cdx")
    CCC = Chan(ST, nc, "ccc")
    PX = [Chan(ST, nc, f"px{i}") for i in range(2)]
    PW = [Chan(ST, nc, f"pw{i}") for i in range(2)]
    PJ = [Chan(ST, nc, f"pjc{i}") for i in range(2)]
    PCD = [Chan(ST, nc, f"pcd{i}") for i in range(2)]

    with nc.Block() as blk:
        with (
            nc.psum_tensor("pA0", [128, 512], F32) as pA0,
            nc.psum_tensor("pA1", [128, 512], F32) as pA1,
            nc.psum_tensor("pB0", [128, 512], F32) as pB0,
            nc.psum_tensor("pB1", [128, 512], F32) as pB1,
            nc.psum_tensor("pC0", [128, 512], F32) as pC0,
            nc.psum_tensor("pC1", [128, 512], F32) as pC1,
            nc.psum_tensor("pT0", [128, 1024], BF16) as pT0,
            nc.psum_tensor("pT1", [128, 1024], BF16) as pT1,
        ):
            pA = [pA0, pA1]; pB = [pB0, pB1]; pC = [pC0, pC1]; pT = [pT0, pT1]

            @blk.gpsimd
            def _(g):
                for name, dst in (("ident32", ident32_sb), ("identbf", identbf_sb),
                                  ("ltinc", ltinc_sb), ("ltstrict", ltstrict_sb),
                                  ("ones_r", ones_r_sb), ("ones_rb", ones_rb_sb),
                                  ("normw", normw_sb)):
                    CDW.inc(g.dma_start(out=dst[:], in_=dt[name].ap()), 16)
                CDW.inc(g.dma_start(out=srow_sb[:, 0:DST], in_=dt["lam_r"].ap()), 16)
                CDW.inc(g.dma_start(out=srow_sb[:, 64:64 + DST], in_=dt["wab_r"].ap()), 16)
                for name, dst in (("wproj", wproj_sb), ("rg_w", rg_w_sb), ("ru_w", ru_w_sb),
                                  ("srg_w", srg_w_sb), ("sru_w", sru_w_sb), ("rd_w", rd_w_sb),
                                  ("srd_w", srd_w_sb), ("scg_w", scg_w_sb), ("scu_w", scu_w_sb),
                                  ("scd_w", scd_w_sb), ("sug_w", sug_w_sb), ("suu_w", suu_w_sb),
                                  ("sud_w", sud_w_sb)):
                    CDW.inc(g.dma_start(out=dst[:], in_=dt[name].ap()), 16)
            n_const_w = CDW.n

            @blk.sync
            def _(sy):
                CDX.inc(sy.dma_start(out=xb[2][0:64, :], in_=dt["wo_w"].ap()), 16)
            n_wo_dma = CDX.n

            @blk.scalar
            def _(s):
                s.wait_ge(CDW.sem, n_const_w)
                CS.inc(s.activation(srow_sb[:, 128:128 + DST], srow_sb[:, 0:DST], AF.Sigmoid))

            @blk.vector
            def _(v):
                v.memset(cba[:], 0.0)
                v.memset(cbb[:], 0.0)
                CS.wait(v)
                CV.inc(v.tensor_scalar_add(srow_sb[:, 128:128 + DST],
                                           srow_sb[:, 128:128 + DST], 1e-8))

            @blk.scalar
            def _(s):
                CV.wait(s)
                CS.inc(s.activation(srow_sb[:, 128:128 + DST], srow_sb[:, 128:128 + DST],
                                    AF.Ln))

            @blk.vector
            def _(v):
                CS.wait(v)
                CV.inc(v.tensor_scalar_mul(srow_sb[:, 128:128 + DST],
                                           srow_sb[:, 128:128 + DST], 8.0))

            @blk.tensor
            def _(t):
                CV.wait(t)
                t.matmul(pB0[:, 0:DST], ones_r_sb[:], srow_sb[:, 128:128 + DST],
                         start=True, stop=True)
                CT.inc(t.matmul(pB0[:, 64:64 + DST], ones_r_sb[:],
                                srow_sb[:, 64:64 + DST], start=True, stop=True))

            @blk.vector
            def _(v):
                CT.wait(v)
                v.tensor_copy(la8b_sb[:], pB0[:, 0:DST])
                CV.inc(v.tensor_copy(wab_b_sb[:], pB0[:, 64:64 + DST]))
                v.wait_ge(CDX.sem, n_wo_dma)
                CV.inc(v.tensor_scalar_mul(wo_sb[:], xb[2][0:64, :], normw_sb[:]))
            n_wo_done = CV.n

            xfree = [0, 0]
            pj_free = [0, 0]
            for tt in range(NT):
                pxc = PX[tt % 2]

                @blk.sync
                def _(sy, tt=tt, pxc=pxc):
                    if tt >= 2:
                        sy.wait_ge(CV.sem, xfree[tt % 2])
                    if tt == 0:
                        sy.wait_ge(CV.sem, n_wo_done)
                    pxc.inc(sy.dma_start(out=xb[tt % 2][:],
                                         in_=dt["x_slice"].ap()[tt * 128:(tt + 1) * 128, :]), 16)

                @blk.vector
                def _(v, tt=tt, pxc=pxc):
                    pxc.wait(v)
                    CV.inc(v.tensor_reduce(cbb[:, 0:1], xb[tt % 2][:], AX.X, ALU.add))
                    CV.wait(v)
                    CV.inc(v.tensor_scalar_mul(cba[:, 0:1], cbb[:, 0:1], -1.0 / D))
                    CV.wait(v)
                    v.tensor_scalar(xb[2][:], xb[tt % 2][:], cba[:, 0:1], None,
                                    ALU.add)
                    v.tensor_mul(xb[2][:], xb[2][:], xb[2][:])
                    CV.inc(v.tensor_reduce(cbb[:, 1:2], xb[2][:], AX.X, ALU.add))
                    CV.wait(v)
                    CV.inc(v.tensor_scalar(cbb[:, 1:2], cbb[:, 1:2],
                                           1.0 / D, 1e-5, ALU.mult, ALU.add))

                @blk.scalar
                def _(s, tt=tt):
                    CV.wait(s)
                    CS.inc(s.activation(cbb[:, 1:2], cbb[:, 1:2],
                                        AF.Sqrt))

                @blk.vector
                def _(v, tt=tt):
                    CS.wait(v)
                    CV.inc(v.reciprocal(cba[:, 1:2], cbb[:, 1:2]))
                    CV.wait(v)
                    CV.inc(v.tensor_scalar(cba[:, 0:1], cbb[:, 0:1],
                                           cba[:, 1:2], -1.0 / D,
                                           ALU.mult, ALU.mult))
                    CV.wait(v)
                    v.tensor_scalar(gch_sb[:], xb[tt % 2][:, 0:D // 2],
                                    cba[:, 1:2], cba[:, 0:1], ALU.mult, ALU.add)
                    CV.inc(v.tensor_scalar(uch_sb[:], xb[tt % 2][:, D // 2:D],
                                           cba[:, 1:2], cba[:, 0:1],
                                           ALU.mult, ALU.add))
                xfree[tt % 2] = CV.n

                for half in range(2):
                    src_half = [gch_sb, uch_sb][half]

                    @blk.tensor
                    def _(t, tt=tt, half=half, src_half=src_half):
                        CV.wait(t)
                        for b in range(8):
                            ins = t.transpose(pT[half][:, b * 128:(b + 1) * 128],
                                              src_half[:, b * 128:(b + 1) * 128],
                                              identbf_sb[:])
                            if b == 7:
                                CT.inc(ins)

                    @blk.vector
                    def _(v, tt=tt, half=half):
                        CT.wait(v)
                        for b in range(8):
                            fb = half * 8 + b
                            ins = v.tensor_copy(
                                union_sb[:, fb * TOK + tt * 128:fb * TOK + (tt + 1) * 128],
                                pT[half][:, b * 128:(b + 1) * 128])
                            if b == 7:
                                CV.inc(ins)

                @blk.tensor
                def _(t, tt=tt):
                    CV.wait(t)
                    ps = pA[tt % 2]
                    for k in range(KT):
                        ins = t.matmul(ps[:, 0:288],
                                       union_sb[:, k * TOK + tt * 128:k * TOK + (tt + 1) * 128],
                                       wproj_sb[:, k * 288:(k + 1) * 288],
                                       start=(k == 0), stop=(k == KT - 1))
                    CT.inc(ins)

                @blk.vector
                def _(v, tt=tt):
                    CT.wait(v)
                    if tt >= 2:
                        v.wait_ge(CDX.sem, pj_free[tt % 2])
                    ps = pA[tt % 2]
                    CV.inc(v.tensor_copy(pj_sb[tt % 2][:], ps[:, 0:256]))
                    CV.inc(v.tensor_copy(g32[:, 0:32], ps[:, 256:288]))

                @blk.sync
                def _(sy, tt=tt):
                    CV.wait(sy)
                    CDX.inc(sy.dma_start(out=cc_in.ap()[tt * 128:(tt + 1) * 128, :],
                                         in_=pj_sb[tt % 2][:]), 16)
                pj_free[tt % 2] = CDX.n

                @blk.vector
                def _(v, tt=tt):
                    CV.inc(v.reduce_max(cba[:, 2:3], g32[:, 0:32], AX.X,
                                        negate=True))
                    CV.wait(v)
                    CV.inc(v.tensor_scalar(g32[:, 32:64], g32[:, 0:32],
                                           cba[:, 2:3], None, ALU.add))

                @blk.scalar
                def _(s, tt=tt):
                    CV.wait(s)
                    CS.inc(s.activation(g32[:, 32:64], g32[:, 32:64], AF.Exp))

                @blk.vector
                def _(v, tt=tt):
                    CS.wait(v)
                    CV.inc(v.tensor_reduce(cba[:, 7:8], g32[:, 32:64], AX.X, ALU.add))
                    CV.wait(v)
                    CV.inc(v.reciprocal(cbb[:, 3:4], cba[:, 7:8]))
                    CV.wait(v)
                    CV.inc(v.tensor_scalar_mul(g32[:, 64:96], g32[:, 32:64], cbb[:, 3:4]))
                    CV.wait(v)
                    CV.inc(v.max(top8[:], g32[:, 64:96]))
                    CV.wait(v)
                    CV.inc(v.tensor_reduce(cbb[:, 6:7], top8[:, 0:4], AX.X, ALU.add))
                    CV.wait(v)
                    CV.inc(v.tensor_single_scalar(g32[:, 96:128], g32[:, 64:96],
                                                  top8[:, 3:4], ALU.is_ge))
                    CV.wait(v)
                    CV.inc(v.tensor_scalar_add(cba[:, 4:5], cbb[:, 6:7], 1e-20))
                    CV.wait(v)
                    CV.inc(v.reciprocal(cba[:, 4:5], cba[:, 4:5]))
                    CV.wait(v)
                    CV.inc(v.scalar_tensor_tensor(we_sb[:, tt * 32:(tt + 1) * 32],
                                                  g32[:, 64:96], cba[:, 4:5],
                                                  g32[:, 96:128], ALU.mult, ALU.mult))

            for batch in range(2):
                @blk.tensor
                def _(t, batch=batch):
                    CV.wait(t)
                    for j in range(4):
                        tt = batch * 4 + j
                        ins = t.transpose(pC[batch][0:32, j * 128:(j + 1) * 128],
                                          we_sb[:, tt * 32:(tt + 1) * 32], ident32_sb[:])
                    CT.inc(ins)

                @blk.vector
                def _(v, batch=batch):
                    CT.wait(v)
                    for j in range(4):
                        tt = batch * 4 + j
                        ins = v.tensor_copy(we_fmb_sb[:, tt * 128:(tt + 1) * 128],
                                            pC[batch][0:32, j * 128:(j + 1) * 128])
                    CV.inc(ins)

            if DEBUG_TAPS:
                @blk.sync
                def _(sy):
                    CV.wait(sy)
                    CS.wait(sy)
                    CDX.inc(sy.dma_start(out=dbg["dbg_xn"].ap(),
                                         in_=union_sb[:, 0:KT * TOK]), 16)
                    CDX.inc(sy.dma_start(out=dbg["dbg_we"].ap(), in_=we_sb[:]), 16)
                    CDX.inc(sy.dma_start(out=dbg["dbg_g32"].ap(), in_=g32[:]), 16)
                    CDX.inc(sy.dma_start(out=dbg["dbg_top8"].ap(), in_=top8[:]), 16)

            @blk.gpsimd
            def _(g):
                for e in range(2):
                    pw = PW[e % 2]
                    pw.inc(g.dma_start(out=ew_sb[e % 2][:, 0:128],
                                       in_=dt["ug_all"].ap()[e]), 16)
                    pw.inc(g.dma_start(out=ew_sb[e % 2][:, 128:256],
                                       in_=dt["uu_all"].ap()[e]), 16)
                    pw.inc(g.dma_start(out=ew_sb[e % 2][:, 256:384],
                                       in_=dt["ud_all"].ap()[e]), 16)
                    pw.inc(g.dma_start(out=ew_sb[e % 2][:, 384:1792],
                                       in_=dt["cg_all"].ap()[e]), 16)
                    pw.inc(g.dma_start(out=ew_sb[e % 2][:, 1792:3200],
                                       in_=dt["cu_all"].ap()[e]), 16)
                g.wait_ge(CDX.sem, pj_free[1])
                CCC.inc(g.collective_compute(
                    "AllGather", ALU.bypass,
                    replica_groups=[[0, 1, 2, 3], [4, 5, 6, 7]],
                    ins=[cc_in.ap().opt()], outs=[cc_out.ap().opt()]))

            def rproj(dst_sb, w_sb):
                for h in range(2):
                    @blk.tensor
                    def _(t, h=h):
                        CV.wait(t)
                        for k in range(KT):
                            ins = t.matmul(pA[h][:],
                                           w_sb[:, k * 128:(k + 1) * 128],
                                           union_sb[:, k * TOK + h * 512:k * TOK + h * 512 + 512],
                                           start=(k == 0), stop=(k == KT - 1))
                        CT.inc(ins)

                    @blk.vector
                    def _(v, h=h):
                        CT.wait(v)
                        CV.inc(v.tensor_copy(dst_sb[:, h * 512:h * 512 + 512], pA[h][:]))

            rproj(rg_sb, rg_w_sb)
            rproj(ru_sb, ru_w_sb)
            rproj(rdx_sb, srg_w_sb)
            rproj(webc_sb, sru_w_sb)

            def stage2(dst_sb, w_sb, src_sb):
                @blk.tensor
                def _(t):
                    CV.wait(t)
                    for h in range(2):
                        ins = t.matmul(pA[h][:], w_sb[:], src_sb[:, h * 512:h * 512 + 512],
                                       start=True, stop=True)
                    CT.inc(ins)

                @blk.vector
                def _(v):
                    CT.wait(v)
                    for h in range(2):
                        ins = v.tensor_copy(dst_sb[:, h * 512:h * 512 + 512], pA[h][:])
                    CV.inc(ins)

            stage2(t1g_sb, sug_w_sb, rdx_sb)
            stage2(t1u_sb, suu_w_sb, webc_sb)

            if DEBUG_TAPS:
                @blk.sync
                def _(sy):
                    CV.wait(sy)
                    CDX.inc(sy.dma_start(out=dbg["dbg_rg"].ap(), in_=rg_sb[:]), 16)
                    CDX.inc(sy.dma_start(out=dbg["dbg_ru"].ap(), in_=ru_sb[:]), 16)

            def chunk_loop(n_chunks, cgw, cuw, gsrc, usrc, rdw, rd_psum):
                ct0, cs0, cv0 = CT.n, CS.n, CV.n

                gu = [gu_sb, gu2_sb]

                @blk.tensor
                def _(t):
                    for ci in range(n_chunks):
                        t.wait_ge(CS.sem, cs0 + ci)
                        t.wait_ge(CV.sem, cv0 + ci)
                        for h in range(2):
                            t.matmul(pA[h][:], cgw[:, ci * 128:(ci + 1) * 128],
                                     gsrc[:, h * 512:h * 512 + 512], start=True, stop=True)
                        for h in range(2):
                            ins = t.matmul(pB[h][:], cuw[:, ci * 128:(ci + 1) * 128],
                                           usrc[:, h * 512:h * 512 + 512],
                                           start=True, stop=True)
                        CT.inc(ins)
                        if ci >= 1:
                            t.wait_ge(CV.sem, cv0 + ci)
                            for h in range(2):
                                t.matmul(rd_psum[h][:], rdw[:, (ci - 1) * 128:ci * 128],
                                         gu[(ci - 1) % 2][:, h * 512:h * 512 + 512],
                                         start=(ci == 1), stop=False)
                    t.wait_ge(CV.sem, cv0 + n_chunks)
                    for h in range(2):
                        ins = t.matmul(rd_psum[h][:],
                                       rdw[:, (n_chunks - 1) * 128:n_chunks * 128],
                                       gu[(n_chunks - 1) % 2][:, h * 512:h * 512 + 512],
                                       start=False, stop=True)
                    CT.inc(ins)

                @blk.scalar
                def _(s):
                    for ci in range(n_chunks):
                        s.wait_ge(CT.sem, ct0 + ci + 1)
                        if ci > 0:
                            s.wait_ge(CV.sem, cv0 + ci)
                        s.activation(gch_sb[:, 0:512], pA[0][:], AF.Silu)
                        ins = s.activation(gch_sb[:, 512:1024], pA[1][:], AF.Silu)
                        CS.inc(ins)

                @blk.vector
                def _(v):
                    for ci in range(n_chunks):
                        v.wait_ge(CS.sem, cs0 + ci + 1)
                        v.tensor_mul(gu[ci % 2][:, 0:512], gch_sb[:, 0:512], pB[0][:])
                        CV.inc(v.tensor_mul(gu[ci % 2][:, 512:1024],
                                            gch_sb[:, 512:1024], pB[1][:]))

            chunk_loop(NSC, scg_w_sb, scu_w_sb, t1g_sb, t1u_sb, srd_w_sb, pC)

            @blk.vector
            def _(v):
                CT.wait(v)
                for h in range(2):
                    ins = v.tensor_copy(rdx_sb[:, h * 512:h * 512 + 512], pC[h][:])
                CV.inc(ins)

            @blk.tensor
            def _(t):
                CV.wait(t)
                for h in range(2):
                    ins = t.matmul(pA[h][:], sud_w_sb[:], rdx_sb[:, h * 512:h * 512 + 512],
                                   start=True, stop=True)
                CT.inc(ins)

            @blk.vector
            def _(v):
                CT.wait(v)
                for h in range(2):
                    ins = v.tensor_copy(union_sb[:, 32 * TOK + h * 512:32 * TOK + h * 512 + 512],
                                        pA[h][:])
                CV.inc(ins)

            ew_free = [0, 0]
            for e in range(NE):
                buf = ew_sb[e % 2]
                pw = PW[e % 2]
                if e >= 2:
                    @blk.gpsimd
                    def _(g, e=e, pw=pw, buf=buf):
                        g.wait_ge(CT.sem, ew_free[e % 2])
                        pw.inc(g.dma_start(out=buf[:, 0:128], in_=dt["ug_all"].ap()[e]), 16)
                        pw.inc(g.dma_start(out=buf[:, 128:256], in_=dt["uu_all"].ap()[e]), 16)
                        pw.inc(g.dma_start(out=buf[:, 256:384], in_=dt["ud_all"].ap()[e]), 16)
                        pw.inc(g.dma_start(out=buf[:, 384:1792], in_=dt["cg_all"].ap()[e]), 16)
                        pw.inc(g.dma_start(out=buf[:, 1792:3200], in_=dt["cu_all"].ap()[e]), 16)

                @blk.tensor
                def _(t, e=e, buf=buf, pw=pw):
                    pw.wait(t)
                    CS.wait(t)
                    CV.wait(t)
                    for h in range(2):
                        t.matmul(pA[h][:], buf[:, 0:128], rg_sb[:, h * 512:h * 512 + 512],
                                 start=True, stop=True)
                    for h in range(2):
                        t.matmul(pB[h][:], buf[:, 128:256], ru_sb[:, h * 512:h * 512 + 512],
                                 start=True, stop=True)
                    for h in range(2):
                        ins = t.matmul(pC[h][0:1, :], identbf_sb[0:32, e:e + 1],
                                       we_fmb_sb[:, h * 512:h * 512 + 512],
                                       start=True, stop=True)
                    CT.inc(ins)

                @blk.scalar
                def _(s, e=e):
                    CT.wait(s)
                    for h in range(2):
                        s.activation(srow_b_sb[0:1, h * 512:h * 512 + 512], pC[h][0:1, :],
                                     AF.Copy)
                    for h in range(2):
                        ins = s.activation(uch_sb[:, h * 512:h * 512 + 512], pB[h][:], AF.Copy)
                    CS.inc(ins)

                @blk.tensor
                def _(t, e=e):
                    CS.wait(t)
                    for h in range(2):
                        ins = t.matmul(pC[h][:], ones_rb_sb[:],
                                       srow_b_sb[0:1, h * 512:h * 512 + 512],
                                       start=True, stop=True)
                    CT.inc(ins)

                @blk.scalar
                def _(s, e=e):
                    CT.wait(s)
                    for h in range(2):
                        ins = s.activation(webc_sb[:, h * 512:h * 512 + 512], pC[h][:], AF.Copy)
                    CS.inc(ins)

                @blk.vector
                def _(v, e=e):
                    CS.wait(v)
                    for h in range(2):
                        v.tensor_copy(t1g_sb[:, h * 512:h * 512 + 512], pA[h][:])
                    CV.inc(v.tensor_mul(t1u_sb[:], uch_sb[:], webc_sb[:]))

                if DEBUG_TAPS and e == 0:
                    @blk.sync
                    def _(sy):
                        CV.wait(sy)
                        CDX.inc(sy.dma_start(out=dbg["dbg_t1g0"].ap(), in_=t1g_sb[:]), 16)
                        CDX.inc(sy.dma_start(out=dbg["dbg_t1u0"].ap(), in_=t1u_sb[:]), 16)

                chunk_loop(NIC, buf[:, 384:1792], buf[:, 1792:3200],
                           t1g_sb, t1u_sb, rd_w_sb, pC)
                ew_free[e % 2] = CT.n

                if DEBUG_TAPS and e == 0:
                    @blk.sync
                    def _(sy):
                        CV.wait(sy)
                        CDX.inc(sy.dma_start(out=dbg["dbg_gu0"].ap(), in_=gu_sb[:]), 16)

                @blk.vector
                def _(v, e=e):
                    CT.wait(v)
                    for h in range(2):
                        ins = v.tensor_copy(rdx_sb[:, h * 512:h * 512 + 512], pC[h][:])
                    CV.inc(ins)

                @blk.tensor
                def _(t, e=e, buf=buf):
                    CV.wait(t)
                    for h in range(2):
                        ins = t.matmul(pA[h][:], buf[:, 256:384],
                                       rdx_sb[:, h * 512:h * 512 + 512], start=True, stop=True)
                    CT.inc(ins)

                @blk.vector
                def _(v, e=e):
                    CT.wait(v)
                    for h in range(2):
                        ins = v.tensor_copy(
                            union_sb[:, e * TOK + h * 512:e * TOK + h * 512 + 512], pA[h][:])
                    CV.inc(ins)

                if DEBUG_TAPS and e == 0:
                    @blk.sync
                    def _(sy):
                        CV.wait(sy)
                        CDX.inc(sy.dma_start(out=dbg["dbg_rd0"].ap(), in_=rdx_sb[:]), 16)

            @blk.gpsimd
            def _(g):
                for e in range(NE):
                    CDW.inc(g.dma_start(out=cdbf.ap()[e], in_=dt["cd_all"].ap()[e]), 16)
            n_cdbf = CDW.n

            @blk.sync
            def _(sy):
                CCC.wait(sy)
                for c in range(2):
                    PJ[c % 2].inc(sy.dma_start(out=pj_sb[c % 2][:],
                                               in_=cc_out.ap()[c * 128:(c + 1) * 128, :]), 16)

            @blk.vector
            def _(v):
                v.memset(s_H0[:], 0.0)
                CV.inc(v.memset(s_H0b[:], 0.0))

            for c in range(NCH):
                pjc = PJ[c % 2]
                pjt = pj_sb[c % 2]
                pj_need = pjc.n

                @blk.vector
                def _(v, c=c, pjc=pjc, pjt=pjt, pj_need=pj_need):
                    pjc.wait(v, pj_need)
                    CV.wait(v)
                    v.tensor_mul(s_scr[:], pjt[:, 0:64], pjt[:, 0:64])
                    v.tensor_reduce(cbb[:, 4:5], s_scr[:], AX.X, ALU.add)
                    v.tensor_mul(s_scr[:], pjt[:, 128:192], pjt[:, 128:192])
                    CV.inc(v.tensor_reduce(cbb[:, 5:6], s_scr[:], AX.X, ALU.add))

                @blk.scalar
                def _(s, c=c):
                    CV.wait(s)
                    s.activation(cbb[:, 4:5], cbb[:, 4:5], AF.Sqrt)
                    CS.inc(s.activation(cbb[:, 5:6], cbb[:, 5:6], AF.Sqrt))

                @blk.vector
                def _(v, c=c, pjt=pjt):
                    CS.wait(v)
                    v.tensor_scalar_max(cbb[:, 4:5], cbb[:, 4:5], 1e-12)
                    v.tensor_scalar_max(cbb[:, 5:6], cbb[:, 5:6], 1e-12)
                    v.reciprocal(cbb[:, 4:5], cbb[:, 4:5])
                    CV.inc(v.reciprocal(cbb[:, 5:6], cbb[:, 5:6]))
                    CV.wait(v)
                    CV.inc(v.tensor_add(s_tmp[:], pjt[:, 192:256], wab_b_sb[:]))

                @blk.scalar
                def _(s, c=c):
                    CV.wait(s)
                    CS.inc(s.activation(s_scr[:], s_tmp[:], AF.Sigmoid))

                @blk.vector
                def _(v, c=c):
                    CS.wait(v)
                    CV.inc(v.tensor_mul(s_tmp[:], s_scr[:], la8b_sb[:]))

                @blk.tensor
                def _(t, c=c):
                    CV.wait(t)
                    CT.inc(t.matmul(pC0[:, 0:64], ltinc_sb[:], s_tmp[:],
                                    start=True, stop=True))

                @blk.vector
                def _(v, c=c):
                    CT.wait(v)
                    v.tensor_copy(s_c[:], pC0[:, 0:64])
                    CV.inc(v.tensor_sub(s_Eneg[:], s_c[:], s_tmp[:]))

                @blk.scalar
                def _(s, c=c):
                    CV.wait(s)
                    s.activation(s_Em1[:], s_Eneg[:], AF.Exp)
                    s.activation(s_A[:], s_c[:], AF.Exp)
                    s.activation(s_scr[:], s_tmp[:], AF.Exp)
                    CS.inc(s.activation(s_Eneg[:], s_c[:], AF.Exp, scale=-1.0))

                @blk.vector
                def _(v, c=c, pjt=pjt):
                    CS.wait(v)
                    v.tensor_scalar(s_scr[:], s_scr[:], -1.0, 1.0, ALU.mult, ALU.add)
                    v.tensor_mul(s_tmp[:], s_scr[:], s_Eneg[:])
                    v.tensor_scalar_mul(s_kn[:], pjt[:, 0:64], cbb[:, 4:5])
                    v.tensor_scalar_mul(s_qn[:], pjt[:, 128:192], cbb[:, 5:6])
                    v.tensor_mul(s_qt[:], s_qn[:], s_Em1[:])
                    v.tensor_mul(s_kt[:], s_kn[:], s_Em1[:])
                    v.tensor_mul(s_kh[:], s_kn[:], s_tmp[:])
                    v.tensor_mul(s_qA[:], s_qn[:], s_A[:])
                    v.tensor_copy(s_qtb[:], s_qt[:])
                    v.tensor_copy(s_ktb[:], s_kt[:])
                    v.tensor_copy(s_khb[:], s_kh[:])
                    v.tensor_copy(s_qAb[:], s_qA[:])
                    CV.inc(v.tensor_copy(s_vb[:], pjt[:, 64:128]))
                pj_consumed = CV.n


                @blk.tensor
                def _(t, c=c):
                    CV.wait(t)
                    t.transpose(pT0[0:64, 0:128], s_qtb[:], identbf_sb[:])
                    t.transpose(pT0[0:64, 128:256], s_ktb[:], identbf_sb[:])
                    t.transpose(pT0[0:64, 256:384], s_khb[:], identbf_sb[:])
                    t.transpose(pT0[0:64, 384:512], s_qAb[:], identbf_sb[:])
                    CT.inc(t.transpose(pC1[0:64, 0:128], s_c[:], ident32_sb[:]))

                @blk.vector
                def _(v, c=c):
                    CT.wait(v)
                    v.tensor_copy(s_qtf[:], pT0[0:64, 0:128])
                    v.tensor_copy(s_ktf[:], pT0[0:64, 128:256])
                    v.tensor_copy(s_khf[:], pT0[0:64, 256:384])
                    v.tensor_copy(s_qAf[:], pT0[0:64, 384:512])
                    CV.inc(v.tensor_copy(s_cfm[:], pC1[0:64, 0:128]))

                @blk.scalar
                def _(s, c=c):
                    CV.wait(s)
                    CS.inc(s.activation(s_alast[:], s_cfm[:, 127:128], AF.Exp))

                @blk.tensor
                def _(t, c=c):
                    CV.wait(t)
                    t.matmul(pA0[:, 0:128], s_khf[:], s_qtf[:], start=True, stop=True)
                    t.matmul(pA0[:, 128:256], s_khf[:], s_ktf[:], start=True, stop=True)
                    t.matmul(pA0[:, 256:384], s_khf[:], s_qAf[:], start=True, stop=True)
                    t.matmul(pB0[:, 0:64], s_qtf[:], s_H0b[:], start=True, stop=True)
                    t.matmul(pB0[:, 64:128], s_ktf[:], s_H0b[:], start=True, stop=True)
                    CT.inc(t.matmul(pB0[:, 128:192], s_qAf[:], s_H0b[:],
                                    start=True, stop=True))

                @blk.vector
                def _(v, c=c, pjt=pjt):
                    CT.wait(v)
                    v.tensor_mul(s_GT[:], pA0[:, 0:128], ltstrict_sb[:])
                    v.tensor_mul(s_MT[:], pA0[:, 128:256], ltstrict_sb[:])
                    v.tensor_mul(s_GaT[:], pA0[:, 256:384], ltinc_sb[:])
                    v.tensor_sub(s_e0w0[:, 0:64], pjt[:, 64:128], pB0[:, 0:64])
                    v.tensor_sub(s_e0w0[:, 64:128], pjt[:, 64:128], pB0[:, 64:128])
                    v.tensor_copy(s_y0[:], pB0[:, 128:192])
                    CV.inc(v.memset(s_U[:], 0.0))
                pj_done = CV.n

                if c + 2 < NCH:
                    @blk.sync
                    def _(sy, c=c, pj_done=pj_done):
                        sy.wait_ge(CV.sem, pj_done)
                        PJ[c % 2].inc(sy.dma_start(
                            out=pj_sb[c % 2][:],
                            in_=cc_out.ap()[(c + 2) * 128:(c + 3) * 128, :]), 16)

                for it in range(NIT):
                    @blk.tensor
                    def _(t, c=c, it=it):
                        CV.wait(t)
                        t.matmul(pA1[:, 0:64], s_GT[:], s_U[:], start=True, stop=True)
                        CT.inc(t.matmul(pA1[:, 64:128], s_MT[:], s_U[:],
                                        start=True, stop=True))

                    @blk.vector
                    def _(v, c=c, it=it):
                        CT.wait(v)
                        v.tensor_sub(s_ew[:], s_e0w0[:], pA1[:, 0:128])
                        v.tensor_mul(s_scr[:], s_ew[:, 0:64], s_ew[:, 0:64])
                        CV.inc(v.tensor_reduce(cbb[:, 2:3], s_scr[:], AX.X, ALU.add))

                    @blk.scalar
                    def _(s, c=c, it=it):
                        CV.wait(s)
                        CS.inc(s.activation(cba[:, 3:4], cbb[:, 2:3],
                                            AF.Sigmoid, scale=1.0 / (1.0 + 1e-6)))

                    @blk.vector
                    def _(v, c=c, it=it):
                        CS.wait(v)
                        CV.inc(v.tensor_scalar_mul(s_U[:], s_ew[:, 64:128],
                                                   cba[:, 3:4]))

                @blk.tensor
                def _(t, c=c):
                    CV.wait(t)
                    t.matmul(pB1[:, 0:64], s_GaT[:], s_U[:], start=True, stop=True)
                    CT.inc(t.matmul(pB1[0:64, 64:128], s_khb[:], s_U[:],
                                    start=True, stop=True))

                @blk.vector
                def _(v, c=c):
                    CT.wait(v)
                    CV.inc(v.tensor_add(s_kn[:], s_y0[:], pB1[:, 0:64]))

                @blk.vector
                def _(v, c=c):
                    CV.wait(v)
                    v.tensor_mul(s_scr[:], s_kn[:], s_kn[:])
                    CV.inc(v.tensor_reduce(cbb[:, 7:8], s_scr[:], AX.X, ALU.add))
                    CV.wait(v)
                    CV.inc(v.tensor_scalar(cba[:, 5:6], cbb[:, 7:8],
                                           1.0 / DST, 1e-6, ALU.mult, ALU.add))

                @blk.scalar
                def _(s, c=c):
                    CV.wait(s)
                    CS.inc(s.activation(cba[:, 5:6], cba[:, 5:6], AF.Sqrt))

                @blk.vector
                def _(v, c=c):
                    CS.wait(v)
                    CV.inc(v.reciprocal(cba[:, 5:6], cba[:, 5:6]))
                    CV.wait(v)
                    CV.inc(v.tensor_scalar_mul(yn_tm_sb[:, c * DST:(c + 1) * DST],
                                               s_kn[:], cba[:, 5:6]))
                    v.tensor_add(s_scr[0:64, :], pB1[0:64, 64:128], s_H0[:])
                    v.tensor_scalar_mul(s_H0[:], s_scr[0:64, :], s_alast[:])
                    CV.inc(v.tensor_copy(s_H0b[:], s_H0[:]))

                @blk.sync
                def _(sy, c=c):
                    CV.wait(sy)
                    CDX.inc(sy.dma_start(out=cc2_in.ap()[c * 128:(c + 1) * 128, :],
                                         in_=yn_tm_sb[:, c * DST:(c + 1) * DST]), 16)
            n_yn_dma = CDX.n

            if DEBUG_TAPS:
                @blk.sync
                def _(sy):
                    sy.wait_ge(CDX.sem, n_yn_dma)
                    CV.wait(sy)
                    CDX.inc(sy.dma_start(out=dbg["dbg_sGT"].ap()[:, 0:128], in_=s_GT[:]), 16)
                    CDX.inc(sy.dma_start(out=dbg["dbg_sGT"].ap()[:, 128:256], in_=s_MT[:]), 16)
                    CDX.inc(sy.dma_start(out=dbg["dbg_sGT"].ap()[:, 256:384], in_=s_GaT[:]), 16)
                    CDX.inc(sy.dma_start(out=dbg["dbg_e0w0"].ap(), in_=s_e0w0[:]), 16)
                    CDX.inc(sy.dma_start(out=dbg["dbg_U"].ap(), in_=s_U[:]), 16)
                    for i, tile in enumerate((s_c, s_A, s_Em1, s_Eneg, s_qt, s_kt, s_kh)):
                        CDX.inc(sy.dma_start(
                            out=dbg["dbg_prep"].ap()[:, i * 64:(i + 1) * 64], in_=tile[:]), 16)
                    CDX.inc(sy.dma_start(out=dbg["dbg_proj"].ap(), in_=cc_out.ap()), 16)
                    CDX.inc(sy.dma_start(out=dbg["dbg_yn"].ap(), in_=cc2_in.ap()), 16)
                    CDX.inc(sy.dma_start(out=dbg["dbg_h0"].ap(), in_=s_H0[:]), 16)

            @blk.gpsimd
            def _(g):
                g.wait_ge(CDX.sem, n_yn_dma)
                CCC.inc(g.collective_compute(
                    "ReduceScatter", ALU.max,
                    replica_groups=[[0, 1, 2, 3], [4, 5, 6, 7]],
                    ins=[cc2_in.ap().opt()], outs=[cc2_out.ap().opt()]))

            @blk.sync
            def _(sy):
                CCC.wait(sy)
                for tt in range(NT):
                    CDX.inc(sy.dma_start(out=yn_tm_sb[:, tt * 64:(tt + 1) * 64],
                                         in_=cc2_out.ap()[tt * 128:(tt + 1) * 128, :]), 16)

            @blk.vector
            def _(v):
                CDX.wait(v)
                CV.inc(v.tensor_copy(ynb_sb[:], yn_tm_sb[:, 0:NT * 64]))

            @blk.tensor
            def _(t):
                CV.wait(t)
                for tt in range(NT):
                    ins = t.transpose(pT0[0:64, tt * 128:(tt + 1) * 128],
                                      ynb_sb[:, tt * 64:(tt + 1) * 64], identbf_sb[:])
                CT.inc(ins)

            @blk.vector
            def _(v):
                CT.wait(v)
                CV.inc(v.tensor_copy(ynfm_sb[:], pT0[0:64, 0:1024]))
            n_ynfm = CV.n

            if DEBUG_TAPS:
                @blk.sync
                def _(sy):
                    CV.wait(sy)
                    CDX.inc(sy.dma_start(out=dbg["dbg_ynown"].ap(), in_=cc2_out.ap()), 16)
                    CDX.inc(sy.dma_start(out=dbg["dbg_t2"].ap(), in_=union_sb[:]), 16)

        with (
            nc.psum_tensor("pyA", [128, 2048], F32) as pyA,
            nc.psum_tensor("pyB", [128, 2048], F32) as pyB,
        ):
            py = [pyA, pyB]
            cdbuf = [wproj_sb[:, 0:2048], wproj_sb[:, 2048:4096]]
            cd_free = [CT.n, CT.n]
            xq_free = [CV.n, CV.n]
            out_dma = [CDX.n, CDX.n]
            for st in range(4):
                for q in range(2):
                    tt = st * 2 + q

                    @blk.sync
                    def _(sy, st=st, q=q, tt=tt):
                        sy.wait_ge(CV.sem, xq_free[q])
                        sy.wait_ge(CDX.sem, out_dma[q])
                        PX[q].inc(sy.dma_start(out=xb[q][:],
                                               in_=dt["x_slice"].ap()[tt * 128:(tt + 1) * 128, :]),
                                  16)

                for e in range(NE + 1):
                    if e < NE:
                        @blk.gpsimd
                        def _(g, st=st, e=e):
                            g.wait_ge(CT.sem, cd_free[e % 2])
                            if st == 0 and e < 2:
                                g.wait_ge(CDW.sem, n_cdbf)
                            PCD[e % 2].inc(g.dma_start(out=cdbuf[e % 2], in_=cdbf.ap()[e]), 16)

                    @blk.tensor
                    def _(t, st=st, e=e):
                        if e < NE:
                            PCD[e % 2].wait(t)
                        if e == 0:
                            CV.wait(t)
                        mov = cdbuf[e % 2] if e < NE else scd_w_sb[:]
                        for q in range(2):
                            tt = st * 2 + q
                            stat = union_sb[:, e * TOK + tt * 128:e * TOK + (tt + 1) * 128]
                            for nch in range(4):
                                ins = t.matmul(py[q][:, nch * 512:(nch + 1) * 512],
                                               stat, mov[:, nch * 512:(nch + 1) * 512],
                                               start=(e == 0), stop=False)
                        if e < NE:
                            CT.inc(ins)
                            cd_free[e % 2] = CT.n
                        else:
                            CT.inc(ins)

                @blk.tensor
                def _(t, st=st):
                    t.wait_ge(CV.sem, n_ynfm)
                    for q in range(2):
                        tt = st * 2 + q
                        stat = ynfm_sb[:, tt * 128:(tt + 1) * 128]
                        for nch in range(4):
                            ins = t.matmul(py[q][:, nch * 512:(nch + 1) * 512],
                                           stat, wo_sb[:, nch * 512:(nch + 1) * 512],
                                           start=False, stop=True)
                    CT.inc(ins)

                for q in range(2):
                    tt = st * 2 + q

                    @blk.vector
                    def _(v, st=st, q=q, tt=tt):
                        CT.wait(v)
                        PX[q].wait(v)
                        CV.inc(v.tensor_add(xb[q][:], xb[q][:], py[q][:]))
                    xq_free[q] = CV.n

                    @blk.sync
                    def _(sy, st=st, q=q, tt=tt):
                        CV.wait(sy)
                        CDX.inc(sy.dma_start(out=out_t.ap()[tt * 128:(tt + 1) * 128, :],
                                             in_=xb[q][:]), 16)
                    out_dma[q] = CDX.n

            @blk.sync
            def _(sy):
                CDX.wait(sy)

    return nc


_NC_CACHE = {}


def kernel(**inputs):
    f32 = np.float32

    def ca(x):
        return np.ascontiguousarray(np.asarray(x, dtype=f32))

    x = ca(inputs["L0_attn_x"])
    Wk = ca(inputs["L0_attn_Wk"]); Wv = ca(inputs["L0_attn_Wv"])
    Wq = ca(inputs["L0_attn_Wq"]); Wa = ca(inputs["L0_attn_Wa_w"])
    Wa_b = ca(inputs["L0_attn_Wa_b"]); lam = ca(inputs["L0_attn_lam"])
    norm_w = ca(inputs["L0_attn_norm_w"]); Wo = ca(inputs["L0_attn_Wo"])
    gate_w = ca(inputs["L0_ffn_gate_w"])
    Rg = ca(inputs["L0_ffn_Rg"]); Ru = ca(inputs["L0_ffn_Ru"]); Rd = ca(inputs["L0_ffn_Rd"])
    Ug = ca(inputs["L0_ffn_Ug"]); Cg = ca(inputs["L0_ffn_Cg"])
    Uu = ca(inputs["L0_ffn_Uu"]); Cu = ca(inputs["L0_ffn_Cu"])
    Ud = ca(inputs["L0_ffn_Ud"]); Cd = ca(inputs["L0_ffn_Cd"])
    s_Rg = ca(inputs["L0_ffn_s_Rg"]); s_Ug = ca(inputs["L0_ffn_s_Ug"])
    s_Cg = ca(inputs["L0_ffn_s_Cg"]); s_Ru = ca(inputs["L0_ffn_s_Ru"])
    s_Uu = ca(inputs["L0_ffn_s_Uu"]); s_Cu = ca(inputs["L0_ffn_s_Cu"])
    s_Rd = ca(inputs["L0_ffn_s_Rd"]); s_Ud = ca(inputs["L0_ffn_s_Ud"])
    s_Cd = ca(inputs["L0_ffn_s_Cd"])

    def kimg(wT, kt):
        Dc, M = wT.shape
        assert Dc == kt * 128
        return np.ascontiguousarray(wT.reshape(kt, 128, M).transpose(1, 0, 2).reshape(128, kt * M))

    wproj = np.concatenate([Wk.T, Wv.T, Wq.T, Wa.T, gate_w.T], axis=1)
    shared = {
        "wproj": kimg(wproj, 16),
        "rg_w": kimg(Rg.T, 16), "ru_w": kimg(Ru.T, 16),
        "srg_w": kimg(s_Rg.T, 16), "sru_w": kimg(s_Ru.T, 16),
        "rd_w": kimg(Rd.T, 11), "srd_w": kimg(s_Rd.T, 22),
        "scg_w": np.ascontiguousarray(s_Cg.T), "scu_w": np.ascontiguousarray(s_Cu.T),
        "scd_w": np.ascontiguousarray(s_Cd.T),
        "sug_w": np.ascontiguousarray(s_Ug.T), "suu_w": np.ascontiguousarray(s_Uu.T),
        "sud_w": np.ascontiguousarray(s_Ud.T),
        "ug_all": np.ascontiguousarray(Ug.transpose(0, 2, 1)),
        "uu_all": np.ascontiguousarray(Uu.transpose(0, 2, 1)),
        "ud_all": np.ascontiguousarray(Ud.transpose(0, 2, 1)),
        "cg_all": np.ascontiguousarray(Cg.transpose(0, 2, 1)),
        "cu_all": np.ascontiguousarray(Cu.transpose(0, 2, 1)),
        "cd_all": np.ascontiguousarray(Cd.transpose(0, 2, 1)),
        "wo_w": np.ascontiguousarray(Wo.T),
        "normw": np.ascontiguousarray(norm_w.reshape(DST, 1)),
        "lam_r": np.ascontiguousarray(lam.reshape(1, DST)),
        "wab_r": np.ascontiguousarray(Wa_b.reshape(1, DST)),
        "ident32": np.eye(128, dtype=f32),
        "identbf": np.eye(128, dtype=NPBF),
        "ltinc": np.triu(np.ones((128, 128), f32)),
        "ltstrict": np.triu(np.ones((128, 128), f32), 1),
        "ones_r": np.ones((1, 128), f32),
        "ones_rb": np.ones((1, 128), NPBF),
    }

    xf = x.reshape(NCORES, TOK, D)
    in_maps = []
    for c in range(NCORES):
        m = dict(shared)
        m["x_slice"] = np.ascontiguousarray(xf[c])
        in_maps.append(m)

    if "nc" not in _NC_CACHE:
        _NC_CACHE["nc"] = build_nc()
    nc = _NC_CACHE["nc"]

    res = run_bass_kernel_spmd(nc, in_maps, core_ids=list(range(NCORES)))
    out = np.concatenate([res.results[c]["out"] for c in range(NCORES)], axis=0)
    return out.reshape(2, SEQ, D)
